# version 03 (final)
# baseline (speedup 1.0000x reference)
import math
import numpy as np
import ml_dtypes
import concourse.bass as bass
import concourse.mybir as mybir
from concourse.bass_utils import run_bass_kernel_spmd

F32 = mybir.dt.float32
BF16 = mybir.dt.bfloat16
AF = mybir.ActivationFunctionType
ALU = mybir.AluOpType
AX = mybir.AxisListType.X
BF = ml_dtypes.bfloat16

W = 8
D = 1024
DH = 4096
NH = 4


class Sem:
    def __init__(self, nc, name):
        self.h = nc.alloc_semaphore(name)
        self.n = 0

    def inc(self, ins, k=1):
        ins.then_inc(self.h, k)
        self.n += k
        return (self, self.n)

    def dma(self, ins):
        return self.inc(ins, 16)

    def tok(self):
        return (self, self.n)

    def at(self, v):
        return (self, v)


def wait(eng, *toks):
    best = {}
    for t in toks:
        if t is None:
            continue
        s, v = t
        cur = best.get(id(s))
        if cur is None or v > cur[1]:
            best[id(s)] = (s, v)
    for s, v in best.values():
        if v > 0:
            eng.wait_ge(s.h, v)


class Arena:
    def __init__(self, nc, name, dtype, cols):
        self.t = nc.alloc_sbuf_tensor(name, [128, cols], dtype)
        self.cols = cols
        self.cur = 0
        self.marks = []

    def alloc(self, name, shape):
        assert shape[0] == 128
        n = int(np.prod(shape[1:]))
        off = (self.cur + 15) // 16 * 16
        self.cur = off + n
        assert self.cur <= self.cols, (
            f"arena overflow at {name}: {self.cur} > {self.cols}")
        return self.t[:, off:off + n]

    def mark(self):
        self.marks.append(self.cur)

    def pop(self):
        self.cur = self.marks.pop()


def build(n_tok, dbg=False):
    TC = n_tok // W
    NT = TC // 128
    NTG = TC // 512
    TC4 = TC // 4
    assert TC % 512 == 0

    nc = bass.Bass("TRN2", target_bir_lowering=False, debug=False,
                   detect_race_conditions=False)

    def din(name, shape, dt=F32):
        return nc.dram_tensor(name, shape, dt, kind="ExternalInput").ap()

    x_rows = din("x_rows", [TC, D])
    phase = din("phase", [NH, D])
    Wq = din("Wq", [D, DH]); Wk = din("Wk", [D, DH]); Wv = din("Wv", [D, DH])
    bq = din("bq", [1, DH]); bk = din("bk", [1, DH]); bv = din("bv", [1, DH])
    Wo = din("Wo", [DH, D]); bo = din("bo", [1, D])
    w1T = din("w1T", [D, DH]); w2T = din("w2T", [D, DH])
    h1T = din("h1T", [DH, DH]); h2T = din("h2T", [DH, DH])
    w3T = din("w3T", [DH, D])
    s1c = din("s1c", [128, 32]); s2c = din("s2c", [128, 32])
    sh1c = din("sh1c", [128, 32]); sh2c = din("sh2c", [128, 32])
    s3c = din("s3c", [128, 8])
    identf = din("identf", [128, 128])
    selmask = din("selmask", [1, 8])
    wq1T = din("wq1T", [D, D]); wk1T = din("wk1T", [D, D])
    wk2T = din("wk2T", [D, D]); wpjT = din("wpjT", [D, D])
    bq1r = din("bq1r", [1, D]); bk1r = din("bk1r", [1, D])
    bk2r = din("bk2r", [1, D]); bpjr = din("bpjr", [1, D])
    rhT = din("rhT", [D, D])
    prevm8 = din("prevm8", [8, 1]); prevo8 = din("prevo8", [8, 1])
    e0m = din("e0m", [128, 1])
    triu128 = din("triu128", [128, 128]); bc127 = din("bc127", [128, 128])

    y = nc.dram_tensor("y", [TC, D], F32, kind="ExternalOutput").ap()

    def dint(name, shape, dt=F32, dbgok=True):
        kind = "ExternalOutput" if (dbg and dbgok) else "Internal"
        return nc.dram_tensor(name, shape, dt, kind=kind).ap()

    pqr = dint("pqr", [TC, DH]); pqi = dint("pqi", [TC, DH])
    pkr = dint("pkr", [TC, DH]); pki = dint("pki", [TC, DH])
    pvv = dint("pvv", [TC, DH])
    poutTh = dint("poutTh", [NT, 128, 32 * 128], BF16)
    poutTl = dint("poutTl", [NT, 128, 32 * 128], BF16)
    pxnTh = dint("pxnTh", [8, 128, NT * 128], BF16)
    pxnTl = dint("pxnTl", [8, 128, NT * 128], BF16)
    pagin = dint("pagin", [NH * TC4, D], dbgok=False)
    pagout = dint("pagout", [W * NH * TC4, D], dbgok=False)
    phah = dint("phah", [32, 128, 512], BF16)
    phal = dint("phal", [32, 128, 512], BF16)
    phbh = dint("phbh", [32, 128, 512], BF16)
    phbl = dint("phbl", [32, 128, 512], BF16)
    phch = dint("phch", [32, 128, 512], BF16)
    phcl = dint("phcl", [32, 128, 512], BF16)
    py0ffn = dint("py0ffn", [TC, D])
    px1 = dint("px1", [TC, D])
    pdbg = dint("pdbg", [TC // 128, 128, 8], dbgok=True) if dbg else None
    pxn1Th = dint("pxn1Th", [8, 128, (TC // 128) * 128], BF16)
    pxn1Tl = dint("pxn1Tl", [8, 128, (TC // 128) * 128], BF16)
    pxn1 = dint("pxn1", [TC + 1, D])
    pq1 = dint("pq1", [TC + 1, D])
    poar = dint("poar", [TC + 1, D])
    pk1 = dint("pk1", [TC, D])
    pkv1 = dint("pkv1", [TC, D])
    pk21 = dint("pk21", [TC, D])
    pk2e = dint("pk2e", [TC, D])
    py1f = dint("py1f", [TC, D])
    pag1in = dint("pag1in", [4, D], dbgok=False)
    pag1out = dint("pag1out", [4 * W, D], dbgok=False)
    pag2in = dint("pag2in", [1, D], dbgok=False)
    pag2out = dint("pag2out", [W, D], dbgok=False)
    prow9 = dint("prow9", [2, D], BF16)
    pprefs = dint("pprefs", [2, D])
    prs1 = dint("prs1", [1, D])

    SEMS = {}

    ALIAS = {
        "z0": "x0", "z1": "x1", "p8r": "pr", "p8pk": "p1pk",
        "wl9": "wt0", "b9": "br2", "x90": "wo0", "x91": "wo1",
        "kr90": "l70", "kr91": "l71",
        "p9k0": "pk0", "p9k1": "pk1", "p9v0": "pk2", "p9v1": "pk3",
        "p9q0": "pkv0", "p9q1": "pkv1", "p9c0": "a3", "p9c1": "b3",
        "a1p": "c3", "mg9": "d3", "pr9s": "v3", "bd9": "o0",
        "rh110": "rh110", "rh111": "rh111", "xl110": "xl110", "xl111": "xl111",
        "yp110": "hal0", "yp111": "hal1", "rsc11": "hbl0", "rsl11": "hbl1",
        "k100": "pk0", "k101": "pk1", "v100": "pk2", "v101": "pk3",
        "q100": "pkv0", "q101": "pkv1", "oa100": "wo0", "oa101": "wo1",
        "os100": "l70", "os101": "l71", "vx100": "a3", "vx101": "b3",
        "kk100": "c3", "kk101": "d3", "ke100": "pa0", "ke101": "pa1",
        "kel100": "pb60", "kel101": "pb61", "q2100": "pc60", "q2101": "pc61",
        "oal100": "xl110", "oal101": "xl111", "x100": "x100", "x101": "x101",
        "yfl100": "rh110", "yfl101": "rh111", "yy100": "y70", "yy101": "y71",
        "pf10": "y06", "g210": "hcl0", "a2p": "hcl1", "wl10": "sl4",
        "b10": "xs6",
    }

    def Q(name):
        name = ALIAS.get(name, name)
        if name not in SEMS:
            SEMS[name] = Sem(nc, name)
        return SEMS[name]

    ts = Sem(nc, "ts")
    vs = Sem(nc, "vs")
    asm = Sem(nc, "asm")
    gs = Sem(nc, "gs")
    cc = Sem(nc, "cc")

    pbank = [nc.alloc_psum_tensor(f"pb{i}", [128, 512], F32) for i in range(8)]
    bankfree = [None] * 8

    sbA = nc.alloc_sbuf_tensor
    ident_s = sbA("ident_s", [128, 128], F32)
    ones1f_s = sbA("ones1f_s", [1, 128], F32)
    pihalf = sbA("pihalf", [1, 1], F32)
    epscol = sbA("epscol", [128, 1], F32)
    rowsh = sbA("rowsh", [1, 2 * D], F32)
    triu_s = sbA("triu_s", [128, 128], F32)
    bc127_s = sbA("bc127_s", [128, 128], F32)
    onescol_s = sbA("onescol_s", [128, 1], F32)
    e0m_s = sbA("e0m_s", [128, 1], F32)
    prevm_s = sbA("prevm_s", [8, 1], F32)
    prevo_s = sbA("prevo_s", [8, 1], F32)
    s1c_s = sbA("s1c_s", [128, 32], F32)
    s2c_s = sbA("s2c_s", [128, 32], F32)
    sh1c_s = sbA("sh1c_s", [128, 32], F32)
    sh2c_s = sbA("sh2c_s", [128, 32], F32)
    s3c_s = sbA("s3c_s", [128, 8], F32)
    bob = sbA("bob", [128, D], F32)
    selrow = sbA("selrow", [1, 8], F32)
    self8 = sbA("self8", [128, 8], F32)

    AF32 = Arena(nc, "AF32", F32, 24320)
    AB16 = Arena(nc, "AB16", BF16, 49152)
    st = {}

    with nc.Block() as block:
        def p0_sync(e):
            c0 = Q("c0")
            c0.dma(e.dma_start(out=ident_s[:, :], in_=identf[:, :]))
            c0.dma(e.dma_start(out=s1c_s[:, :], in_=s1c[:, :]))
            c0.dma(e.dma_start(out=s2c_s[:, :], in_=s2c[:, :]))
            c0.dma(e.dma_start(out=sh1c_s[:, :], in_=sh1c[:, :]))
            c0.dma(e.dma_start(out=sh2c_s[:, :], in_=sh2c[:, :]))
            c0.dma(e.dma_start(out=s3c_s[:, :], in_=s3c[:, :]))
            c0.dma(e.dma_start(out=triu_s[:, :], in_=triu128[:, :]))
            c0.dma(e.dma_start(out=bc127_s[:, :], in_=bc127[:, :]))
            c0.dma(e.dma_start(out=e0m_s[:, :], in_=e0m[:, :]))
            c0.dma(e.dma_start(out=prevm_s[0:8, :], in_=prevm8[0:8, :]))
            c0.dma(e.dma_start(out=prevo_s[0:8, :], in_=prevo8[0:8, :]))
            st["c_all"] = c0.dma(e.dma_start(out=selrow[0:1, :],
                                             in_=selmask[0:1, :]))

        def p0_gps(e):
            e.memset(pihalf[:, :], float(np.pi / 2))
            e.memset(ones1f_s[:, :], 1.0)
            e.memset(onescol_s[:, :], 1.0)
            st["c_mem"] = gs.inc(e.memset(epscol[:, :], 1e-5))

        block.sync(p0_sync)
        block.gpsimd(p0_gps)

        AF32.mark()
        cbt = AF32.alloc("cbt", [128, DH])
        sbt = AF32.alloc("sbt", [128, DH])

        p0b_a = asm.n
        p0b_t = ts.n
        p0b_v = vs.n
        p0b_ld = []

        def p0b_sync(e):
            for step in range(8):
                h = step % 4
                if step >= 1:
                    wait(e, (asm, p0b_a + step))
                p0b_ld.append(Q("pr").dma(e.dma_start(
                    out=rowsh[0:1, 0:D], in_=phase[h:h + 1, :])))
            for step in range(2):
                wait(e, (ts, p0b_t + 16 + step))
                p0b_ld.append(Q("pr").dma(e.dma_start(
                    out=rowsh[0:1, D + 512 * step:D + 512 * (step + 1)],
                    in_=bo[0:1, 512 * step:512 * (step + 1)])))

        def p0b_act(e):
            for step in range(8):
                wait(e, p0b_ld[step], st["c_mem"])
                if step >= 1:
                    wait(e, (ts, p0b_t + 2 * step))
                if step < 4:
                    asm.inc(e.activation(out=rowsh[0:1, D:2 * D],
                                         in_=rowsh[0:1, 0:D], func=AF.Sin,
                                         bias=pihalf[0:1, 0:1]))
                else:
                    asm.inc(e.activation(out=rowsh[0:1, D:2 * D],
                                         in_=rowsh[0:1, 0:D], func=AF.Sin))

        block.sync(p0b_sync)
        block.scalar(p0b_act)

        def p0b_tsec(teng):
            g = 0
            for step in range(8):
                wait(teng, (asm, p0b_a + step + 1), st["c_all"],
                     (gs, st["c_mem"][1]))
                for half in range(2):
                    if g >= 4:
                        wait(teng, (vs, p0b_v + g - 4 + 1))
                    ts.inc(teng.matmul(
                        pbank[g % 4][:, :], ones1f_s[0:1, :],
                        rowsh[0:1, D + 512 * half:D + 512 * (half + 1)],
                        start=True, stop=True))
                    g += 1
            for step in range(2):
                wait(teng, p0b_ld[8 + step])
                if g >= 4:
                    wait(teng, (vs, p0b_v + g - 4 + 1))
                ts.inc(teng.matmul(
                    pbank[g % 4][:, :], ones1f_s[0:1, :],
                    rowsh[0:1, D + 512 * step:D + 512 * (step + 1)],
                    start=True, stop=True))
                g += 1
            if g >= 4:
                wait(teng, (vs, p0b_v + g - 4 + 1))
            ts.inc(teng.matmul(pbank[g % 4][:, 0:8], ones1f_s[0:1, :],
                               selrow[0:1, :], start=True, stop=True))

        block.tensor(p0b_tsec)

        def p0b_vec(e):
            g = 0
            for step in range(8):
                dstt = cbt if step < 4 else sbt
                h = step % 4
                for half in range(2):
                    wait(e, (ts, p0b_t + g + 1))
                    vs.inc(e.tensor_copy(
                        out=dstt[:, 1024 * h + 512 * half:
                                 1024 * h + 512 * (half + 1)],
                        in_=pbank[g % 4][:, :]))
                    g += 1
            for step in range(2):
                wait(e, (ts, p0b_t + g + 1))
                vs.inc(e.tensor_copy(out=bob[:, 512 * step:512 * (step + 1)],
                                     in_=pbank[g % 4][:, :]))
                g += 1
            wait(e, (ts, p0b_t + g + 1))
            vs.inc(e.tensor_copy(out=self8[:, :], in_=pbank[g % 4][:, 0:8]))

        block.vector(p0b_vec)
        for i in range(4):
            bankfree[i] = (vs, p0b_v + 19)
        assert ts.n == p0b_t + 19 and vs.n == p0b_v + 19

        AB16.mark()
        xnTh = AB16.alloc("xnTh", [128, 8 * NT * 128])
        xnTl = AB16.alloc("xnTl", [128, 8 * NT * 128])
        AF32.mark()
        xin = [AF32.alloc(f"xin{i}", [128, D]) for i in range(2)]
        xnb = [AF32.alloc(f"xnb{i}", [128, D]) for i in range(2)]
        strash = AF32.alloc("strash", [128, D])
        c_sq = AF32.alloc("c_sq", [128, 16])
        c_sum = AF32.alloc("c_sum", [128, 16])
        c_mu = AF32.alloc("c_mu", [128, 16])
        c_mu2 = AF32.alloc("c_mu2", [128, 16])
        c_sq2 = AF32.alloc("c_sq2", [128, 16])
        c_var = AF32.alloc("c_var", [128, 16])
        c_std = AF32.alloc("c_std", [128, 16])
        c_rstd = AF32.alloc("c_rstd", [128, 16])
        c_nmr = AF32.alloc("c_nmr", [128, 16])
        c_ngm = AF32.alloc("c_ngm", [128, 16])
        fil1 = AF32.alloc("fil1", [128, 16])
        ttmp = [AF32.alloc(f"ttmp{i}", [128, 128]) for i in range(2)]

        p1_ld = []
        p1a = asm.n
        p1v = vs.n
        p1t = ts.n
        t_stats = {}
        t_nm = {}
        t_ev = {}
        for tt in range(NT):
            p1_ld.append((Q(f"x{tt % 2}"), Q(f"x{tt % 2}").n + 16 * (tt // 2 + 1)))

        def p1_evac(e, tt, dc):
            col = dc * (NT * 128) + tt * 128
            tb = ttmp[dc % 2]
            wait(e, (ts, p1t + 8 * tt + dc + 1))
            e.tensor_copy(out=tb[:, :],
                          in_=pbank[4 + (8 * tt + dc) % 4][:, 0:128])
            e.tensor_copy(out=xnTh[:, col:col + 128], in_=tb[:, :])
            t_ev[(tt, dc)] = vs.inc(e.tensor_tensor(
                out=xnTl[:, col:col + 128], in0=tb[:, :],
                in1=xnTh[:, col:col + 128], op=ALU.subtract))

        def p1_fill(e, n):
            for _ in range(n):
                e.tensor_copy(out=fil1[:, 0:1], in_=epscol[:, 0:1])

        def p1_vec(e):
            for tt in range(NT):
                xi = xin[tt % 2]
                wait(e, p1_ld[tt], (asm, p1a + 3 * tt + 1))
                if dbg and tt >= 1:
                    wait(e, (Q("dbg1"), 16 * 8 * tt))
                e.reduce_sum(out=c_sum[:, 0:1], in_=xi[:, :], axis=AX)
                e.tensor_scalar(out=c_sq2[:, 0:1], in0=c_sq[:, 0:1],
                                scalar1=1.0 / D, scalar2=None, op0=ALU.mult)
                if tt >= 1:
                    for dc in range(4):
                        p1_evac(e, tt - 1, dc)
                else:
                    p1_fill(e, 4)
                e.tensor_scalar(out=c_mu[:, 0:1], in0=c_sum[:, 0:1],
                                scalar1=1.0 / D, scalar2=None, op0=ALU.mult)
                e.tensor_scalar(out=c_ngm[:, 0:1], in0=c_sum[:, 0:1],
                                scalar1=-1.0 / D, scalar2=None, op0=ALU.mult)
                if tt >= 1:
                    for dc in range(4, 6):
                        p1_evac(e, tt - 1, dc)
                else:
                    p1_fill(e, 4)
                e.tensor_tensor(out=c_mu2[:, 0:1], in0=c_mu[:, 0:1],
                                in1=c_mu[:, 0:1], op=ALU.mult)
                if tt >= 1:
                    for dc in range(6, 8):
                        p1_evac(e, tt - 1, dc)
                else:
                    p1_fill(e, 4)
                t_stats[tt] = vs.inc(e.tensor_tensor(
                    out=c_var[:, 0:1], in0=c_sq2[:, 0:1],
                    in1=c_mu2[:, 0:1], op=ALU.subtract))
                wait(e, (asm, p1a + 3 * tt + 2))
                e.reciprocal(out=c_rstd[:, 0:1], in_=c_std[:, 0:1])
                p1_fill(e, 4)
                t_nm[tt] = vs.inc(e.scalar_tensor_tensor(
                    out=c_nmr[:, 0:1], in0=c_ngm[:, 0:1], scalar=1.0,
                    in1=c_rstd[:, 0:1], op0=ALU.mult, op1=ALU.mult))
            for dc in range(8):
                p1_evac(e, NT - 1, dc)

        block.vector(p1_vec)

        def p1_dbgpark(e, tt):
            wait(e, t_nm[tt])
            cols = [c_sum, c_sq2, c_mu, c_mu2, c_var, c_std, c_rstd, c_nmr]
            for k, cc_ in enumerate(cols):
                Q("dbg1").dma(e.dma_start(out=pdbg[tt, :, k:k + 1],
                                          in_=cc_[:, 0:1],
                                          allow_slow_non_contiguous=True))

        def p1_sync(e):
            for tt in range(NT):
                if tt >= 2:
                    wait(e, (asm, p1a + 3 * (tt - 2) + 3))
                Q(f"x{tt % 2}").dma(e.dma_start(
                    out=xin[tt % 2][:, :], in_=x_rows[128 * tt:128 * (tt + 1), :]))
                if dbg and tt >= 1:
                    p1_dbgpark(e, tt - 1)
            if dbg:
                p1_dbgpark(e, NT - 1)

        block.sync(p1_sync)

        def p1_act(e):
            for tt in range(NT):
                wait(e, p1_ld[tt], st["c_mem"])
                if tt >= 1:
                    wait(e, t_stats[tt - 1])
                asm.inc(e.activation(out=strash[:, :], in_=xin[tt % 2][:, :],
                                     func=AF.Square, accum_out=c_sq[:, 0:1]))
                wait(e, t_stats[tt])
                asm.inc(e.activation(out=c_std[:, 0:1], in_=c_var[:, 0:1],
                                     func=AF.Sqrt, bias=epscol[0:128, 0:1]))
                wait(e, t_nm[tt])
                if tt >= 2:
                    wait(e, (ts, p1t + 8 * (tt - 2) + 8))
                asm.inc(e.activation(out=xnb[tt % 2][:, :], in_=xin[tt % 2][:, :],
                                     func=AF.Identity,
                                     bias=c_nmr[0:128, 0:1],
                                     scale=c_rstd[0:128, 0:1]))

        block.scalar(p1_act)

        def p1_tsec(teng):
            for i in range(4, 8):
                wait(teng, bankfree[i])
                bankfree[i] = None
            for tt in range(NT):
                for dc in range(8):
                    g = 8 * tt + dc
                    wait(teng, (asm, p1a + 3 * tt + 3))
                    if g >= 4:
                        ptt, pdc = divmod(g - 4, 8)
                        wait(teng, t_ev[(ptt, pdc)])
                    ts.inc(teng.transpose(
                        pbank[4 + g % 4][:, 0:128],
                        xnb[tt % 2][:, 128 * dc:128 * (dc + 1)], ident_s[:, :]))

        block.tensor(p1_tsec)
        for i in range(4, 8):
            bankfree[i] = (vs, vs.n)
        st["p1_done"] = vs.tok()
        assert vs.n == p1v + 10 * NT
        assert ts.n == p1t + 8 * NT
        assert asm.n == p1a + 3 * NT

        def p1_park(e):
            wait(e, st["p1_done"])
            for dc in range(8):
                Q("p1pk").dma(e.dma_start(
                    out=pxnTh[dc, :, :],
                    in_=xnTh[:, dc * NT * 128:(dc + 1) * NT * 128]))
                Q("p1pk").dma(e.dma_start(
                    out=pxnTl[dc, :, :],
                    in_=xnTl[:, dc * NT * 128:(dc + 1) * NT * 128]))
            st["p1_park"] = Q("p1pk").tok()

        block.sync(p1_park)
        AF32.pop()

        AF32.mark()
        wtmp = [AF32.alloc(f"wtmp{i}", [128, 8 * 512]) for i in range(2)]
        qst = [AF32.alloc(f"qst{i}", [128, 512]) for i in range(2)]
        rst = [AF32.alloc(f"rst{i}", [128, 512]) for i in range(4)]
        biasb = AF32.alloc("biasb", [128, DH])
        AB16.mark()
        whh = [AB16.alloc(f"whh{i}", [128, 8 * 512]) for i in range(2)]
        wll = [AB16.alloc(f"wll{i}", [128, 8 * 512]) for i in range(2)]

        NPP = 8 + 8 * NT
        p2g = gs.n
        p2t = ts.n
        p2v = vs.n
        p2_wld = []
        p2_bld = []
        WSRC = {0: Wq, 1: Wk, 2: Wv}
        BIN = {0: bq, 1: bk, 2: bv}

        def p2G(pj, fc, tt):
            return pj * NPP + 8 + fc * NT + tt

        def p2_pktok(pj, fc, tt):
            gi = fc * NT + tt
            sl = gi % 2
            if pj < 2:
                return (Q(f"pk{2 * sl + 1}"), 16 * (pj * 4 * NT + gi // 2 + 1))
            return (Q(f"pkv{sl}"), 16 * (gi // 2 + 1))

        def p2_parks(e, pj, fc):
            for tt in range(NT):
                gi = fc * NT + tt
                G = p2G(pj, fc, tt)
                wait(e, (vs, p2v + G + 1))
                sl = gi % 2
                if pj < 2:
                    dst0 = (pqr, pqi)[0] if pj == 0 else pkr
                    PA, PB = (pqr, pqi) if pj == 0 else (pkr, pki)
                    Q(f"pk{2 * sl}").dma(e.dma_start(
                        out=PA[128 * tt:128 * (tt + 1), 512 * fc:512 * (fc + 1)],
                        in_=rst[2 * sl][:, :]))
                    Q(f"pk{2 * sl + 1}").dma(e.dma_start(
                        out=PB[128 * tt:128 * (tt + 1), 512 * fc:512 * (fc + 1)],
                        in_=rst[2 * sl + 1][:, :]))
                else:
                    Q(f"pkv{sl}").dma(e.dma_start(
                        out=pvv[128 * tt:128 * (tt + 1), 512 * fc:512 * (fc + 1)],
                        in_=qst[sl][:, :]))

        def p2_sync(e):
            for pj in range(3):
                for c in range(8):
                    bi = pj * 8 + c
                    if bi >= 1:
                        pgj, pc = divmod(bi - 1, 8)
                        wait(e, (ts, p2t + pgj * NPP + pc + 1))
                    else:
                        wait(e, (ts, p2t))
                    p2_bld.append(Q("br2").dma(e.dma_start(
                        out=rowsh[0:1, D:D + 512],
                        in_=BIN[pj][0:1, 512 * c:512 * (c + 1)])))
                for fc in range(8):
                    idx = pj * 8 + fc
                    if idx >= 2:
                        wait(e, (gs, p2g + idx - 1))
                    p2_wld.append(Q(f"wt{idx % 2}").dma(e.dma_start(
                        out=wtmp[idx % 2][:, :],
                        in_=WSRC[pj][:, 512 * fc:512 * (fc + 1)].rearrange(
                            "(kc p) n -> p kc n", p=128))))
                    if fc >= 2:
                        p2_parks(e, pj, fc - 2)
                p2_parks(e, pj, 6)
                p2_parks(e, pj, 7)
            st["p2_park"] = ([Q(f"pk{i}").tok() for i in range(4)] +
                             [Q("pkv0").tok(), Q("pkv1").tok()])

        def p2_gps(e):
            for idx in range(24):
                pj, fc = divmod(idx, 8)
                wait(e, p2_wld[idx])
                if idx >= 2:
                    pgj, pfc = divmod(idx - 2, 8)
                    wait(e, (ts, p2t + p2G(pgj, pfc, NT - 1) + 1))
                e.tensor_copy(out=whh[idx % 2][:, :], in_=wtmp[idx % 2][:, :])
                e.tensor_tensor(out=wtmp[idx % 2][:, :], in0=wtmp[idx % 2][:, :],
                                in1=whh[idx % 2][:, :], op=ALU.subtract)
                gs.inc(e.tensor_copy(out=wll[idx % 2][:, :],
                                     in_=wtmp[idx % 2][:, :]))

        block.sync(p2_sync)
        block.gpsimd(p2_gps)

        def p2_tsec(teng):
            for i in range(4):
                wait(teng, bankfree[i])
                bankfree[i] = None
            G = 0
            for pj in range(3):
                for c in range(8):
                    wait(teng, p2_bld[pj * 8 + c])
                    if G >= 4:
                        wait(teng, (vs, p2v + G - 4 + 1))
                    ts.inc(teng.matmul(pbank[G % 4][:, :], ones1f_s[0:1, :],
                                       rowsh[0:1, D:D + 512],
                                       start=True, stop=True))
                    G += 1
                for fc in range(8):
                    idx = pj * 8 + fc
                    for tt in range(NT):
                        wait(teng, (gs, p2g + idx + 1), st["p1_done"])
                        if G >= 4:
                            wait(teng, (vs, p2v + G - 4 + 1))
                        wh, wl = whh[idx % 2], wll[idx % 2]
                        last = None
                        for kc in range(8):
                            colx = kc * (NT * 128) + tt * 128
                            lh = xnTh[:, colx:colx + 128]
                            ll = xnTl[:, colx:colx + 128]
                            rh = wh[:, 512 * kc:512 * (kc + 1)]
                            rl = wl[:, 512 * kc:512 * (kc + 1)]
                            teng.matmul(pbank[G % 4][:, :], lh, rh,
                                        start=(kc == 0), stop=False)
                            teng.matmul(pbank[G % 4][:, :], lh, rl,
                                        start=False, stop=False)
                            last = teng.matmul(pbank[G % 4][:, :], ll, rh,
                                               start=False, stop=(kc == 7))
                        ts.inc(last)
                        G += 1

        block.tensor(p2_tsec)

        def p2_vec(e):
            G = 0
            for pj in range(3):
                for c in range(8):
                    wait(e, (ts, p2t + G + 1))
                    vs.inc(e.tensor_copy(out=biasb[:, 512 * c:512 * (c + 1)],
                                         in_=pbank[G % 4][:, :]))
                    G += 1
                for fc in range(8):
                    for tt in range(NT):
                        gi = fc * NT + tt
                        sl = gi % 2
                        wait(e, (ts, p2t + G + 1))
                        if gi >= 2:
                            wait(e, p2_pktok(pj, (gi - 2) // NT, (gi - 2) % NT))
                        if pj < 2:
                            e.tensor_tensor(out=qst[sl][:, :],
                                            in0=pbank[G % 4][:, :],
                                            in1=biasb[:, 512 * fc:512 * (fc + 1)],
                                            op=ALU.add)
                            e.tensor_tensor(out=rst[2 * sl][:, :],
                                            in0=qst[sl][:, :],
                                            in1=cbt[:, 512 * fc:512 * (fc + 1)],
                                            op=ALU.mult)
                            vs.inc(e.tensor_tensor(
                                out=rst[2 * sl + 1][:, :], in0=qst[sl][:, :],
                                in1=sbt[:, 512 * fc:512 * (fc + 1)],
                                op=ALU.mult))
                        else:
                            vs.inc(e.tensor_tensor(
                                out=qst[sl][:, :], in0=pbank[G % 4][:, :],
                                in1=biasb[:, 512 * fc:512 * (fc + 1)],
                                op=ALU.add))
                        G += 1

        block.vector(p2_vec)
        for i in range(4):
            bankfree[i] = (vs, p2v + 3 * NPP)
        assert ts.n == p2t + 3 * NPP
        assert vs.n == p2v + 3 * NPP
        AB16.pop()
        AF32.pop()
        AF32.pop()
        AB16.pop()

        _p3(block, locals())
        _p4(block, locals())
        _p6(block, locals())
        _p7(block, locals())
        _p8(block, locals())
        _p9(block, locals())
        _p11(block, locals())
        _p10(block, locals())

    return nc


def _p3(block, X):
    g = X
    nc = g["nc"]; NT = g["NT"]; st = g["st"]; Q = g["Q"]
    ts = g["ts"]; vs = g["vs"]; asm = g["asm"]
    pbank = g["pbank"]; bankfree = g["bankfree"]
    AF32 = g["AF32"]; AB16 = g["AB16"]
    ident_s = g["ident_s"]; epscol = g["epscol"]
    pqr = g["pqr"]; pqi = g["pqi"]; pkr = g["pkr"]; pki = g["pki"]; pvv = g["pvv"]
    poutTh = g["poutTh"]; poutTl = g["poutTl"]

    AF32.mark()
    qrb = AF32.alloc("qrb", [128, DH])
    qib = AF32.alloc("qib", [128, DH])
    krb = AF32.alloc("krb", [128, DH])
    kib = AF32.alloc("kib", [128, DH])
    vb = AF32.alloc("vb", [128, DH])
    outh = AF32.alloc("outh", [128, 2 * D])
    scr = AF32.alloc("scr", [128, 16])
    sci = AF32.alloc("sci", [128, 16])
    ssum = AF32.alloc("ssum", [128, 16])
    sexp = AF32.alloc("sexp", [128, 16])
    attw4 = AF32.alloc("attw4", [128, 16])
    c_mx = AF32.alloc("c_mx", [128, 16])
    c_nm = AF32.alloc("c_nm", [128, 16])
    c_sm4 = AF32.alloc("c_sm4", [128, 16])
    c_rs4 = AF32.alloc("c_rs4", [128, 16])
    fil3 = AF32.alloc("fil3", [128, 16])
    AB16.mark()
    oth = AB16.alloc("oth", [128, DH])
    otl = AB16.alloc("otl", [128, DH])

    p3v = vs.n
    p3a = asm.n
    p3t = ts.n
    lds = {}
    for tt in range(NT):
        for i, nm in enumerate(("a3", "b3", "c3", "d3", "v3")):
            lds[(tt, i)] = (Q(nm), Q(nm).n + 16 * (tt + 1))
    t_sc = {}
    t_nmx = {}
    t_rcp = {}
    t_ap1 = {}
    t_ap2 = {}
    t_lo = {}

    def atok(tt, k):
        return (asm, p3a + 10 * tt + k)

    def qpark(tt):
        return (Q(f"o{tt % 2}"), 32 * (tt // 2 + 1))

    pairs = [(h, gg, ri) for ri in range(2) for h in range(NH) for gg in range(NH)]

    def p3_lo_evac(e, tt, grp):
        wait(e, atok(tt, 3 + grp))
        if grp == 0 and tt >= 1:
            wait(e, qpark(tt - 1))
        t_lo[(tt, grp)] = vs.inc(e.tensor_tensor(
            out=otl[:, 512 * grp:512 * (grp + 1)],
            in0=pbank[4 + (8 * tt + grp) % 4][:, :],
            in1=oth[:, 512 * grp:512 * (grp + 1)], op=ALU.subtract))

    def p3_vec(e):
        for tt in range(NT):
            if tt >= 1:
                for grp in range(2):
                    p3_lo_evac(e, tt - 1, grp)
            wait(e, lds[(tt, 0)], lds[(tt, 1)], lds[(tt, 2)], lds[(tt, 3)])
            if tt >= 1:
                wait(e, (ts, p3t + 8 * (tt - 1) + 6))
            ins = None
            for (h, gg, ri) in pairs:
                a = qrb if ri == 0 else qib
                b = krb if ri == 0 else kib
                dst = scr if ri == 0 else sci
                ins = e.scalar_tensor_tensor(
                    out=outh[:, 0:1024], in0=a[:, 1024 * h:1024 * (h + 1)],
                    scalar=1.0, in1=b[:, 1024 * gg:1024 * (gg + 1)],
                    op0=ALU.mult, op1=ALU.mult,
                    accum_out=dst[:, 4 * h + gg:4 * h + gg + 1])
            t_sc[tt] = vs.inc(ins)
            if tt >= 1:
                for grp in range(2, 6):
                    p3_lo_evac(e, tt - 1, grp)
            else:
                for _ in range(4):
                    e.tensor_copy(out=fil3[:, 0:1], in_=epscol[:, 0:1])
            for h in range(NH):
                e.tensor_tensor(out=ssum[:, 4 * h:4 * (h + 1)],
                                in0=scr[:, 4 * h:4 * (h + 1)],
                                in1=sci[:, 4 * h:4 * (h + 1)], op=ALU.add)
            e.tensor_copy(out=fil3[:, 0:1], in_=epscol[:, 0:1])
            for h in range(NH):
                e.tensor_reduce(out=c_mx[:, h:h + 1],
                                in_=ssum[:, 4 * h:4 * (h + 1)], axis=AX,
                                op=ALU.max)
            if tt >= 1:
                for grp in range(6, 8):
                    p3_lo_evac(e, tt - 1, grp)
                for _ in range(2):
                    e.tensor_copy(out=fil3[:, 0:1], in_=epscol[:, 0:1])
            else:
                for _ in range(4):
                    e.tensor_copy(out=fil3[:, 0:1], in_=epscol[:, 0:1])
            t_nmx[tt] = vs.inc(e.tensor_scalar(
                out=c_nm[:, 0:4], in0=c_mx[:, 0:4],
                scalar1=-1.0 / 32.0, scalar2=None, op0=ALU.mult))
            wait(e, atok(tt, 1))
            t_rcp[tt] = vs.inc(e.reciprocal(out=c_rs4[:, 0:4],
                                            in_=c_sm4[:, 0:4]))
            wait(e, lds[(tt, 4)], atok(tt, 2))
            for hp in range(2):
                if hp == 0 and tt >= 1:
                    wait(e, (ts, p3t + 8 * tt))
                if hp == 1:
                    wait(e, (ts, p3t + 8 * tt + 4))
                ins = None
                for h2 in range(2):
                    h = 2 * hp + h2
                    dst = outh[:, 1024 * h2:1024 * (h2 + 1)]
                    e.tensor_scalar(out=dst, in0=vb[:, 0:1024],
                                    scalar1=attw4[:, 4 * h:4 * h + 1],
                                    scalar2=None, op0=ALU.mult)
                    for gg in range(1, NH):
                        ins = e.scalar_tensor_tensor(
                            out=dst, in0=vb[:, 1024 * gg:1024 * (gg + 1)],
                            scalar=attw4[:, 4 * h + gg:4 * h + gg + 1],
                            in1=dst, op0=ALU.mult, op1=ALU.add)
                if hp == 0:
                    t_ap1[tt] = vs.inc(ins)
                else:
                    t_ap2[tt] = vs.inc(ins)
        for grp in range(8):
            p3_lo_evac(e, NT - 1, grp)

    block.vector(p3_vec)

    def p3_park1(e, tt):
        wait(e, t_lo[(tt, 7)], atok(tt, 10))
        Q(f"o{tt % 2}").dma(e.dma_start(out=poutTh[tt, :, :], in_=oth[:, :]))
        Q(f"o{tt % 2}").dma(e.dma_start(out=poutTl[tt, :, :], in_=otl[:, :]))

    def p3_sync(e):
        wait(e, *st["p2_park"])
        for tt in range(NT):
            if tt >= 1:
                wait(e, t_sc[tt - 1])
            Q("a3").dma(e.dma_start(
                out=qrb[:, :], in_=pqr[128 * tt:128 * (tt + 1), :]))
            Q("b3").dma(e.dma_start(
                out=qib[:, :], in_=pqi[128 * tt:128 * (tt + 1), :]))
            Q("c3").dma(e.dma_start(
                out=krb[:, :], in_=pkr[128 * tt:128 * (tt + 1), :]))
            Q("d3").dma(e.dma_start(
                out=kib[:, :], in_=pki[128 * tt:128 * (tt + 1), :]))
            if tt >= 1:
                wait(e, t_ap2[tt - 1])
            Q("v3").dma(e.dma_start(
                out=vb[:, :], in_=pvv[128 * tt:128 * (tt + 1), :]))
            if tt >= 1:
                p3_park1(e, tt - 1)
        p3_park1(e, NT - 1)
        st["p3_park"] = [Q("o0").tok(), Q("o1").tok()]

    block.sync(p3_sync)

    def p3_tsec(teng):
        for i in range(4, 8):
            wait(teng, bankfree[i])
            bankfree[i] = None
        for tt in range(NT):
            for grp in range(8):
                gidx = 8 * tt + grp
                wait(teng, t_ap1[tt] if grp < 4 else t_ap2[tt])
                if gidx >= 4:
                    ptt, pgrp = divmod(gidx - 4, 8)
                    wait(teng, t_lo[(ptt, pgrp)])
                last = None
                for c in range(4):
                    blk = grp * 4 + c
                    h = blk // 8
                    dc = blk % 8
                    lcol = 128 * (8 * (h % 2) + dc)
                    last = teng.transpose(
                        pbank[4 + gidx % 4][:, 128 * c:128 * (c + 1)],
                        outh[:, lcol:lcol + 128],
                        ident_s[:, :])
                ts.inc(last)

    block.tensor(p3_tsec)

    def p3_act(e):
        for tt in range(NT):
            wait(e, t_nmx[tt])
            if tt >= 1:
                wait(e, t_rcp[tt - 1])
            ins = None
            for h in range(NH):
                ins = e.activation(out=sexp[:, 4 * h:4 * (h + 1)],
                                   in_=ssum[:, 4 * h:4 * (h + 1)],
                                   func=AF.Exp, scale=1.0 / 32.0,
                                   bias=c_nm[0:128, h:h + 1],
                                   accum_out=c_sm4[:, h:h + 1])
            asm.inc(ins)
            wait(e, t_rcp[tt])
            if tt >= 1:
                wait(e, t_ap2[tt - 1])
            ins = None
            for h in range(NH):
                ins = e.activation(out=attw4[:, 4 * h:4 * (h + 1)],
                                   in_=sexp[:, 4 * h:4 * (h + 1)],
                                   func=AF.Identity,
                                   scale=c_rs4[0:128, h:h + 1])
            asm.inc(ins)
            for grp in range(8):
                wait(e, (ts, p3t + 8 * tt + grp + 1))
                if grp == 0 and tt >= 1:
                    wait(e, qpark(tt - 1))
                asm.inc(e.activation(
                    out=oth[:, 512 * grp:512 * (grp + 1)],
                    in_=pbank[4 + (8 * tt + grp) % 4][:, :], func=AF.Identity))

    block.scalar(p3_act)
    for i in range(4, 8):
        bankfree[i] = (vs, vs.n)
    assert vs.n == p3v + 13 * NT
    assert asm.n == p3a + 10 * NT
    assert ts.n == p3t + 8 * NT
    AB16.pop()
    AF32.pop()


def _p4(block, X):
    g = X
    NT = g["NT"]; NTG = g["NTG"]; TC4 = g["TC4"]; st = g["st"]; Q = g["Q"]
    ts = g["ts"]; vs = g["vs"]; gs = g["gs"]; cc = g["cc"]
    pbank = g["pbank"]; bankfree = g["bankfree"]
    AF32 = g["AF32"]; AB16 = g["AB16"]
    Wo = g["Wo"]; poutTh = g["poutTh"]; poutTl = g["poutTl"]
    pagin = g["pagin"]; pagout = g["pagout"]

    AF32.mark()
    wotmp = [AF32.alloc(f"wotmp{i}", [128, 512]) for i in range(2)]
    agst = [AF32.alloc(f"agst{i}", [128, 512]) for i in range(2)]
    AB16.mark()
    woh = AB16.alloc("woh", [128, 32 * 512])
    wol = AB16.alloc("wol", [128, 32 * 512])
    slh = AB16.alloc("slh", [128, 8 * 512])
    sll = AB16.alloc("sll", [128, 8 * 512])

    NG4 = NTG * NH
    p4g = gs.n
    p4t = ts.n
    p4v = vs.n
    p4_wld = []
    p4_sld = {}
    p4_pk = {}

    def p4_sync(e):
        wait(e, *st["p3_park"])
        for half in range(2):
            for m in range(32):
                mi = 32 * half + m
                if mi >= 2:
                    wait(e, (gs, p4g + mi - 1))
                if half == 1 and m < 2:
                    wait(e, (ts, p4t + NG4))
                p4_wld.append(Q(f"wo{mi % 2}").dma(e.dma_start(
                    out=wotmp[mi % 2][:, :],
                    in_=Wo[128 * m:128 * (m + 1), 512 * half:512 * (half + 1)])))
            for gidx in range(NG4):
                gg = NG4 * half + gidx
                tg, h = divmod(gidx, NH)
                if gg >= 1:
                    wait(e, (ts, p4t + gg))
                for ti in range(4):
                    tt = 4 * tg + ti
                    Q("sl4").dma(e.dma_start(
                        out=slh[:, :].rearrange("p (k t) -> p k t", k=8)[
                            :, :, 128 * ti:128 * (ti + 1)],
                        in_=poutTh[tt, :, :].rearrange("p (f t) -> p f t", f=32)[
                            :, 8 * h:8 * (h + 1), :]))
                    p4_sld[gg] = Q("sl4").dma(e.dma_start(
                        out=sll[:, :].rearrange("p (k t) -> p k t", k=8)[
                            :, :, 128 * ti:128 * (ti + 1)],
                        in_=poutTl[tt, :, :].rearrange("p (f t) -> p f t", f=32)[
                            :, 8 * h:8 * (h + 1), :]))
                if gg >= 1:
                    pg = gg - 1
                    ptg, ph = divmod(pg % NG4, NH)
                    phf = pg // NG4
                    wait(e, (vs, p4v + pg + 1))
                    p4_pk[pg] = Q(f"ag{pg % 2}").dma(e.dma_start(
                        out=pagin[ph * TC4 + 128 * ptg:ph * TC4 + 128 * (ptg + 1),
                                  512 * phf:512 * (phf + 1)],
                        in_=agst[pg % 2][:, :]))
        pg = 2 * NG4 - 1
        ptg, ph = divmod(pg % NG4, NH)
        phf = pg // NG4
        wait(e, (vs, p4v + pg + 1))
        p4_pk[pg] = Q(f"ag{pg % 2}").dma(e.dma_start(
            out=pagin[ph * TC4 + 128 * ptg:ph * TC4 + 128 * (ptg + 1),
                      512 * phf:512 * (phf + 1)],
            in_=agst[pg % 2][:, :]))
        st["p4_park"] = [Q("ag0").tok(), Q("ag1").tok()]

    def p4_gps(e):
        for half in range(2):
            for m in range(32):
                mi = 32 * half + m
                wait(e, p4_wld[mi])
                e.tensor_copy(out=woh[:, 512 * m:512 * (m + 1)],
                              in_=wotmp[mi % 2][:, :])
                e.tensor_tensor(out=wotmp[mi % 2][:, :], in0=wotmp[mi % 2][:, :],
                                in1=woh[:, 512 * m:512 * (m + 1)],
                                op=ALU.subtract)
                gs.inc(e.tensor_copy(out=wol[:, 512 * m:512 * (m + 1)],
                                     in_=wotmp[mi % 2][:, :]))

    block.sync(p4_sync)
    block.gpsimd(p4_gps)

    def p4_tsec(teng):
        for i in range(4):
            wait(teng, bankfree[i])
            bankfree[i] = None
        for half in range(2):
            for gidx in range(NG4):
                gg = NG4 * half + gidx
                wait(teng, (gs, p4g + 32 * (half + 1)), p4_sld[gg])
                if gg >= 2:
                    wait(teng, (vs, p4v + gg - 1))
                bi = pbank[gg % 4]
                last = None
                for r in range(4):
                    for k in range(8):
                        m = r * 8 + k
                        first = (m == 0)
                        lastm = (m == 31)
                        lh = slh[:, 512 * k + r:512 * k + r + 509:4]
                        ll = sll[:, 512 * k + r:512 * k + r + 509:4]
                        rh = woh[:, 512 * m:512 * (m + 1)]
                        rl = wol[:, 512 * m:512 * (m + 1)]
                        teng.matmul(bi[:, :], lh, rh, start=first, stop=False)
                        teng.matmul(bi[:, :], lh, rl, start=False, stop=False)
                        last = teng.matmul(bi[:, :], ll, rh, start=False,
                                           stop=lastm)
                ts.inc(last)

    block.tensor(p4_tsec)

    def p4_vec(e):
        for gg in range(2 * NG4):
            wait(e, (ts, p4t + gg + 1))
            if gg >= 2:
                wait(e, p4_pk[gg - 2])
            vs.inc(e.tensor_copy(out=agst[gg % 2][:, :],
                                 in_=pbank[gg % 4][:, :]))

    block.vector(p4_vec)
    for i in range(4):
        bankfree[i] = (vs, p4v + 2 * NG4)
    assert ts.n == p4t + 2 * NG4
    assert vs.n == p4v + 2 * NG4
    st["p4_tend"] = ts.tok()
    st["p4_vend"] = vs.tok()

    def p5_gps(e):
        wait(e, *st["p4_park"])
        cc.inc(e.collective_compute(
            "AllGather", ALU.bypass,
            replica_groups=[list(range(W))],
            ins=[pagin[:, :]], outs=[pagout[:, :]]))

    block.gpsimd(p5_gps)
    AB16.pop()
    AF32.pop()


def _p6(block, X):
    g = X
    NT = g["NT"]; NTG = g["NTG"]; st = g["st"]; Q = g["Q"]
    ts = g["ts"]; vs = g["vs"]; asm = g["asm"]
    pbank = g["pbank"]; bankfree = g["bankfree"]
    AF32 = g["AF32"]; AB16 = g["AB16"]
    ident_s = g["ident_s"]
    w1T = g["w1T"]; w2T = g["w2T"]; h1T = g["h1T"]; h2T = g["h2T"]; w3T = g["w3T"]
    s1c_s = g["s1c_s"]; s2c_s = g["s2c_s"]
    sh1c_s = g["sh1c_s"]; sh2c_s = g["sh2c_s"]; s3c_s = g["s3c_s"]
    pxnTh = g["pxnTh"]; pxnTl = g["pxnTl"]
    phah = g["phah"]; phal = g["phal"]
    phbh = g["phbh"]; phbl = g["phbl"]
    phch = g["phcl"] if False else g["phch"]; phcl = g["phcl"]
    py0ffn = g["py0ffn"]

    AF32.mark()
    t1f = [AF32.alloc(f"t1f{i}", [128, 512]) for i in range(2)]
    haf = [AF32.alloc(f"haf{i}", [128, 512]) for i in range(2)]
    y0sm = [AF32.alloc(f"y0sm{i}", [128, 512]) for i in range(2)]
    y0r = [AF32.alloc(f"y0r{i}", [128, D]) for i in range(4)]
    AB16.mark()
    xnslh = AB16.alloc("xnslh", [128, 8 * 512])
    xnsll = AB16.alloc("xnsll", [128, 8 * 512])
    w12s = [AB16.alloc(f"w12s{i}", [128, 8 * 128]) for i in range(8)]
    h1s = [AB16.alloc(f"h1s{i}", [128, 8 * 128]) for i in range(4)]
    h2s = [AB16.alloc(f"h2s{i}", [128, 8 * 128]) for i in range(4)]
    w3s = [AB16.alloc(f"w3s{i}", [128, 8 * 128]) for i in range(4)]
    hald = [AB16.alloc(f"hald{i}", [128, 512]) for i in range(8)]
    hbld = [AB16.alloc(f"hbld{i}", [128, 512]) for i in range(8)]
    hcld = [AB16.alloc(f"hcld{i}", [128, 512]) for i in range(8)]
    pstg = [AB16.alloc(f"pstg{i}", [128, 512]) for i in range(4)]

    TPS = 360
    VPS = 112
    APS = 96
    p6 = dict(t=ts.n, v=vs.n, a=asm.n)

    def tP(ss, k):
        return (ts, p6["t"] + TPS * ss + k)

    def vP(ss, k):
        return (vs, p6["v"] + VPS * ss + k)

    def aP(ss, k):
        return (asm, p6["a"] + APS * ss + k)

    T_w12 = lambda oc, j: 2 * oc + j + 1
    T_h1 = lambda og, ic: 64 + 32 * og + ic + 1
    T_h2 = lambda og, ic: 192 + 32 * og + ic + 1
    T_w3 = lambda ic: 320 + ic + 1
    T_tr = lambda o3: 352 + o3 + 1
    V_w12 = lambda oc: oc + 1
    V_h1 = lambda og, b: 32 + 8 * og + b + 1
    V_h2 = lambda og, b: 64 + 8 * og + b + 1
    V_w3 = lambda o3: 97 if o3 == 0 else 96 + 2 * o3
    V_tr = lambda o3: 112 if o3 == 7 else 99 + 2 * o3
    A_w12 = lambda oc: oc + 1
    A_h1 = lambda og, b: 32 + 8 * og + b + 1
    A_h2 = lambda og, b: 64 + 8 * og + b + 1

    def q_xn(ss):
        return (Q("xs6"), 32 * (ss + 1))

    def q_w12(ss, oc):
        return (Q(f"wa{oc % 4}"), 32 * (8 * ss + oc // 4 + 1))

    def q_h1s(ss, og, ic):
        gi = 32 * og + ic
        return (Q(f"h1w{gi % 4}"), 16 * (32 * ss + gi // 4 + 1))

    def q_h2s(ss, og, ic):
        gi = 32 * og + ic
        return (Q(f"h2w{gi % 4}"), 16 * (32 * ss + gi // 4 + 1))

    def q_w3s(ss, ic):
        return (Q(f"w3w{ic % 2}"), 16 * (16 * ss + ic // 2 + 1))

    def q_hal(ss, og, ic):
        gi = 32 * og + ic
        return (Q(f"hal{gi % 4}"), 32 * (32 * ss + gi // 4 + 1))

    def q_hbl(ss, og, ic):
        gi = 32 * og + ic
        return (Q(f"hbl{gi % 4}"), 32 * (32 * ss + gi // 4 + 1))

    def q_hcl(ss, ic):
        return (Q(f"hcl{ic % 2}"), 32 * (16 * ss + ic // 2 + 1))

    def q_pa(ss, oc):
        return (Q(f"pa{oc % 2}"), 32 * (16 * ss + oc // 2 + 1))

    def q_pb(ss, o):
        return (Q(f"pb6{o % 2}"), 32 * (16 * ss + o // 2 + 1))

    def q_pc(ss, o):
        return (Q(f"pc6{o % 2}"), 32 * (16 * ss + o // 2 + 1))

    def q_y0(ss):
        return (Q("y06"), 64 * (ss + 1))

    def p6_gps(e):
        wait(e, st["p4_tend"])
        for ss in range(NTG):
            for oc in range(32):
                if oc >= 4:
                    wait(e, tP(ss, T_w12(oc - 4, 1)))
                elif ss >= 1:
                    wait(e, tP(ss - 1, T_w12(28 + oc, 1)))
                Q(f"wa{oc % 4}").dma(e.dma_start(
                    out=w12s[2 * (oc % 4)][:, :],
                    in_=w1T[:, 128 * oc:128 * (oc + 1)].rearrange(
                        "(k p) n -> p k n", p=128)))
                Q(f"wa{oc % 4}").dma(e.dma_start(
                    out=w12s[2 * (oc % 4) + 1][:, :],
                    in_=w2T[:, 128 * oc:128 * (oc + 1)].rearrange(
                        "(k p) n -> p k n", p=128)))
            for og in range(4):
                for ic in range(32):
                    gi = 32 * og + ic
                    if gi >= 4:
                        wait(e, tP(ss, T_h1(*divmod(gi - 4, 32))))
                    elif ss >= 1:
                        wait(e, tP(ss - 1, T_h1(3, 28 + gi)))
                    Q(f"h1w{gi % 4}").dma(e.dma_start(
                        out=h1s[gi % 4][:, :],
                        in_=h1T[128 * ic:128 * (ic + 1),
                                1024 * og:1024 * (og + 1)]))
            for og in range(4):
                for ic in range(32):
                    gi = 32 * og + ic
                    if gi >= 4:
                        wait(e, tP(ss, T_h2(*divmod(gi - 4, 32))))
                    elif ss >= 1:
                        wait(e, tP(ss - 1, T_h2(3, 28 + gi)))
                    Q(f"h2w{gi % 4}").dma(e.dma_start(
                        out=h2s[gi % 4][:, :],
                        in_=h2T[128 * ic:128 * (ic + 1),
                                1024 * og:1024 * (og + 1)]))
            for ic in range(32):
                if ic >= 2:
                    wait(e, tP(ss, T_w3(ic - 2)))
                elif ss >= 1:
                    wait(e, tP(ss - 1, T_w3(30 + ic)))
                Q(f"w3w{ic % 2}").dma(e.dma_start(
                    out=w3s[ic % 2][:, :],
                    in_=w3T[128 * ic:128 * (ic + 1), :]))

    block.gpsimd(p6_gps)

    def p6_qs(e):
        for ss in range(NTG):
            for oc in range(32):
                wait(e, vP(ss, V_w12(oc)))
                sl = 2 * (oc % 2)
                Q(f"pa{oc % 2}").dma(e.dma_start(out=phah[oc, :, :],
                                                 in_=pstg[sl][:, :]))
                Q(f"pa{oc % 2}").dma(e.dma_start(out=phal[oc, :, :],
                                                 in_=pstg[sl + 1][:, :]))
            for o in range(32):
                wait(e, vP(ss, V_h1(o // 8, o % 8)))
                sl = 2 * (o % 2)
                Q(f"pb6{o % 2}").dma(e.dma_start(out=phbh[o, :, :],
                                                 in_=pstg[sl][:, :]))
                Q(f"pb6{o % 2}").dma(e.dma_start(out=phbl[o, :, :],
                                                 in_=pstg[sl + 1][:, :]))
            for o in range(32):
                wait(e, vP(ss, V_h2(o // 8, o % 8)))
                sl = 2 * (o % 2)
                Q(f"pc6{o % 2}").dma(e.dma_start(out=phch[o, :, :],
                                                 in_=pstg[sl][:, :]))
                Q(f"pc6{o % 2}").dma(e.dma_start(out=phcl[o, :, :],
                                                 in_=pstg[sl + 1][:, :]))
            wait(e, vP(ss, V_tr(7)))
            for ttl in range(4):
                Q("y06").dma(e.dma_start(
                    out=py0ffn[512 * ss + 128 * ttl:512 * ss + 128 * (ttl + 1), :],
                    in_=y0r[ttl][:, :]))
        st["p6_park"] = Q("y06").tok()

    block.sync(p6_qs)

    def p6_act(e):
        for ss in range(NTG):
            if ss >= 1:
                wait(e, tP(ss - 1, T_w12(31, 1)))
            else:
                wait(e, st["p1_park"], st["p4_tend"], *st["p3_park"])
            Q("xs6").dma(e.dma_start(
                out=xnslh[:, :].rearrange("p (k t) -> p k t", k=8),
                in_=pxnTh[:, :, 512 * ss:512 * (ss + 1)].rearrange(
                    "k p t -> p k t")))
            Q("xs6").dma(e.dma_start(
                out=xnsll[:, :].rearrange("p (k t) -> p k t", k=8),
                in_=pxnTl[:, :, 512 * ss:512 * (ss + 1)].rearrange(
                    "k p t -> p k t")))
            for oc in range(32):
                wait(e, tP(ss, T_w12(oc, 0)))
                if oc >= 2:
                    wait(e, vP(ss, V_w12(oc - 2)))
                asm.inc(e.activation(out=t1f[oc % 2][:, :],
                                     in_=pbank[4 * (oc % 2)][:, :], func=AF.Silu,
                                     scale=s1c_s[:, oc:oc + 1]))
            for og in range(4):
                for ic in range(32):
                    gi = 32 * og + ic
                    wait(e, q_pa(ss, ic))
                    if gi >= 4:
                        wait(e, tP(ss, T_h1(*divmod(gi - 4, 32))))
                    sl = 2 * (gi % 4)
                    Q(f"hal{gi % 4}").dma(e.dma_start(out=hald[sl][:, :],
                                                      in_=phah[ic, :, :]))
                    Q(f"hal{gi % 4}").dma(e.dma_start(out=hald[sl + 1][:, :],
                                                      in_=phal[ic, :, :]))
                wait(e, tP(ss, T_h1(og, 31)))
                for b in range(8):
                    gi2 = 8 * og + b
                    if gi2 >= 2:
                        wait(e, vP(ss, 32 + gi2 - 1))
                    asm.inc(e.activation(out=t1f[b % 2][:, :],
                                         in_=pbank[b][:, :], func=AF.Gelu,
                                         scale=sh1c_s[:, 8 * og + b:
                                                      8 * og + b + 1]))
            for og in range(4):
                for ic in range(32):
                    gi = 32 * og + ic
                    wait(e, q_pb(ss, ic))
                    if gi >= 4:
                        wait(e, tP(ss, T_h2(*divmod(gi - 4, 32))))
                    sl = 2 * (gi % 4)
                    Q(f"hbl{gi % 4}").dma(e.dma_start(out=hbld[sl][:, :],
                                                      in_=phbh[ic, :, :]))
                    Q(f"hbl{gi % 4}").dma(e.dma_start(out=hbld[sl + 1][:, :],
                                                      in_=phbl[ic, :, :]))
                wait(e, tP(ss, T_h2(og, 31)))
                for b in range(8):
                    gi2 = 8 * og + b
                    if gi2 >= 2:
                        wait(e, vP(ss, 64 + gi2 - 1))
                    asm.inc(e.activation(out=t1f[b % 2][:, :],
                                         in_=pbank[b][:, :], func=AF.Gelu,
                                         scale=sh2c_s[:, 8 * og + b:
                                                      8 * og + b + 1]))
            for ic in range(32):
                wait(e, q_pc(ss, ic))
                if ic >= 2:
                    wait(e, tP(ss, T_w3(ic - 2)))
                sl = 2 * (ic % 2)
                Q(f"hcl{ic % 2}").dma(e.dma_start(out=hcld[sl][:, :],
                                                  in_=phch[ic, :, :]))
                Q(f"hcl{ic % 2}").dma(e.dma_start(out=hcld[sl + 1][:, :],
                                                  in_=phcl[ic, :, :]))

    block.scalar(p6_act)

    def p6_tsec(teng):
        for i in range(8):
            wait(teng, bankfree[i])
            bankfree[i] = None
        for ss in range(NTG):
            for oc in range(32):
                bA = pbank[4 * (oc % 2)]
                bB = pbank[4 * (oc % 2) + 1]
                wait(teng, q_w12(ss, oc), q_xn(ss))
                if oc >= 2:
                    wait(teng, vP(ss, V_w12(oc - 2)), aP(ss, A_w12(oc - 2)))
                elif ss >= 1:
                    wait(teng, vP(ss - 1, VPS), aP(ss - 1, APS))
                last = None
                for kc in range(8):
                    lw1 = w12s[2 * (oc % 4)][:, 128 * kc:128 * (kc + 1)]
                    teng.matmul(bA[:, :], lw1, xnslh[:, 512 * kc:512 * (kc + 1)],
                                start=(kc == 0), stop=False)
                    last = teng.matmul(bA[:, :], lw1,
                                       xnsll[:, 512 * kc:512 * (kc + 1)],
                                       start=False, stop=(kc == 7))
                ts.inc(last)
                last = None
                for kc in range(8):
                    lw2 = w12s[2 * (oc % 4) + 1][:, 128 * kc:128 * (kc + 1)]
                    teng.matmul(bB[:, :], lw2, xnslh[:, 512 * kc:512 * (kc + 1)],
                                start=(kc == 0), stop=False)
                    last = teng.matmul(bB[:, :], lw2,
                                       xnsll[:, 512 * kc:512 * (kc + 1)],
                                       start=False, stop=(kc == 7))
                ts.inc(last)
            for stg in range(2):
                Tfn = (T_h1, T_h2)[stg]
                Afn = (A_w12, A_h1)[stg]
                for og in range(4):
                    if og == 0 and stg == 0:
                        wait(teng, aP(ss, A_w12(31)), vP(ss, V_w12(31)))
                    if og == 0 and stg == 1:
                        wait(teng, aP(ss, A_h1(3, 7)), vP(ss, V_h1(3, 7)))
                    if og >= 1:
                        wait(teng, aP(ss, (A_h1, A_h2)[stg](og - 1, 7)))
                    for ic in range(32):
                        gi = 32 * og + ic
                        if stg == 0:
                            wait(teng, q_h1s(ss, og, ic), q_hal(ss, og, ic))
                            slab = h1s[gi % 4]
                            ldp = hald
                        else:
                            wait(teng, q_h2s(ss, og, ic), q_hbl(ss, og, ic))
                            slab = h2s[gi % 4]
                            ldp = hbld
                        sl = 2 * (gi % 4)
                        last = None
                        for o in range(8):
                            lw = slab[:, 128 * o:128 * (o + 1)]
                            teng.matmul(pbank[o][:, :], lw, ldp[sl][:, :],
                                        start=(ic == 0), stop=False)
                            last = teng.matmul(pbank[o][:, :], lw,
                                               ldp[sl + 1][:, :],
                                               start=False, stop=(ic == 31))
                        ts.inc(last)
            for ic in range(32):
                if ic == 0:
                    wait(teng, aP(ss, A_h2(3, 7)), vP(ss, V_h2(3, 7)))
                wait(teng, q_w3s(ss, ic), q_hcl(ss, ic))
                sl = 2 * (ic % 2)
                last = None
                for o3 in range(8):
                    lw = w3s[ic % 2][:, 128 * o3:128 * (o3 + 1)]
                    teng.matmul(pbank[o3][:, :], lw, hcld[sl][:, :],
                                start=(ic == 0), stop=False)
                    last = teng.matmul(pbank[o3][:, :], lw, hcld[sl + 1][:, :],
                                       start=False, stop=(ic == 31))
                ts.inc(last)
            for o3 in range(8):
                wait(teng, vP(ss, V_w3(o3)))
                if o3 >= 2:
                    wait(teng, vP(ss, V_tr(o3 - 2)))
                last = None
                for ttl in range(4):
                    last = teng.transpose(
                        pbank[o3][:, 128 * ttl:128 * (ttl + 1)],
                        y0sm[o3 % 2][:, 128 * ttl:128 * (ttl + 1)], ident_s[:, :])
                ts.inc(last)

    block.tensor(p6_tsec)

    def p6_vec(e):
        for ss in range(NTG):
            for oc in range(32):
                wait(e, tP(ss, T_w12(oc, 1)), aP(ss, A_w12(oc)))
                sl = 2 * (oc % 2)
                if oc >= 2:
                    wait(e, q_pa(ss, oc - 2))
                elif ss >= 1:
                    wait(e, q_pc(ss - 1, 30 + oc))
                e.scalar_tensor_tensor(out=haf[oc % 2][:, :],
                                       in0=pbank[4 * (oc % 2) + 1][:, :],
                                       scalar=s2c_s[:, oc:oc + 1],
                                       in1=t1f[oc % 2][:, :],
                                       op0=ALU.mult, op1=ALU.mult)
                e.tensor_copy(out=pstg[sl][:, :], in_=haf[oc % 2][:, :])
                vs.inc(e.tensor_tensor(out=pstg[sl + 1][:, :],
                                       in0=haf[oc % 2][:, :],
                                       in1=pstg[sl][:, :], op=ALU.subtract))
            for stgi, (Vp, Ap, qprev, qpark_prev) in enumerate((
                    (V_h1, A_h1, q_pa, q_pb), (V_h2, A_h2, q_pb, q_pc))):
                for og in range(4):
                    for b in range(8):
                        gi2 = 8 * og + b
                        wait(e, aP(ss, Ap(og, b)))
                        sl = 2 * (gi2 % 2)
                        if gi2 >= 2:
                            wait(e, qpark_prev(ss, gi2 - 2))
                        else:
                            wait(e, qprev(ss, 30 + gi2))
                        e.tensor_copy(out=pstg[sl][:, :], in_=t1f[b % 2][:, :])
                        vs.inc(e.tensor_tensor(out=pstg[sl + 1][:, :],
                                               in0=t1f[b % 2][:, :],
                                               in1=pstg[sl][:, :],
                                               op=ALU.subtract))
            def w3ev(o3):
                wait(e, tP(ss, T_w3(31)))
                if o3 >= 2:
                    wait(e, tP(ss, T_tr(o3 - 2)))
                vs.inc(e.tensor_scalar(out=y0sm[o3 % 2][:, :],
                                       in0=pbank[o3][:, :],
                                       scalar1=s3c_s[:, o3:o3 + 1],
                                       scalar2=None, op0=ALU.mult))

            def trev(o3):
                wait(e, tP(ss, T_tr(o3)))
                if o3 == 0 and ss >= 1:
                    wait(e, q_y0(ss - 1))
                ins = None
                for ttl in range(4):
                    ins = e.tensor_copy(
                        out=y0r[ttl][:, 128 * o3:128 * (o3 + 1)],
                        in_=pbank[o3][:, 128 * ttl:128 * (ttl + 1)])
                vs.inc(ins)

            for o3 in range(8):
                w3ev(o3)
                if o3 >= 1:
                    trev(o3 - 1)
            trev(7)

    block.vector(p6_vec)
    for i in range(8):
        bankfree[i] = (vs, p6["v"] + VPS * NTG)
    assert ts.n == p6["t"] + TPS * NTG, (ts.n, p6["t"])
    assert vs.n == p6["v"] + VPS * NTG
    assert asm.n == p6["a"] + APS * NTG
    AB16.pop()
    AF32.pop()


def _p7(block, X):
    g = X
    NT = g["NT"]; TC4 = g["TC4"]; st = g["st"]; Q = g["Q"]
    vs = g["vs"]; cc = g["cc"]
    AF32 = g["AF32"]
    x_rows = g["x_rows"]; pagout = g["pagout"]; py0ffn = g["py0ffn"]
    px1 = g["px1"]; y = g["y"]; bob = g["bob"]; self8 = g["self8"]

    AF32.mark()
    xa = [AF32.alloc(f"xa{i}", [128, D]) for i in range(2)]
    agc = [AF32.alloc(f"agc{i}", [128, 8 * D]) for i in range(2)]
    ffb = [AF32.alloc(f"ffb{i}", [128, D]) for i in range(2)]
    x1b = [AF32.alloc(f"x1b{i}", [128, D]) for i in range(2)]

    p7v = vs.n
    p7_ld = {}

    def p7_outpark(e, tt):
        wait(e, (vs, p7v + tt + 1))
        Q(f"y7{tt % 2}").dma(e.dma_start(
            out=px1[128 * tt:128 * (tt + 1), :], in_=x1b[tt % 2][:, :]))
        Q(f"y7{tt % 2}").dma(e.dma_start(
            out=y[128 * tt:128 * (tt + 1), :], in_=x1b[tt % 2][:, :]))

    def p7_sync(e):
        wait(e, cc.tok(), st["p6_park"])
        for tt in range(NT):
            if tt >= 2:
                wait(e, (vs, p7v + (tt - 2) + 1))
            sem = Q(f"l7{tt % 2}")
            sem.dma(e.dma_start(
                out=xa[tt % 2][:, :], in_=x_rows[128 * tt:128 * (tt + 1), :]))
            R = TC4 // 128
            for b in range(2):
                for h in range(NH):
                    srcrow = ((4 * b + tt // R) * (NH * TC4) + h * TC4 +
                              128 * (tt % R))
                    sem.dma(e.dma_start(
                        out=agc[tt % 2][:, D * (4 * b + h):D * (4 * b + h + 1)],
                        in_=pagout[srcrow:srcrow + 128, :]))
            p7_ld[tt] = sem.dma(e.dma_start(
                out=ffb[tt % 2][:, :], in_=py0ffn[128 * tt:128 * (tt + 1), :]))
            if tt >= 2:
                p7_outpark(e, tt - 2)
        p7_outpark(e, NT - 2)
        p7_outpark(e, NT - 1)
        st["p7_park"] = [Q("y70").tok(), Q("y71").tok()]

    def p7_vec(e):
        for tt in range(NT):
            wait(e, p7_ld[tt])
            if tt >= 2:
                wait(e, (Q(f"y7{tt % 2}"), 32 * (tt // 2)))
            xb = x1b[tt % 2]
            ag = agc[tt % 2]
            e.tensor_scalar(out=xb[:, :], in0=ag[:, 0:D],
                            scalar1=self8[:, 0:1], scalar2=None, op0=ALU.mult)
            for i in range(1, 8):
                e.scalar_tensor_tensor(out=xb[:, :], in0=ag[:, D * i:D * (i + 1)],
                                       scalar=self8[:, i:i + 1], in1=xb[:, :],
                                       op0=ALU.mult, op1=ALU.add)
            e.tensor_tensor(out=xb[:, :], in0=xb[:, :], in1=xa[tt % 2][:, :],
                            op=ALU.add)
            e.tensor_tensor(out=xb[:, :], in0=xb[:, :], in1=bob[:, :],
                            op=ALU.add)
            vs.inc(e.tensor_tensor(out=xb[:, :], in0=xb[:, :],
                                   in1=ffb[tt % 2][:, :], op=ALU.add))

    block.sync(p7_sync)
    block.vector(p7_vec)

    def p7_fin(e):
        wait(e, *st["p7_park"])

    block.sync(p7_fin)
    AF32.pop()


def host_inputs(n_tok, core_id, full):
    TC = n_tok // W
    j = core_id

    def blockscale(s, noc):
        return np.ascontiguousarray(s.reshape(noc, 128).T).astype(np.float32)

    sel = np.zeros((1, 8), np.float32)
    sel[0, 4 * (j % 2) + (j // 2)] = 1.0
    d = {
        "x_rows": full["L0_attn_x"][0, j * TC:(j + 1) * TC, :],
        "phase": full["L0_attn_phase"],
        "Wq": full["L0_attn_Wq"], "Wk": full["L0_attn_Wk"], "Wv": full["L0_attn_Wv"],
        "bq": full["L0_attn_bq"].reshape(1, -1),
        "bk": full["L0_attn_bk"].reshape(1, -1),
        "bv": full["L0_attn_bv"].reshape(1, -1),
        "Wo": full["L0_attn_Wo"], "bo": full["L0_attn_bo"].reshape(1, -1),
        "w1T": np.ascontiguousarray(full["L0_ffn_w1_w"].T),
        "w2T": np.ascontiguousarray(full["L0_ffn_w2_w"].T),
        "h1T": np.ascontiguousarray(full["L0_ffn_h1_w"].T),
        "h2T": np.ascontiguousarray(full["L0_ffn_h2_w"].T),
        "w3T": np.ascontiguousarray(full["L0_ffn_w3_w"].T),
        "s1c": blockscale(full["L0_ffn_w1_s"], 32),
        "s2c": blockscale(full["L0_ffn_w2_s"], 32),
        "sh1c": blockscale(full["L0_ffn_h1_s"], 32),
        "sh2c": blockscale(full["L0_ffn_h2_s"], 32),
        "s3c": blockscale(full["L0_ffn_w3_s"], 8),
        "identf": np.eye(128, dtype=np.float32),
        "selmask": sel,
        "wq1T": np.ascontiguousarray(full["L1_attn_Wq1"].T),
        "wk1T": np.ascontiguousarray(full["L1_attn_Wk1"].T),
        "wk2T": np.ascontiguousarray(full["L1_attn_Wk2"].T),
        "wpjT": np.ascontiguousarray(full["L1_attn_Wproj"].T),
        "bq1r": full["L1_attn_bq1"].reshape(1, -1),
        "bk1r": full["L1_attn_bk1"].reshape(1, -1),
        "bk2r": full["L1_attn_bk2"].reshape(1, -1),
        "bpjr": full["L1_attn_bproj"].reshape(1, -1),
        "rhT": np.ascontiguousarray(full["L1_ffn_rhs"].T),
        "prevm8": (np.arange(8) < j).astype(np.float32).reshape(8, 1),
        "prevo8": (np.arange(8) == j - 1).astype(np.float32).reshape(8, 1),
        "e0m": (np.arange(128) != 0).astype(np.float32).reshape(128, 1)
        if j == 0 else np.ones((128, 1), np.float32),
        "triu128": np.triu(np.ones((128, 128), np.float32)),
        "bc127": np.vstack([np.zeros((127, 128), np.float32),
                            np.ones((1, 128), np.float32)]),
    }
    return {k: np.ascontiguousarray(np.asarray(v, np.float32)) for k, v in d.items()}


def run(n_tok, full):
    nc = build(n_tok)
    in_maps = [host_inputs(n_tok, j, full) for j in range(W)]
    res = run_bass_kernel_spmd(nc, in_maps, core_ids=list(range(W)))
    out = np.concatenate([res.results[j]["y"] for j in range(W)], axis=0)
    return out.reshape(1, n_tok, D)


def _p8(block, X):
    g = X
    NT = g["NT"]; st = g["st"]; Q = g["Q"]
    ts = g["ts"]; vs = g["vs"]; asm = g["asm"]
    pbank = g["pbank"]; bankfree = g["bankfree"]
    AF32 = g["AF32"]; AB16 = g["AB16"]
    ident_s = g["ident_s"]; epscol = g["epscol"]
    px1 = g["px1"]; pxn1Th = g["pxn1Th"]; pxn1Tl = g["pxn1Tl"]; pxn1 = g["pxn1"]

    AB16.mark()
    xnTh = AB16.alloc("x8Th", [128, 8 * NT * 128])
    xnTl = AB16.alloc("x8Tl", [128, 8 * NT * 128])
    AF32.mark()
    xin = [AF32.alloc(f"x8in{i}", [128, D]) for i in range(2)]
    xnb = [AF32.alloc(f"x8nb{i}", [128, D]) for i in range(2)]
    strash = AF32.alloc("s8tr", [128, D])
    c_sq = AF32.alloc("c8sq", [128, 16])
    c_sum = AF32.alloc("c8sum", [128, 16])
    c_mu = AF32.alloc("c8mu", [128, 16])
    c_mu2 = AF32.alloc("c8mu2", [128, 16])
    c_sq2 = AF32.alloc("c8sq2", [128, 16])
    c_var = AF32.alloc("c8var", [128, 16])
    c_std = AF32.alloc("c8std", [128, 16])
    c_rstd = AF32.alloc("c8rstd", [128, 16])
    c_nmr = AF32.alloc("c8nmr", [128, 16])
    c_ngm = AF32.alloc("c8ngm", [128, 16])
    fil1 = AF32.alloc("f8il", [128, 16])
    ttmp = [AF32.alloc(f"t8mp{i}", [128, 128]) for i in range(2)]

    p8_ld = []
    p8a = asm.n
    p8v = vs.n
    p8t = ts.n
    t_stats = {}
    t_nm = {}
    t_ev = {}
    for tt in range(NT):
        p8_ld.append((Q(f"z{tt % 2}"), Q(f"z{tt % 2}").n + 16 * (tt // 2 + 1)))

    def p8_evac(e, tt, dc):
        col = dc * (NT * 128) + tt * 128
        tb = ttmp[dc % 2]
        wait(e, (ts, p8t + 8 * tt + dc + 1))
        e.tensor_copy(out=tb[:, :], in_=pbank[4 + (8 * tt + dc) % 4][:, 0:128])
        e.tensor_copy(out=xnTh[:, col:col + 128], in_=tb[:, :])
        t_ev[(tt, dc)] = vs.inc(e.tensor_tensor(
            out=xnTl[:, col:col + 128], in0=tb[:, :],
            in1=xnTh[:, col:col + 128], op=ALU.subtract))

    def p8_fill(e, n):
        for _ in range(n):
            e.tensor_copy(out=fil1[:, 0:1], in_=epscol[:, 0:1])

    def p8_vec(e):
        for tt in range(NT):
            xi = xin[tt % 2]
            wait(e, p8_ld[tt], (asm, p8a + 3 * tt + 1))
            e.reduce_sum(out=c_sum[:, 0:1], in_=xi[:, :], axis=AX)
            e.tensor_scalar(out=c_sq2[:, 0:1], in0=c_sq[:, 0:1],
                            scalar1=1.0 / D, scalar2=None, op0=ALU.mult)
            if tt >= 1:
                for dc in range(4):
                    p8_evac(e, tt - 1, dc)
            else:
                p8_fill(e, 4)
            e.tensor_scalar(out=c_mu[:, 0:1], in0=c_sum[:, 0:1],
                            scalar1=1.0 / D, scalar2=None, op0=ALU.mult)
            e.tensor_scalar(out=c_ngm[:, 0:1], in0=c_sum[:, 0:1],
                            scalar1=-1.0 / D, scalar2=None, op0=ALU.mult)
            if tt >= 1:
                for dc in range(4, 6):
                    p8_evac(e, tt - 1, dc)
            else:
                p8_fill(e, 4)
            e.tensor_tensor(out=c_mu2[:, 0:1], in0=c_mu[:, 0:1],
                            in1=c_mu[:, 0:1], op=ALU.mult)
            if tt >= 1:
                for dc in range(6, 8):
                    p8_evac(e, tt - 1, dc)
            else:
                p8_fill(e, 4)
            t_stats[tt] = vs.inc(e.tensor_tensor(
                out=c_var[:, 0:1], in0=c_sq2[:, 0:1],
                in1=c_mu2[:, 0:1], op=ALU.subtract))
            wait(e, (asm, p8a + 3 * tt + 2))
            e.reciprocal(out=c_rstd[:, 0:1], in_=c_std[:, 0:1])
            p8_fill(e, 4)
            t_nm[tt] = vs.inc(e.scalar_tensor_tensor(
                out=c_nmr[:, 0:1], in0=c_ngm[:, 0:1], scalar=1.0,
                in1=c_rstd[:, 0:1], op0=ALU.mult, op1=ALU.mult))
        for dc in range(8):
            p8_evac(e, NT - 1, dc)

    block.vector(p8_vec)

    def p8_sync(e):
        wait(e, *st["p7_park"])
        for tt in range(NT):
            if tt >= 2:
                wait(e, (asm, p8a + 3 * (tt - 2) + 3))
            Q(f"z{tt % 2}").dma(e.dma_start(
                out=xin[tt % 2][:, :], in_=px1[128 * tt:128 * (tt + 1), :]))
            if tt >= 1:
                wait(e, (asm, p8a + 3 * (tt - 1) + 3))
                Q("p8r").dma(e.dma_start(
                    out=pxn1[1 + 128 * (tt - 1):1 + 128 * tt, :],
                    in_=xnb[(tt - 1) % 2][:, :]))
        wait(e, (asm, p8a + 3 * (NT - 1) + 3))
        Q("p8r").dma(e.dma_start(
            out=pxn1[1 + 128 * (NT - 1):1 + 128 * NT, :],
            in_=xnb[(NT - 1) % 2][:, :]))
        st["p8_rows"] = Q("p8r").tok()

    block.sync(p8_sync)

    def p8_act(e):
        for tt in range(NT):
            wait(e, p8_ld[tt], st["c_mem"])
            if tt >= 1:
                wait(e, t_stats[tt - 1])
            asm.inc(e.activation(out=strash[:, :], in_=xin[tt % 2][:, :],
                                 func=AF.Square, accum_out=c_sq[:, 0:1]))
            wait(e, t_stats[tt])
            asm.inc(e.activation(out=c_std[:, 0:1], in_=c_var[:, 0:1],
                                 func=AF.Sqrt, bias=epscol[0:128, 0:1]))
            wait(e, t_nm[tt])
            if tt >= 2:
                wait(e, (ts, p8t + 8 * (tt - 2) + 8),
                     (Q("p8r"), 16 * (tt - 1)))
            asm.inc(e.activation(out=xnb[tt % 2][:, :], in_=xin[tt % 2][:, :],
                                 func=AF.Identity,
                                 bias=c_nmr[0:128, 0:1],
                                 scale=c_rstd[0:128, 0:1]))

    block.scalar(p8_act)

    def p8_tsec(teng):
        for i in range(4, 8):
            wait(teng, bankfree[i])
            bankfree[i] = None
        for tt in range(NT):
            for dc in range(8):
                gdx = 8 * tt + dc
                wait(teng, (asm, p8a + 3 * tt + 3))
                if gdx >= 4:
                    ptt, pdc = divmod(gdx - 4, 8)
                    wait(teng, t_ev[(ptt, pdc)])
                ts.inc(teng.transpose(
                    pbank[4 + gdx % 4][:, 0:128],
                    xnb[tt % 2][:, 128 * dc:128 * (dc + 1)], ident_s[:, :]))

    block.tensor(p8_tsec)
    for i in range(4, 8):
        bankfree[i] = (vs, vs.n)
    st["p8_done"] = vs.tok()
    assert vs.n == p8v + 10 * NT
    assert ts.n == p8t + 8 * NT
    assert asm.n == p8a + 3 * NT

    def p8_park(e):
        wait(e, st["p8_done"])
        for dc in range(8):
            Q("p8pk").dma(e.dma_start(
                out=pxn1Th[dc, :, :],
                in_=xnTh[:, dc * NT * 128:(dc + 1) * NT * 128]))
            Q("p8pk").dma(e.dma_start(
                out=pxn1Tl[dc, :, :],
                in_=xnTl[:, dc * NT * 128:(dc + 1) * NT * 128]))
        st["p8_park"] = Q("p8pk").tok()

    block.sync(p8_park)
    AF32.pop()
    AB16.pop()


def _p9(block, X):
    g = X
    NT = g["NT"]; NTG = g["NTG"]; TC = g["TC"]; st = g["st"]; Q = g["Q"]
    ts = g["ts"]; vs = g["vs"]; asm = g["asm"]; gs = g["gs"]; cc = g["cc"]
    pbank = g["pbank"]; bankfree = g["bankfree"]
    AF32 = g["AF32"]; AB16 = g["AB16"]
    ones1f_s = g["ones1f_s"]; onescol_s = g["onescol_s"]
    prevm_s = g["prevm_s"]; prevo_s = g["prevo_s"]; epscol = g["epscol"]
    pxn1Th = g["pxn1Th"]; pxn1Tl = g["pxn1Tl"]; pxn1 = g["pxn1"]
    pq1 = g["pq1"]; poar = g["poar"]; pk1 = g["pk1"]; pkv1 = g["pkv1"]
    pk21 = g["pk21"]
    pag1in = g["pag1in"]; pag1out = g["pag1out"]; prow9 = g["prow9"]
    pprefs = g["pprefs"]
    WT = {0: g["wk1T"], 1: g["wq1T"], 2: g["wk2T"]}
    BR = {0: g["bk1r"], 1: g["bq1r"], 2: g["bk2r"]}
    C0232 = float(np.float32(0.02) / 32.0)

    AF32.mark()
    wtmp9 = AF32.alloc("wtmp9", [128, 4096])
    b9r = AF32.alloc("b9r", [128, 1024])
    ebuf = [AF32.alloc(f"e9b{i}", [128, 1024]) for i in range(2)]
    xnr9 = [AF32.alloc(f"x9r{i}", [128, 1024]) for i in range(2)]
    kvb = [AF32.alloc(f"k9v{i}", [128, 1024]) for i in range(2)]
    g9 = AF32.alloc("g9", [128, 4096])
    r9 = AF32.alloc("r9", [128, 8192])
    AB16.mark()
    w9h = AB16.alloc("w9h", [128, 8192])
    w9l = AB16.alloc("w9l", [128, 8192])
    xslh = [AB16.alloc(f"xs9h{i}", [128, 4096]) for i in range(2)]
    xsll = [AB16.alloc(f"xs9l{i}", [128, 4096]) for i in range(2)]
    scrh = AB16.alloc("scr9h", [128, 1024])
    scrl = AB16.alloc("scr9l", [128, 1024])
    brow = AB16.alloc("b9row", [128, 2048])
    bndh = AB16.alloc("bnd9h", [128, 16])
    bndl = AB16.alloc("bnd9l", [128, 16])

    p9t = ts.n
    p9a = asm.n
    p9v = vs.n
    p9g = gs.n
    B9 = {n: Q(n).n for n in (
        "wl9", "b9", "x90", "x91", "kr90", "kr91", "p9k0", "p9k1",
        "p9v0", "p9v1", "p9q0", "p9q1", "p9c0", "p9c1", "a1p", "mg9",
        "pr9s", "bd9")}

    def R9(k):
        return r9[0:1, 1024 * k:1024 * (k + 1)]

    def T9mm(pj, tt, ob):
        if pj == 0:
            return (ts, p9t + 6 * tt + ob + 1)
        if pj == 1:
            return (ts, p9t + 6 * NT + 2 * tt + ob + 1)
        return (ts, p9t + 8 * NT + 8 + 2 * tt + ob + 1)

    def T9tot(tt, q):
        return (ts, p9t + 6 * tt + 2 + q + 1)

    def T9msk(q):
        return (ts, p9t + 8 * NT + q + 1)

    def A9(pj, tt, ob):
        return (asm, p9a + 2 * NT * pj + 2 * tt + ob + 1)

    def V9kv(tt):
        return (vs, p9v + tt + 1)
    V9totrow = (vs, p9v + NT + 1)

    def V9msk(q):
        return (vs, p9v + NT + 1 + q + 1)
    V9rows = (vs, p9v + NT + 10)

    def V9scr(ss):
        return (vs, p9v + NT + 10 + ss + 1)

    def G9sp(pj, ob):
        return (gs, p9g + 2 * pj + ob + 1)

    def q_w9(pj, ob):
        return (Q("wl9"), B9["wl9"] + 16 * (2 * pj + ob + 1))

    def q_x9(pj, ss):
        sl = ss % 2
        return (Q(f"x9{sl}"), B9[f"x9{sl}"] +
                32 * (pj * ((NTG + 1 - sl) // 2) + ss // 2 + 1))

    def q_b9(pj):
        return (Q("b9"), B9["b9"] + 16 * (pj + 1))

    def q_xr9(tt):
        return (Q(f"kr9{tt % 2}"), B9[f"kr9{tt % 2}"] + 16 * (tt // 2 + 1))

    def q_pk(tt):
        return (Q(f"p9k{tt % 2}"), B9[f"p9k{tt % 2}"] + 16 * (tt // 2 + 1))

    def q_pq(tt):
        return (Q(f"p9q{tt % 2}"), B9[f"p9q{tt % 2}"] + 16 * (tt // 2 + 1))

    q_gath = (Q("mg9"), B9["mg9"] + 64)
    q_bd = (Q("bd9"), B9["bd9"] + 32)

    def p9_sync(e):
        for pj in range(3):
            if pj >= 1:
                wait(e, T9mm(pj - 1, NT - 1, 1))
            else:
                wait(e, st["p8_park"], st["p8_rows"])
            for ob in range(2):
                if not (pj == 0 and ob == 0):
                    wait(e, G9sp(pj, ob - 1) if ob == 1 else G9sp(pj - 1, 1))
                Q("wl9").dma(e.dma_start(
                    out=wtmp9[:, :],
                    in_=WT[pj][:, 512 * ob:512 * (ob + 1)].rearrange(
                        "(kc p) n -> p kc n", p=128)))
            Q("b9").dma(e.dma_start(out=b9r[0:1, :], in_=BR[pj][0:1, :]))
            for ss in range(NTG):
                sl = ss % 2
                if ss >= 2:
                    wait(e, T9mm(pj, 4 * (ss - 2) + 3, 1))
                if pj == 2 and ss >= 1:
                    wait(e, V9scr(ss - 1))
                Q(f"x9{sl}").dma(e.dma_start(
                    out=xslh[sl][:, :].rearrange("p (k t) -> p k t", k=8),
                    in_=pxn1Th[:, :, 512 * ss:512 * (ss + 1)].rearrange(
                        "k p t -> p k t")))
                Q(f"x9{sl}").dma(e.dma_start(
                    out=xsll[sl][:, :].rearrange("p (k t) -> p k t", k=8),
                    in_=pxn1Tl[:, :, 512 * ss:512 * (ss + 1)].rearrange(
                        "k p t -> p k t")))
                for tl in range(4):
                    tt = 4 * ss + tl
                    if pj == 0:
                        if tt >= 2:
                            wait(e, V9kv(tt - 2))
                        Q(f"kr9{tt % 2}").dma(e.dma_start(
                            out=xnr9[tt % 2][:, :],
                            in_=pxn1[1 + 128 * tt:1 + 128 * (tt + 1), :]))
                    if tt >= 1:
                        ptt = tt - 1
                        if pj == 0:
                            wait(e, A9(0, ptt, 1))
                            Q(f"p9k{ptt % 2}").dma(e.dma_start(
                                out=pk1[128 * ptt:128 * (ptt + 1), :],
                                in_=ebuf[ptt % 2][:, :]))
                            wait(e, V9kv(ptt))
                            Q(f"p9v{ptt % 2}").dma(e.dma_start(
                                out=pkv1[128 * ptt:128 * (ptt + 1), :],
                                in_=kvb[ptt % 2][:, :]))
                        elif pj == 1:
                            wait(e, A9(1, ptt, 1))
                            Q(f"p9q{ptt % 2}").dma(e.dma_start(
                                out=pq1[1 + 128 * ptt:1 + 128 * (ptt + 1), :],
                                in_=ebuf[ptt % 2][:, :]))
                        else:
                            wait(e, A9(2, ptt, 1))
                            Q(f"p9c{ptt % 2}").dma(e.dma_start(
                                out=pk21[128 * ptt:128 * (ptt + 1), :],
                                in_=ebuf[ptt % 2][:, :]))
            tt = NT - 1
            if pj == 0:
                wait(e, A9(0, tt, 1))
                Q(f"p9k{tt % 2}").dma(e.dma_start(
                    out=pk1[128 * tt:128 * (tt + 1), :], in_=ebuf[tt % 2][:, :]))
                wait(e, V9kv(tt))
                Q(f"p9v{tt % 2}").dma(e.dma_start(
                    out=pkv1[128 * tt:128 * (tt + 1), :], in_=kvb[tt % 2][:, :]))
            elif pj == 1:
                wait(e, A9(1, tt, 1))
                Q(f"p9q{tt % 2}").dma(e.dma_start(
                    out=pq1[1 + 128 * tt:1 + 128 * (tt + 1), :],
                    in_=ebuf[tt % 2][:, :]))
                wait(e, V9totrow)
                Q("a1p").dma(e.dma_start(out=pag1in[0:1, :],
                                         in_=g9[0:1, 0:1024]))
                Q("a1p").dma(e.dma_start(out=pag1in[1:2, :],
                                         in_=g9[0:1, 1024:2048]))
                wait(e, q_pq(NT - 1))
                Q("a1p").dma(e.dma_start(out=pag1in[2:3, :],
                                         in_=pq1[TC:TC + 1, :]))
                Q("a1p").dma(e.dma_start(out=pag1in[3:4, :],
                                         in_=pxn1[TC:TC + 1, :]))
                wait(e, cc.at(2))
                for qd in range(4):
                    Q("mg9").dma(e.dma_start(
                        out=g9[0:8, 1024 * qd:1024 * (qd + 1)],
                        in_=pag1out[qd::4, :]))
                wait(e, V9rows)
                Q("mg9").dma(e.dma_start(out=pq1[0:1, :], in_=R9(2)))
                Q("mg9").dma(e.dma_start(out=poar[0:1, :], in_=R9(7)))
                Q("pr9s").dma(e.dma_start(out=prow9[0:1, :],
                                          in_=brow[0:1, 0:1024]))
                Q("pr9s").dma(e.dma_start(out=prow9[1:2, :],
                                          in_=brow[0:1, 1024:2048]))
                Q("pr9s").dma(e.dma_start(out=pprefs[0:1, :], in_=R9(4)))
                Q("pr9s").dma(e.dma_start(out=pprefs[1:2, :], in_=R9(1)))
                wait(e, (Q("pr9s"), B9["pr9s"] + 32))
                Q("bd9").dma(e.dma_start(
                    out=bndh[:, 0:8],
                    in_=prow9[0:1, :].rearrange("r (k p) -> p (r k)", p=128),
                    allow_slow_non_contiguous=True))
                Q("bd9").dma(e.dma_start(
                    out=bndl[:, 0:8],
                    in_=prow9[1:2, :].rearrange("r (k p) -> p (r k)", p=128),
                    allow_slow_non_contiguous=True))
        tt = NT - 1
        wait(e, A9(2, tt, 1))
        Q(f"p9c{tt % 2}").dma(e.dma_start(
            out=pk21[128 * tt:128 * (tt + 1), :], in_=ebuf[tt % 2][:, :]))
        st["p9_kpark"] = [Q("p9k0").tok(), Q("p9k1").tok()]
        st["p9_vpark"] = [Q("p9v0").tok(), Q("p9v1").tok()]
        st["p9_qpark"] = [Q("p9q0").tok(), Q("p9q1").tok()]
        st["p9_cpark"] = [Q("p9c0").tok(), Q("p9c1").tok()]
        st["p9_bnd"] = Q("mg9").tok()
        st["p9_prefs"] = (Q("pr9s"), B9["pr9s"] + 64)

    block.sync(p9_sync)

    def p9_gps(e):
        for pj in range(3):
            for ob in range(2):
                if pj == 2 and ob == 0:
                    wait(e, (Q("a1p"), B9["a1p"] + 64))
                    cc.inc(e.collective_compute(
                        "AllGather", ALU.bypass,
                        replica_groups=[list(range(W))],
                        ins=[pag1in[:, :]], outs=[pag1out[:, :]]))
                wait(e, q_w9(pj, ob))
                if pj + ob == 0:
                    wait(e, st["p8_park"])
                e.tensor_copy(out=w9h[:, 4096 * ob:4096 * (ob + 1)],
                              in_=wtmp9[:, :])
                e.tensor_tensor(out=wtmp9[:, :], in0=wtmp9[:, :],
                                in1=w9h[:, 4096 * ob:4096 * (ob + 1)],
                                op=ALU.subtract)
                gs.inc(e.tensor_copy(out=w9l[:, 4096 * ob:4096 * (ob + 1)],
                                     in_=wtmp9[:, :]))

    block.gpsimd(p9_gps)

    def p9_tsec(teng):
        for i in range(4):
            wait(teng, bankfree[i])
            bankfree[i] = None
        for i in range(4, 8):
            wait(teng, bankfree[i])
            bankfree[i] = None
        G = 0
        for pj in range(3):
            for ss in range(NTG):
                sl = ss % 2
                for tl in range(4):
                    tt = 4 * ss + tl
                    for ob in range(2):
                        wait(teng, q_x9(pj, ss), G9sp(pj, ob), q_b9(pj))
                        if pj == 2 and tl == 0:
                            wait(teng, V9scr(ss))
                        bk = pbank[G % 4]
                        if G >= 4:
                            pG = G - 4
                            ppj, prem = divmod(pG, 2 * NT)
                            ptt, pob = divmod(prem, 2)
                            wait(teng, A9(ppj, ptt, pob))
                        last = None
                        for kc in range(8):
                            if pj == 2 and tl == 0:
                                lh = scrh[:, 128 * kc:128 * (kc + 1)]
                                ll = scrl[:, 128 * kc:128 * (kc + 1)]
                            elif pj == 2:
                                c0 = 512 * kc + 128 * tl - 1
                                lh = xslh[sl][:, c0:c0 + 128]
                                ll = xsll[sl][:, c0:c0 + 128]
                            else:
                                c0 = 512 * kc + 128 * tl
                                lh = xslh[sl][:, c0:c0 + 128]
                                ll = xsll[sl][:, c0:c0 + 128]
                            rh = w9h[:, 4096 * ob + 512 * kc:
                                     4096 * ob + 512 * (kc + 1)]
                            rl = w9l[:, 4096 * ob + 512 * kc:
                                     4096 * ob + 512 * (kc + 1)]
                            teng.matmul(bk[:, :], lh, rh,
                                        start=(kc == 0), stop=False)
                            teng.matmul(bk[:, :], lh, rl,
                                        start=False, stop=False)
                            last = teng.matmul(bk[:, :], ll, rh,
                                               start=False, stop=False)
                        last = teng.matmul(bk[:, :], ones1f_s[0:1, :],
                                           b9r[0:1, 512 * ob:512 * (ob + 1)],
                                           start=False, stop=True)
                        ts.inc(last)
                        G += 1
                    if pj == 0:
                        for q in range(2):
                            wait(teng, A9(0, tt, q))
                            ts.inc(teng.matmul(
                                pbank[4 + q][0:1, :], onescol_s[:, 0:1],
                                ebuf[tt % 2][:, 512 * q:512 * (q + 1)],
                                start=(tt == 0), stop=(tt == NT - 1)))
                        for q in range(2):
                            wait(teng, V9kv(tt))
                            ts.inc(teng.matmul(
                                pbank[6 + q][0:1, :], onescol_s[:, 0:1],
                                kvb[tt % 2][:, 512 * q:512 * (q + 1)],
                                start=(tt == 0), stop=(tt == NT - 1)))
            if pj == 1:
                for q in range(8):
                    qd, blk = divmod(q, 2)
                    lm = prevm_s[0:8, 0:1] if qd < 2 else prevo_s[0:8, 0:1]
                    wait(teng, q_gath, V9totrow)
                    if q >= 1:
                        wait(teng, V9msk(q - 1))
                    ts.inc(teng.matmul(
                        pbank[4][0:1, 0:512], lm,
                        g9[0:8, 1024 * qd + 512 * blk:
                           1024 * qd + 512 * (blk + 1)],
                        start=True, stop=True))

    block.tensor(p9_tsec)

    def p9_act(e):
        for pj in range(3):
            for tt in range(NT):
                for ob in range(2):
                    wait(e, T9mm(pj, tt, ob))
                    if tt >= 2:
                        if pj == 0:
                            wait(e, q_pk(tt - 2), V9kv(tt - 2))
                        elif pj == 1:
                            wait(e, q_pq(tt - 2))
                        else:
                            wait(e, (Q(f"p9c{tt % 2}"),
                                     B9[f"p9c{tt % 2}"] + 16 * ((tt - 2) // 2 + 1)))
                    elif pj == 1:
                        wait(e, q_pk(NT - 2 + tt), V9kv(NT - 1))
                    elif pj == 2:
                        wait(e, q_pq(NT - 2 + tt))
                    dst = ebuf[tt % 2][:, 512 * ob:512 * (ob + 1)]
                    bk = pbank[(2 * NT * pj + 2 * tt + ob) % 4]
                    if pj == 0:
                        asm.inc(e.activation(out=dst, in_=bk[:, :],
                                             func=AF.Exp))
                    elif pj == 1:
                        asm.inc(e.activation(out=dst, in_=bk[:, :],
                                             func=AF.Sigmoid))
                    else:
                        asm.inc(e.activation(out=dst, in_=bk[:, :],
                                             func=AF.Sigmoid, scale=C0232))

    block.scalar(p9_act)

    def p9_vec(e):
        for tt in range(NT):
            wait(e, A9(0, tt, 1), q_xr9(tt))
            if tt >= 2:
                wait(e, (Q(f"p9v{tt % 2}"),
                         B9[f"p9v{tt % 2}"] + 16 * ((tt - 2) // 2 + 1)))
            vs.inc(e.tensor_tensor(out=kvb[tt % 2][:, :],
                                   in0=ebuf[tt % 2][:, :],
                                   in1=xnr9[tt % 2][:, :], op=ALU.mult))
        wait(e, T9tot(NT - 1, 3))
        e.tensor_copy(out=g9[0:1, 0:512], in_=pbank[4][0:1, 0:512])
        e.tensor_copy(out=g9[0:1, 512:1024], in_=pbank[5][0:1, 0:512])
        e.tensor_copy(out=g9[0:1, 1024:1536], in_=pbank[6][0:1, 0:512])
        vs.inc(e.tensor_copy(out=g9[0:1, 1536:2048], in_=pbank[7][0:1, 0:512]))
        for q in range(8):
            qd, blk = divmod(q, 2)
            wait(e, T9msk(q))
            vs.inc(e.tensor_copy(
                out=r9[0:1, 1024 * qd + 512 * blk:1024 * qd + 512 * (blk + 1)],
                in_=pbank[4][0:1, 0:512]))
        e.tensor_scalar(out=R9(4), in0=R9(0), scalar1=1e-6, scalar2=None,
                        op0=ALU.add)
        e.reciprocal(out=R9(5), in_=R9(4))
        e.tensor_tensor(out=R9(6), in0=R9(1), in1=R9(5), op=ALU.mult)
        e.tensor_tensor(out=R9(7), in0=R9(6), in1=R9(2), op=ALU.mult)
        e.tensor_copy(out=brow[0:1, 0:1024], in_=R9(3))
        vs.inc(e.tensor_tensor(out=brow[0:1, 1024:2048], in0=R9(3),
                               in1=brow[0:1, 0:1024], op=ALU.subtract))
        for ss in range(NTG):
            sl = ss % 2
            wait(e, q_x9(2, ss))
            if ss >= 1:
                wait(e, T9mm(2, 4 * (ss - 1), 1))
            for kc in range(8):
                if ss == 0:
                    wait(e, q_bd)
                    e.tensor_copy(out=scrh[:, 128 * kc:128 * kc + 1],
                                  in_=bndh[:, kc:kc + 1])
                    e.tensor_copy(out=scrl[:, 128 * kc:128 * kc + 1],
                                  in_=bndl[:, kc:kc + 1])
                else:
                    psl = (ss - 1) % 2
                    e.tensor_copy(out=scrh[:, 128 * kc:128 * kc + 1],
                                  in_=xslh[psl][:, 512 * kc + 511:512 * (kc + 1)])
                    e.tensor_copy(out=scrl[:, 128 * kc:128 * kc + 1],
                                  in_=xsll[psl][:, 512 * kc + 511:512 * (kc + 1)])
                e.tensor_copy(out=scrh[:, 128 * kc + 1:128 * (kc + 1)],
                              in_=xslh[sl][:, 512 * kc:512 * kc + 127])
                ins = e.tensor_copy(out=scrl[:, 128 * kc + 1:128 * (kc + 1)],
                                    in_=xsll[sl][:, 512 * kc:512 * kc + 127])
            vs.inc(ins)

    block.vector(p9_vec)
    for i in range(4):
        bankfree[i] = (asm, p9a + 6 * NT)
    for i in range(4, 8):
        bankfree[i] = V9msk(8)
    assert ts.n == p9t + 8 * NT + 8 + 2 * NT
    assert asm.n == p9a + 6 * NT
    assert vs.n == p9v + NT + 10 + NTG
    assert gs.n == p9g + 6
    AB16.pop()
    AF32.pop()


def _p11(block, X):
    g = X
    NT = g["NT"]; st = g["st"]; Q = g["Q"]
    ts = g["ts"]; vs = g["vs"]; asm = g["asm"]
    pbank = g["pbank"]; bankfree = g["bankfree"]
    AF32 = g["AF32"]; AB16 = g["AB16"]
    ident_s = g["ident_s"]; ones1f_s = g["ones1f_s"]; epscol = g["epscol"]
    pxn1 = g["pxn1"]; py1f = g["py1f"]; rhT = g["rhT"]; prs1 = g["prs1"]

    AF32.mark()
    rht = [AF32.alloc(f"r11t{i}", [128, 1024]) for i in range(2)]
    xnt = [AF32.alloc(f"x11t{i}", [128, 1024]) for i in range(2)]
    qv = AF32.alloc("q11v", [128, 1024])
    qa = AF32.alloc("q11a", [128, 1024])
    qb = AF32.alloc("q11b", [128, 1024])
    qm = AF32.alloc("q11m", [128, 1024])
    ydq = [AF32.alloc(f"y11d{i}", [128, 1024]) for i in range(2)]
    rsb = AF32.alloc("r11sb", [128, 1024])
    rsrow = AF32.alloc("r11sr", [128, 1024])
    cq = AF32.alloc("c11q", [128, 16])
    fil = AF32.alloc("f11", [128, 16])
    AB16.mark()
    qr9 = AB16.alloc("q11r", [128, 8192])
    qlT = [AB16.alloc(f"q11l{i}", [128, 1024]) for i in range(2)]

    p11t = ts.n
    p11a = asm.n
    p11v = vs.n
    B11 = {n: Q(n).n for n in ("rh110", "rh111", "xl110", "xl111",
                               "yp110", "yp111", "rsc11", "rsl11")}


    def T11rt(i, ic):
        return (ts, p11t + 8 * i + ic + 1)
    T11rsb = (ts, p11t + 64 + 1)

    def T11lt(t, dc):
        return (ts, p11t + 65 + 10 * t + dc + 1)

    def T11mm(t, ob):
        return (ts, p11t + 65 + 10 * t + 8 + ob + 1)

    def A11rq(i):
        return (asm, p11a + 9 * i + 1)

    def A11re(i, ic):
        return (asm, p11a + 9 * i + 1 + ic + 1)

    def A11lq(t):
        return (asm, p11a + 72 + 11 * t + 1)

    def A11le(t, dc):
        return (asm, p11a + 72 + 11 * t + 1 + dc + 1)

    def A11dq(t, ob):
        return (asm, p11a + 72 + 11 * t + 9 + ob + 1)

    def V11rs(i):
        return (vs, p11v + 2 * i + 1)

    def V11rq(i):
        return (vs, p11v + 2 * i + 2)
    V11rsb = (vs, p11v + 16 + 1)
    V11rsr = (vs, p11v + 16 + 2)

    def V11ls(t):
        return (vs, p11v + 18 + 3 * t + 1)

    def V11lq(t):
        return (vs, p11v + 18 + 3 * t + 2)

    def V11yf(t):
        return (vs, p11v + 18 + 3 * t + 3)

    def q_rh(i):
        return (Q(f"rh11{i % 2}"), B11[f"rh11{i % 2}"] + 16 * (i // 2 + 1))

    def q_xl(t):
        return (Q(f"xl11{t % 2}"), B11[f"xl11{t % 2}"] + 16 * (t // 2 + 1))

    def q_yp(t):
        return (Q(f"yp11{t % 2}"), B11[f"yp11{t % 2}"] + 16 * (t // 2 + 1))
    q_rsl = (Q("rsl11"), B11["rsl11"] + 16)

    def p11_sync(e):
        wait(e, *st["p9_cpark"])
        for i in range(8):
            if i >= 2:
                wait(e, A11rq(i - 2), (Q("rsc11"), B11["rsc11"] + 16 * (i - 1)))
            Q(f"rh11{i % 2}").dma(e.dma_start(
                out=rht[i % 2][:, :], in_=rhT[128 * i:128 * (i + 1), :]))
            wait(e, V11rs(i))
            Q("rsc11").dma(e.dma_start(
                out=prs1[0:1, 128 * i:128 * (i + 1)].rearrange("r k -> k r"),
                in_=cq[:, 8 + (i % 2):8 + (i % 2) + 1]))
        wait(e, (Q("rsc11"), B11["rsc11"] + 16 * 8))
        Q("rsl11").dma(e.dma_start(out=rsrow[0:1, :], in_=prs1[0:1, :]))
        for t in range(NT):
            if t >= 2:
                wait(e, A11lq(t - 2))
            Q(f"xl11{t % 2}").dma(e.dma_start(
                out=xnt[t % 2][:, :],
                in_=pxn1[1 + 128 * t:1 + 128 * (t + 1), :]))
            if t >= 1:
                wait(e, V11yf(t - 1))
                Q(f"yp11{(t - 1) % 2}").dma(e.dma_start(
                    out=py1f[128 * (t - 1):128 * t, :],
                    in_=ydq[(t - 1) % 2][:, :]))
        wait(e, V11yf(NT - 1))
        Q(f"yp11{(NT - 1) % 2}").dma(e.dma_start(
            out=py1f[128 * (NT - 1):128 * NT, :], in_=ydq[(NT - 1) % 2][:, :]))
        st["p11_park"] = [Q("yp110").tok(), Q("yp111").tok()]

    def p11_ladder(e):
        e.tensor_scalar(out=qb[:, :], in0=qv[:, :], scalar1=127.5,
                        scalar2=None, op0=ALU.add)
        e.tensor_copy(out=qm[:, :], in_=qb[:, :])
        for k in range(7, -1, -1):
            e.tensor_scalar(out=qa[:, :], in0=qm[:, :],
                            scalar1=float(2 ** k), scalar2=float(2 ** k),
                            op0=ALU.is_ge, op1=ALU.mult)
            e.tensor_tensor(out=qm[:, :], in0=qm[:, :], in1=qa[:, :],
                            op=ALU.subtract)
        e.tensor_tensor(out=qa[:, :], in0=qb[:, :], in1=qm[:, :],
                        op=ALU.subtract)
        e.tensor_scalar(out=qa[:, :], in0=qa[:, :], scalar1=-127.0,
                        scalar2=None, op0=ALU.add)
        e.tensor_scalar(out=qa[:, :], in0=qa[:, :], scalar1=-127.0,
                        scalar2=None, op0=ALU.max)
        return vs.inc(e.tensor_scalar(out=qa[:, :], in0=qa[:, :],
                                      scalar1=127.0, scalar2=None,
                                      op0=ALU.min))

    def p11_vec(e):
        for i in range(8):
            wait(e, q_rh(i))
            if i >= 2:
                wait(e, (Q("rsc11"), B11["rsc11"] + 16 * (i - 1)))
            sc = 8 + (i % 2)
            e.tensor_reduce(out=cq[:, 0:1], in_=rht[i % 2][:, :], axis=AX,
                            op=ALU.max, apply_absolute_value=True)
            for _ in range(4):
                e.tensor_copy(out=fil[:, 0:1], in_=epscol[:, 0:1])
            e.tensor_scalar(out=cq[:, 1:2], in0=cq[:, 0:1], scalar1=1e-12,
                            scalar2=None, op0=ALU.max)
            for _ in range(4):
                e.tensor_copy(out=fil[:, 0:1], in_=epscol[:, 0:1])
            e.reciprocal(out=cq[:, 2:3], in_=cq[:, 1:2])
            e.tensor_scalar(out=cq[:, sc:sc + 1], in0=cq[:, 1:2],
                            scalar1=1.0 / 127.0, scalar2=None, op0=ALU.mult)
            for _ in range(3):
                e.tensor_copy(out=fil[:, 0:1], in_=epscol[:, 0:1])
            vs.inc(e.tensor_scalar(out=cq[:, 3:4], in0=cq[:, 2:3],
                                   scalar1=127.0, scalar2=None, op0=ALU.mult))
            wait(e, A11rq(i))
            if i >= 1:
                wait(e, T11rt(i - 1, 7))
            p11_ladder(e)
        wait(e, T11rsb)
        e.tensor_copy(out=rsb[:, 0:512], in_=pbank[2][:, 0:512])
        vs.inc(e.tensor_copy(out=rsb[:, 512:1024], in_=pbank[3][:, 0:512]))
        wait(e, q_rsl)
        vs.inc(e.tensor_copy(out=fil[:, 1:2], in_=epscol[:, 0:1]))
        for t in range(NT):
            wait(e, q_xl(t))
            if t < 2:
                wait(e, (Q("rsc11"), B11["rsc11"] + 128))
            else:
                wait(e, A11dq(t - 2, 1))
            sc = 8 + (t % 2)
            e.tensor_reduce(out=cq[:, 0:1], in_=xnt[t % 2][:, :], axis=AX,
                            op=ALU.max, apply_absolute_value=True)
            for _ in range(4):
                e.tensor_copy(out=fil[:, 0:1], in_=epscol[:, 0:1])
            e.tensor_scalar(out=cq[:, 1:2], in0=cq[:, 0:1], scalar1=1e-12,
                            scalar2=None, op0=ALU.max)
            for _ in range(4):
                e.tensor_copy(out=fil[:, 0:1], in_=epscol[:, 0:1])
            e.reciprocal(out=cq[:, 2:3], in_=cq[:, 1:2])
            e.tensor_scalar(out=cq[:, sc:sc + 1], in0=cq[:, 1:2],
                            scalar1=1.0 / 127.0, scalar2=None, op0=ALU.mult)
            for _ in range(3):
                e.tensor_copy(out=fil[:, 0:1], in_=epscol[:, 0:1])
            vs.inc(e.tensor_scalar(out=cq[:, 3:4], in0=cq[:, 2:3],
                                   scalar1=127.0, scalar2=None, op0=ALU.mult))
            wait(e, A11lq(t))
            wait(e, T11rt(7, 7) if t == 0 else T11lt(t - 1, 7))
            p11_ladder(e)
            wait(e, A11dq(t, 1))
            if t >= 2:
                wait(e, q_yp(t - 2))
            vs.inc(e.tensor_tensor(out=ydq[t % 2][:, :], in0=ydq[t % 2][:, :],
                                   in1=rsb[:, :], op=ALU.mult))

    def p11_act(e):
        for i in range(8):
            wait(e, V11rs(i))
            if i >= 1:
                wait(e, V11rq(i - 1))
            asm.inc(e.activation(out=qv[:, :], in_=rht[i % 2][:, :],
                                 func=AF.Identity, scale=cq[0:128, 3:4]))
            for ic in range(8):
                wait(e, T11rt(i, ic))
                asm.inc(e.activation(
                    out=qr9[:, 1024 * ic + 128 * i:1024 * ic + 128 * (i + 1)],
                    in_=pbank[4 + (8 * i + ic) % 4][:, 0:128],
                    func=AF.Identity))
        for t in range(NT):
            wait(e, V11ls(t))
            wait(e, V11rq(7) if t == 0 else V11lq(t - 1))
            asm.inc(e.activation(out=qv[:, :], in_=xnt[t % 2][:, :],
                                 func=AF.Identity, scale=cq[0:128, 3:4]))
            for dc in range(8):
                wait(e, T11lt(t, dc))
                asm.inc(e.activation(
                    out=qlT[t % 2][:, 128 * dc:128 * (dc + 1)],
                    in_=pbank[4 + (8 * t + dc) % 4][:, 0:128],
                    func=AF.Identity))
            for ob in range(2):
                wait(e, T11mm(t, ob))
                if t >= 2:
                    wait(e, q_yp(t - 2))
                sc = 8 + (t % 2)
                asm.inc(e.activation(
                    out=ydq[t % 2][:, 512 * ob:512 * (ob + 1)],
                    in_=pbank[ob][:, :], func=AF.Identity,
                    scale=cq[0:128, sc:sc + 1]))

    def p11_tsec(teng):
        for i in range(8):
            wait(teng, bankfree[i])
            bankfree[i] = None
        for i in range(8):
            for ic in range(8):
                gdx = 8 * i + ic
                wait(teng, V11rq(i))
                if gdx >= 4:
                    pi, pic = divmod(gdx - 4, 8)
                    wait(teng, A11re(pi, pic))
                ts.inc(teng.transpose(
                    pbank[4 + gdx % 4][:, 0:128],
                    qa[:, 128 * ic:128 * (ic + 1)], ident_s[:, :]))
        wait(teng, q_rsl)
        teng.matmul(pbank[2][:, 0:512], ones1f_s[0:1, :], rsrow[0:1, 0:512],
                    start=True, stop=True)
        ts.inc(teng.matmul(pbank[3][:, 0:512], ones1f_s[0:1, :],
                           rsrow[0:1, 512:1024], start=True, stop=True))
        for t in range(NT):
            for dc in range(8):
                gdx = 8 * t + dc
                wait(teng, V11lq(t))
                if gdx >= 4:
                    pt, pdc = divmod(gdx - 4, 8)
                    wait(teng, A11le(pt, pdc))
                else:
                    wait(teng, A11re(7, 4 + dc))
                ts.inc(teng.transpose(
                    pbank[4 + gdx % 4][:, 0:128],
                    qa[:, 128 * dc:128 * (dc + 1)], ident_s[:, :]))
            for ob in range(2):
                wait(teng, A11le(t, 7))
                if t >= 2:
                    wait(teng, A11dq(t - 2, ob))
                if t == 0 and ob == 0:
                    wait(teng, V11rsb)
                last = None
                for ic in range(8):
                    last = teng.matmul(
                        pbank[ob][:, :],
                        qlT[t % 2][:, 128 * ic:128 * (ic + 1)],
                        qr9[:, 1024 * ic + 512 * ob:1024 * ic + 512 * (ob + 1)],
                        start=(ic == 0), stop=(ic == 7))
                ts.inc(last)

    block.sync(p11_sync)
    block.vector(p11_vec)
    block.scalar(p11_act)
    block.tensor(p11_tsec)
    for i in range(8):
        bankfree[i] = (asm, p11a + 72 + 11 * NT)
    assert ts.n == p11t + 65 + 10 * NT
    assert asm.n == p11a + 72 + 11 * NT
    assert vs.n == p11v + 18 + 3 * NT
    AB16.pop()
    AF32.pop()


def _p10(block, X):
    g = X
    NT = g["NT"]; st = g["st"]; Q = g["Q"]
    ts = g["ts"]; vs = g["vs"]; asm = g["asm"]; gs = g["gs"]; cc = g["cc"]
    pbank = g["pbank"]; bankfree = g["bankfree"]
    AF32 = g["AF32"]; AB16 = g["AB16"]
    ident_s = g["ident_s"]; ones1f_s = g["ones1f_s"]; onescol_s = g["onescol_s"]
    triu_s = g["triu_s"]; bc127_s = g["bc127_s"]
    prevm_s = g["prevm_s"]; e0m_s = g["e0m_s"]
    pk1 = g["pk1"]; pkv1 = g["pkv1"]; pq1 = g["pq1"]; poar = g["poar"]
    pxn1 = g["pxn1"]; pk21 = g["pk21"]; pk2e = g["pk2e"]; py1f = g["py1f"]
    px1 = g["px1"]; y = g["y"]; pprefs = g["pprefs"]
    pag2in = g["pag2in"]; pag2out = g["pag2out"]
    wpjT = g["wpjT"]; bpjr = g["bpjr"]
    C0232 = float(np.float32(0.02) / 32.0)

    p10t = ts.n
    p10a_ = asm.n
    p10v = vs.n
    p10g = gs.n
    B10 = {n: Q(n).n for n in (
        "k100", "k101", "v100", "v101", "q100", "q101", "oa100", "oa101",
        "os100", "os101", "vx100", "vx101", "kk100", "kk101", "ke100",
        "ke101", "kel100", "kel101", "q2100", "q2101", "oal100", "oal101",
        "x100", "x101", "yfl100", "yfl101", "yy100", "yy101", "pf10",
        "g210", "a2p", "wl10", "b10")}

    def BQ(pre, t):
        nm = f"{pre}{t % 2}"
        return (Q(nm), B10[nm] + 16 * (t // 2 + 1))

    AF32.mark()
    pf2r = AF32.alloc("pf2r", [128, 1024])

    def T10ch(t, c):
        return (ts, p10t + 4 * t + c + 1)

    def T10tt(t, b):
        return (ts, p10t + 4 * NT + 2 * t + b + 1)

    def T10mk(b):
        return (ts, p10t + 6 * NT + b + 1)

    def T10s2(t, b):
        return (ts, p10t + 6 * NT + 2 + 12 * t + b + 1)

    def T10tr(t, dc):
        return (ts, p10t + 6 * NT + 2 + 12 * t + 2 + dc + 1)

    def T10pm(t, ob):
        return (ts, p10t + 6 * NT + 2 + 12 * t + 10 + ob + 1)

    def A10h(t, dc):
        return (asm, p10a_ + 8 * t + dc + 1)

    def V10ev(t, c):
        return (vs, p10v + 5 * t + c + 1)

    def V10oa(t):
        return (vs, p10v + 5 * t + 5)

    def V10ke(t):
        return (vs, p10v + 5 * NT + t + 1)
    V10t2 = (vs, p10v + 6 * NT + 1)

    def V10pf(b):
        return (vs, p10v + 6 * NT + 1 + b + 1)

    def V10s2(t, b):
        return (vs, p10v + 6 * NT + 3 + 12 * t + b + 1)

    def V10os(t):
        return (vs, p10v + 6 * NT + 3 + 12 * t + 3)

    def V10lo(t, dc):
        return (vs, p10v + 6 * NT + 3 + 12 * t + 3 + dc + 1)

    def V10y(t):
        return (vs, p10v + 6 * NT + 3 + 12 * t + 12)

    def q_k10(t):
        return BQ("k10", t)

    def q_v10(t):
        return BQ("v10", t)

    def q_q10(t):
        return BQ("q10", t)

    def q_oa(t):
        return BQ("oa10", t)

    def q_os(t):
        return BQ("os10", t)

    def q_vx(t):
        return BQ("vx10", t)

    def q_kk(t):
        return BQ("kk10", t)

    def q_ke(t):
        return BQ("ke10", t)

    def q_kel(t):
        return BQ("kel10", t)

    def q_q2(t):
        return BQ("q210", t)

    def q_oal(t):
        return BQ("oal10", t)

    def q_x1(t):
        return BQ("x10", t)

    def q_yfl(t):
        return BQ("yfl10", t)

    def q_yy(t):
        return BQ("yy10", t)
    q_pf10 = (Q("pf10"), B10["pf10"] + 32)
    q_g2 = (Q("g210"), B10["g210"] + 16)
    q_a2p = (Q("a2p"), B10["a2p"] + 16)
    q_bp = (Q("b10"), B10["b10"] + 16)

    AF32.mark()
    kcl = [AF32.alloc(f"kcl{i}", [128, 1024]) for i in range(2)]
    kvl = [AF32.alloc(f"kvl{i}", [128, 1024]) for i in range(2)]
    ql10 = [AF32.alloc(f"ql10{i}", [128, 1024]) for i in range(2)]
    kce = [AF32.alloc(f"kce{i}", [128, 1024]) for i in range(2)]
    kvce = [AF32.alloc(f"kvce{i}", [128, 1024]) for i in range(2)]
    rkb = AF32.alloc("rkb", [128, 1024])
    Sb = AF32.alloc("Sb", [128, 1024])
    oarb = [AF32.alloc(f"oarb{i}", [128, 1024]) for i in range(2)]
    pf10 = AF32.alloc("pf10b", [128, 2048])

    def p10a_sync(e):
        wait(e, st["p9_prefs"], *st["p11_park"])
        Q("pf10").dma(e.dma_start(out=pf10[0:1, 0:1024], in_=pprefs[0:1, :]))
        Q("pf10").dma(e.dma_start(out=pf10[0:1, 1024:2048], in_=pprefs[1:2, :]))
        for t in range(NT):
            if t >= 2:
                wait(e, T10ch(t - 2, 1))
            wait(e, *st["p9_kpark"]) if t == 0 else None
            Q(f"k10{t % 2}").dma(e.dma_start(
                out=kcl[t % 2][:, :], in_=pk1[128 * t:128 * (t + 1), :]))
            if t == 0:
                wait(e, *st["p9_vpark"])
            if t >= 2:
                wait(e, T10ch(t - 2, 3))
            Q(f"v10{t % 2}").dma(e.dma_start(
                out=kvl[t % 2][:, :], in_=pkv1[128 * t:128 * (t + 1), :]))
            if t == 0:
                wait(e, *st["p9_qpark"], st["p9_bnd"])
            if t >= 2:
                wait(e, V10oa(t - 2))
            Q(f"q10{t % 2}").dma(e.dma_start(
                out=ql10[t % 2][:, :], in_=pq1[1 + 128 * t:1 + 128 * (t + 1), :]))
            if t >= 1:
                wait(e, V10oa(t - 1))
                Q(f"oa10{(t - 1) % 2}").dma(e.dma_start(
                    out=poar[1 + 128 * (t - 1):1 + 128 * t, :],
                    in_=oarb[(t - 1) % 2][:, :]))
        wait(e, V10oa(NT - 1))
        Q(f"oa10{(NT - 1) % 2}").dma(e.dma_start(
            out=poar[1 + 128 * (NT - 1):1 + 128 * NT, :],
            in_=oarb[(NT - 1) % 2][:, :]))

    def p10a_tsec(teng):
        for i in range(8):
            wait(teng, bankfree[i])
            bankfree[i] = None
        for t in range(NT):
            for c in range(4):
                qq, b = divmod(c, 2)
                src = kcl if qq == 0 else kvl
                wait(teng, q_k10(t) if qq == 0 else q_v10(t))
                if t == 0:
                    wait(teng, q_pf10)
                else:
                    wait(teng, V10ev(t - 1, c))
                teng.matmul(pbank[c][:, :], triu_s[:, :],
                            src[t % 2][:, 512 * b:512 * (b + 1)],
                            start=True, stop=False)
                if t == 0:
                    ts.inc(teng.matmul(
                        pbank[c][:, :], ones1f_s[0:1, :],
                        pf10[0:1, 1024 * qq + 512 * b:1024 * qq + 512 * (b + 1)],
                        start=False, stop=True))
                else:
                    ts.inc(teng.matmul(
                        pbank[c][:, :], bc127_s[:, :],
                        (kce if qq == 0 else kvce)[(t - 1) % 2][
                            :, 512 * b:512 * (b + 1)],
                        start=False, stop=True))

    def p10a_vec(e):
        for t in range(NT):
            for c in range(4):
                qq, b = divmod(c, 2)
                dst = (kce if qq == 0 else kvce)[t % 2]
                wait(e, T10ch(t, c))
                vs.inc(e.tensor_copy(out=dst[:, 512 * b:512 * (b + 1)],
                                     in_=pbank[c][:, :]))
            e.reciprocal(out=rkb[:, :], in_=kce[t % 2][:, :])
            e.tensor_tensor(out=Sb[:, :], in0=kvce[t % 2][:, :],
                            in1=rkb[:, :], op=ALU.mult)
            wait(e, q_q10(t))
            if t >= 2:
                wait(e, q_oa(t - 2))
            vs.inc(e.tensor_tensor(out=oarb[t % 2][:, :], in0=Sb[:, :],
                                   in1=ql10[t % 2][:, :], op=ALU.mult))

    block.sync(p10a_sync)
    block.tensor(p10a_tsec)
    block.vector(p10a_vec)
    AF32.pop()

    AF32.mark()
    oas = [AF32.alloc(f"oas{i}", [128, 1024]) for i in range(2)]
    vtl = [AF32.alloc(f"vtl{i}", [128, 1024]) for i in range(2)]
    k2l = [AF32.alloc(f"k2l{i}", [128, 1024]) for i in range(2)]
    k2eb = [AF32.alloc(f"k2eb{i}", [128, 1024]) for i in range(2)]
    ag2r = AF32.alloc("ag2r", [128, 1024])
    g2t = AF32.alloc("g2t", [128, 1024])

    def p10b_sync(e):
        wait(e, T10ch(NT - 1, 3), V10oa(NT - 1), q_oa(NT - 1),
             q_oa(NT - 2) if NT >= 2 else None)
        for t in range(NT):
            if t >= 2:
                wait(e, V10ke(t - 2))
            wait(e, q_oa(t))
            if t >= 1:
                wait(e, q_oa(t - 1))
            Q(f"os10{t % 2}").dma(e.dma_start(
                out=oas[t % 2][:, :], in_=poar[128 * t:128 * (t + 1), :]))
            Q(f"vx10{t % 2}").dma(e.dma_start(
                out=vtl[t % 2][:, :], in_=pxn1[1 + 128 * t:1 + 128 * (t + 1), :]))
            if t == 0:
                wait(e, *st["p9_cpark"])
            Q(f"kk10{t % 2}").dma(e.dma_start(
                out=k2l[t % 2][:, :], in_=pk21[128 * t:128 * (t + 1), :]))
            if t >= 1:
                wait(e, V10ke(t - 1), T10tt(t - 1, 1))
                Q(f"ke10{(t - 1) % 2}").dma(e.dma_start(
                    out=pk2e[128 * (t - 1):128 * t, :],
                    in_=k2eb[(t - 1) % 2][:, :]))
        wait(e, V10ke(NT - 1), T10tt(NT - 1, 1))
        Q(f"ke10{(NT - 1) % 2}").dma(e.dma_start(
            out=pk2e[128 * (NT - 1):128 * NT, :],
            in_=k2eb[(NT - 1) % 2][:, :]))
        wait(e, V10t2)
        Q("a2p").dma(e.dma_start(out=pag2in[0:1, :], in_=ag2r[0:1, :]))
        wait(e, cc.at(3))
        Q("g210").dma(e.dma_start(out=g2t[0:8, :], in_=pag2out[0:8, :]))

    def p10b_tsec(teng):
        for t in range(NT):
            for b in range(2):
                wait(teng, V10ke(t))
                ts.inc(teng.matmul(
                    pbank[6 + b][0:1, :], onescol_s[:, 0:1],
                    k2eb[t % 2][:, 512 * b:512 * (b + 1)],
                    start=(t == 0), stop=(t == NT - 1)))
        for b in range(2):
            wait(teng, q_g2, V10t2)
            if b >= 1:
                wait(teng, V10pf(b - 1))
            ts.inc(teng.matmul(
                pbank[6][0:1, 0:512], prevm_s[0:8, 0:1],
                g2t[0:8, 512 * b:512 * (b + 1)], start=True, stop=True))

    def p10b_vec(e):
        for t in range(NT):
            wait(e, q_os(t), q_vx(t), q_kk(t))
            e.tensor_tensor(out=k2eb[t % 2][:, :], in0=vtl[t % 2][:, :],
                            in1=oas[t % 2][:, :], op=ALU.subtract)
            if t == 0:
                e.tensor_scalar(out=k2eb[0][:, :], in0=k2eb[0][:, :],
                                scalar1=e0m_s[:, 0:1], scalar2=None,
                                op0=ALU.mult)
            if t >= 2:
                wait(e, q_ke(t - 2))
            vs.inc(e.tensor_tensor(out=k2eb[t % 2][:, :], in0=k2eb[t % 2][:, :],
                                   in1=k2l[t % 2][:, :], op=ALU.mult))
        wait(e, T10tt(NT - 1, 1))
        e.tensor_copy(out=ag2r[0:1, 0:512], in_=pbank[6][0:1, 0:512])
        vs.inc(e.tensor_copy(out=ag2r[0:1, 512:1024], in_=pbank[7][0:1, 0:512]))
        for b in range(2):
            wait(e, T10mk(b))
            vs.inc(e.tensor_copy(out=pf2r[0:1, 512 * b:512 * (b + 1)],
                                 in_=pbank[6][0:1, 0:512]))

    def p10b_gps(e):
        wait(e, q_a2p)
        cc.inc(e.collective_compute(
            "AllGather", ALU.bypass,
            replica_groups=[list(range(W))],
            ins=[pag2in[:, :]], outs=[pag2out[:, :]]))

    block.sync(p10b_sync)
    block.tensor(p10b_tsec)
    block.vector(p10b_vec)
    block.gpsimd(p10b_gps)
    AF32.pop()

    AF32.mark()
    wtmp10 = AF32.alloc("wtmp10", [128, 4096])
    bprow = AF32.alloc("bprow", [128, 1024])
    x1l = [AF32.alloc(f"x1l{i}", [128, 1024]) for i in range(2)]
    yfl = [AF32.alloc(f"yfl{i}", [128, 1024]) for i in range(2)]
    k2el = [AF32.alloc(f"k2el{i}", [128, 1024]) for i in range(2)]
    q2l = [AF32.alloc(f"q2l{i}", [128, 1024]) for i in range(2)]
    oarl = [AF32.alloc(f"oarl{i}", [128, 1024]) for i in range(2)]
    s2ce = [AF32.alloc(f"s2ce{i}", [128, 1024]) for i in range(2)]
    omb = AF32.alloc("omb", [128, 1024])
    osb = AF32.alloc("osb", [128, 1024])
    ybuf = [AF32.alloc(f"ybuf{i}", [128, 1024]) for i in range(2)]
    AB16.mark()
    wph = AB16.alloc("wph", [128, 8192])
    wpl = AB16.alloc("wpl", [128, 8192])
    osTh = [AB16.alloc(f"osTh{i}", [128, 1024]) for i in range(2)]
    osTl = [AB16.alloc(f"osTl{i}", [128, 1024]) for i in range(2)]

    def G10sp(ob):
        return (gs, p10g + ob + 1)

    def p10c_sync(e):
        wait(e, V10pf(1), q_ke(NT - 1), q_ke(NT - 2) if NT >= 2 else None)
        for ob in range(2):
            if ob == 1:
                wait(e, G10sp(0))
            Q("wl10").dma(e.dma_start(
                out=wtmp10[:, :],
                in_=wpjT[:, 512 * ob:512 * (ob + 1)].rearrange(
                    "(kc p) n -> p kc n", p=128)))
        Q("b10").dma(e.dma_start(out=bprow[0:1, :], in_=bpjr[0:1, :]))
        for t in range(NT):
            if t >= 2:
                wait(e, T10s2(t - 2, 1))
            Q(f"kel10{t % 2}").dma(e.dma_start(
                out=k2el[t % 2][:, :], in_=pk2e[128 * t:128 * (t + 1), :]))
            if t >= 2:
                wait(e, V10os(t - 2))
            Q(f"q210{t % 2}").dma(e.dma_start(
                out=q2l[t % 2][:, :], in_=pq1[128 * t:128 * (t + 1), :]))
            Q(f"oal10{t % 2}").dma(e.dma_start(
                out=oarl[t % 2][:, :], in_=poar[1 + 128 * t:1 + 128 * (t + 1), :]))
            if t >= 2:
                wait(e, V10y(t - 2))
            Q(f"x10{t % 2}").dma(e.dma_start(
                out=x1l[t % 2][:, :], in_=px1[128 * t:128 * (t + 1), :]))
            Q(f"yfl10{t % 2}").dma(e.dma_start(
                out=yfl[t % 2][:, :], in_=py1f[128 * t:128 * (t + 1), :]))
            if t >= 1:
                wait(e, V10y(t - 1))
                Q(f"yy10{(t - 1) % 2}").dma(e.dma_start(
                    out=y[128 * (t - 1):128 * t, :],
                    in_=ybuf[(t - 1) % 2][:, :]))
        wait(e, V10y(NT - 1))
        Q(f"yy10{(NT - 1) % 2}").dma(e.dma_start(
            out=y[128 * (NT - 1):128 * NT, :], in_=ybuf[(NT - 1) % 2][:, :]))
        st["p10_y"] = [Q("yy100").tok(), Q("yy101").tok()]

    def p10c_gps(e):
        for ob in range(2):
            wait(e, (Q("wl10"), B10["wl10"] + 16 * (ob + 1)))
            if ob == 0:
                wait(e, V10pf(1))
            e.tensor_copy(out=wph[:, 4096 * ob:4096 * (ob + 1)],
                          in_=wtmp10[:, :])
            e.tensor_tensor(out=wtmp10[:, :], in0=wtmp10[:, :],
                            in1=wph[:, 4096 * ob:4096 * (ob + 1)],
                            op=ALU.subtract)
            gs.inc(e.tensor_copy(out=wpl[:, 4096 * ob:4096 * (ob + 1)],
                                 in_=wtmp10[:, :]))

    def p10c_tsec(teng):
        for t in range(NT):
            for b in range(2):
                wait(teng, q_kel(t))
                if t == 0:
                    wait(teng, V10pf(1))
                else:
                    wait(teng, V10s2(t - 1, b))
                teng.matmul(pbank[b][:, :], triu_s[:, :],
                            k2el[t % 2][:, 512 * b:512 * (b + 1)],
                            start=True, stop=False)
                if t == 0:
                    ts.inc(teng.matmul(
                        pbank[b][:, :], ones1f_s[0:1, :],
                        pf2r[0:1, 512 * b:512 * (b + 1)],
                        start=False, stop=True))
                else:
                    ts.inc(teng.matmul(
                        pbank[b][:, :], bc127_s[:, :],
                        s2ce[(t - 1) % 2][:, 512 * b:512 * (b + 1)],
                        start=False, stop=True))
            for dc in range(8):
                gdx = 8 * t + dc
                wait(teng, V10os(t))
                if gdx >= 4:
                    pt, pdc = divmod(gdx - 4, 8)
                    wait(teng, V10lo(pt, pdc))
                ts.inc(teng.transpose(
                    pbank[4 + gdx % 4][:, 0:128],
                    osb[:, 128 * dc:128 * (dc + 1)], ident_s[:, :]))
            for ob in range(2):
                wait(teng, A10h(t, 7), V10lo(t, 7), G10sp(ob), q_bp)
                if t >= 2:
                    wait(teng, V10y(t - 2))
                bk = pbank[2 + ob]
                for kc in range(8):
                    lh = osTh[t % 2][:, 128 * kc:128 * (kc + 1)]
                    ll = osTl[t % 2][:, 128 * kc:128 * (kc + 1)]
                    rh = wph[:, 4096 * ob + 512 * kc:4096 * ob + 512 * (kc + 1)]
                    rl = wpl[:, 4096 * ob + 512 * kc:4096 * ob + 512 * (kc + 1)]
                    teng.matmul(bk[:, :], lh, rh, start=(kc == 0), stop=False)
                    teng.matmul(bk[:, :], lh, rl, start=False, stop=False)
                    teng.matmul(bk[:, :], ll, rh, start=False, stop=False)
                ts.inc(teng.matmul(bk[:, :], ones1f_s[0:1, :],
                                   bprow[0:1, 512 * ob:512 * (ob + 1)],
                                   start=False, stop=True))

    def p10c_act(e):
        for t in range(NT):
            for dc in range(8):
                wait(e, (ts, T10tr(t, dc)[1]))
                if t >= 2:
                    wait(e, T10pm(t - 2, 1))
                asm.inc(e.activation(
                    out=osTh[t % 2][:, 128 * dc:128 * (dc + 1)],
                    in_=pbank[4 + (8 * t + dc) % 4][:, 0:128],
                    func=AF.Identity))

    def p10c_vec(e):
        for t in range(NT):
            for b in range(2):
                wait(e, T10s2(t, b))
                vs.inc(e.tensor_copy(out=s2ce[t % 2][:, 512 * b:512 * (b + 1)],
                                     in_=pbank[b][:, :]))
            wait(e, q_q2(t), q_oal(t))
            e.scalar_tensor_tensor(out=omb[:, :], in0=s2ce[t % 2][:, :],
                                   scalar=C0232, in1=q2l[t % 2][:, :],
                                   op0=ALU.mult, op1=ALU.mult)
            if t >= 1:
                wait(e, (ts, T10tr(t - 1, 7)[1]))
            vs.inc(e.tensor_tensor(out=osb[:, :], in0=omb[:, :],
                                   in1=oarl[t % 2][:, :], op=ALU.add))
            for dc in range(8):
                wait(e, A10h(t, dc))
                vs.inc(e.tensor_tensor(
                    out=osTl[t % 2][:, 128 * dc:128 * (dc + 1)],
                    in0=pbank[4 + (8 * t + dc) % 4][:, 0:128],
                    in1=osTh[t % 2][:, 128 * dc:128 * (dc + 1)],
                    op=ALU.subtract))
            wait(e, T10pm(t, 0), q_x1(t), q_yfl(t))
            if t >= 2:
                wait(e, q_yy(t - 2))
            yb = ybuf[t % 2]
            e.tensor_tensor(out=yb[:, 0:512], in0=pbank[2][:, :],
                            in1=x1l[t % 2][:, 0:512], op=ALU.add)
            wait(e, T10pm(t, 1))
            e.tensor_tensor(out=yb[:, 512:1024], in0=pbank[3][:, :],
                            in1=x1l[t % 2][:, 512:1024], op=ALU.add)
            vs.inc(e.tensor_tensor(out=yb[:, :], in0=yb[:, :],
                                   in1=yfl[t % 2][:, :], op=ALU.add))

    block.sync(p10c_sync)
    block.gpsimd(p10c_gps)
    block.tensor(p10c_tsec)
    block.scalar(p10c_act)
    block.vector(p10c_vec)
    for i in range(8):
        bankfree[i] = (vs, vs.n)
    assert ts.n == p10t + 18 * NT + 2
    assert asm.n == p10a_ + 8 * NT
    assert vs.n == p10v + 18 * NT + 3
    assert gs.n == p10g + 2
    AB16.pop()
    AF32.pop()
    AF32.pop()

    def p10_fin(e):
        wait(e, *st["p10_y"])

    block.sync(p10_fin)


def kernel(**inputs):
    full = {k: np.asarray(v) for k, v in inputs.items()}
    n_tok = full["L0_attn_x"].shape[1]
    out = run(n_tok, full)
    return out.astype(np.float32)


# version 04
# speedup vs baseline: 1.0116x; 1.0116x over previous
import math
import numpy as np
import ml_dtypes
import concourse.bass as bass
import concourse.mybir as mybir
from concourse.bass_utils import run_bass_kernel_spmd

F32 = mybir.dt.float32
BF16 = mybir.dt.bfloat16
AF = mybir.ActivationFunctionType
ALU = mybir.AluOpType
AX = mybir.AxisListType.X
BF = ml_dtypes.bfloat16

W = 8
D = 1024
DH = 4096
NH = 4


class Sem:
    def __init__(self, nc, name):
        self.h = nc.alloc_semaphore(name)
        self.n = 0

    def inc(self, ins, k=1):
        ins.then_inc(self.h, k)
        self.n += k
        return (self, self.n)

    def dma(self, ins):
        return self.inc(ins, 16)

    def tok(self):
        return (self, self.n)

    def at(self, v):
        return (self, v)


def wait(eng, *toks):
    best = {}
    for t in toks:
        if t is None:
            continue
        s, v = t
        cur = best.get(id(s))
        if cur is None or v > cur[1]:
            best[id(s)] = (s, v)
    for s, v in best.values():
        if v > 0:
            eng.wait_ge(s.h, v)


class Arena:
    def __init__(self, nc, name, dtype, cols):
        self.t = nc.alloc_sbuf_tensor(name, [128, cols], dtype)
        self.cols = cols
        self.cur = 0
        self.marks = []

    def alloc(self, name, shape):
        assert shape[0] == 128
        n = int(np.prod(shape[1:]))
        off = (self.cur + 15) // 16 * 16
        self.cur = off + n
        assert self.cur <= self.cols, (
            f"arena overflow at {name}: {self.cur} > {self.cols}")
        return self.t[:, off:off + n]

    def mark(self):
        self.marks.append(self.cur)

    def pop(self):
        self.cur = self.marks.pop()


def build(n_tok, dbg=False):
    TC = n_tok // W
    NT = TC // 128
    NTG = TC // 512
    TC4 = TC // 4
    assert TC % 512 == 0

    nc = bass.Bass("TRN2", target_bir_lowering=False, debug=False,
                   detect_race_conditions=False)

    def din(name, shape, dt=F32):
        return nc.dram_tensor(name, shape, dt, kind="ExternalInput").ap()

    x_rows = din("x_rows", [TC, D])
    phase = din("phase", [NH, D])
    Wq = din("Wq", [D, DH]); Wk = din("Wk", [D, DH]); Wv = din("Wv", [D, DH])
    bq = din("bq", [1, DH]); bk = din("bk", [1, DH]); bv = din("bv", [1, DH])
    Wo = din("Wo", [DH, D]); bo = din("bo", [1, D])
    w1T = din("w1T", [D, DH]); w2T = din("w2T", [D, DH])
    h1T = din("h1T", [DH, DH]); h2T = din("h2T", [DH, DH])
    w3T = din("w3T", [DH, D])
    s1c = din("s1c", [128, 32]); s2c = din("s2c", [128, 32])
    sh1c = din("sh1c", [128, 32]); sh2c = din("sh2c", [128, 32])
    s3c = din("s3c", [128, 8])
    identf = din("identf", [128, 128])
    selmask = din("selmask", [1, 8])
    wq1T = din("wq1T", [D, D]); wk1T = din("wk1T", [D, D])
    wk2T = din("wk2T", [D, D]); wpjT = din("wpjT", [D, D])
    bq1r = din("bq1r", [1, D]); bk1r = din("bk1r", [1, D])
    bk2r = din("bk2r", [1, D]); bpjr = din("bpjr", [1, D])
    rhT = din("rhT", [D, D])
    prevm8 = din("prevm8", [8, 1]); prevo8 = din("prevo8", [8, 1])
    e0m = din("e0m", [128, 1])
    triu128 = din("triu128", [128, 128]); bc127 = din("bc127", [128, 128])

    y = nc.dram_tensor("y", [TC, D], F32, kind="ExternalOutput").ap()

    def dint(name, shape, dt=F32, dbgok=True):
        kind = "ExternalOutput" if (dbg and dbgok) else "Internal"
        return nc.dram_tensor(name, shape, dt, kind=kind).ap()

    pqr = dint("pqr", [TC, DH]); pqi = dint("pqi", [TC, DH])
    pkr = dint("pkr", [TC, DH]); pki = dint("pki", [TC, DH])
    pvv = dint("pvv", [TC, DH])
    poutTh = dint("poutTh", [NT, 128, 32 * 128], BF16)
    poutTl = dint("poutTl", [NT, 128, 32 * 128], BF16)
    pxnTh = dint("pxnTh", [8, 128, NT * 128], BF16)
    pxnTl = dint("pxnTl", [8, 128, NT * 128], BF16)
    pagin = dint("pagin", [NH * TC4, D], dbgok=False)
    pagout = dint("pagout", [W * NH * TC4, D], dbgok=False)
    phah = dint("phah", [32, 128, 512], BF16)
    phal = dint("phal", [32, 128, 512], BF16)
    phbh = dint("phbh", [32, 128, 512], BF16)
    phbl = dint("phbl", [32, 128, 512], BF16)
    phch = dint("phch", [32, 128, 512], BF16)
    phcl = dint("phcl", [32, 128, 512], BF16)
    py0ffn = dint("py0ffn", [TC, D])
    px1 = dint("px1", [TC, D])
    pdbg = dint("pdbg", [TC // 128, 128, 8], dbgok=True) if dbg else None
    pxn1Th = dint("pxn1Th", [8, 128, (TC // 128) * 128], BF16)
    pxn1Tl = dint("pxn1Tl", [8, 128, (TC // 128) * 128], BF16)
    pxn1 = dint("pxn1", [TC + 1, D])
    pq1 = dint("pq1", [TC + 1, D])
    poar = dint("poar", [TC + 1, D])
    pk1 = dint("pk1", [TC, D])
    pkv1 = dint("pkv1", [TC, D])
    pk21 = dint("pk21", [TC, D])
    pk2e = dint("pk2e", [TC, D])
    py1f = dint("py1f", [TC, D])
    pag1in = dint("pag1in", [4, D], dbgok=False)
    pag1out = dint("pag1out", [4 * W, D], dbgok=False)
    pag2in = dint("pag2in", [1, D], dbgok=False)
    pag2out = dint("pag2out", [W, D], dbgok=False)
    prow9 = dint("prow9", [2, D], BF16)
    pprefs = dint("pprefs", [2, D])
    prs1 = dint("prs1", [1, D])

    SEMS = {}

    ALIAS = {
        "z0": "x0", "z1": "x1", "p8r": "pr", "p8pk": "p1pk",
        "wl9": "wt0", "b9": "br2", "x90": "wo0", "x91": "wo1",
        "kr90": "l70", "kr91": "l71",
        "p9k0": "pk0", "p9k1": "pk1", "p9v0": "pk2", "p9v1": "pk3",
        "p9q0": "pkv0", "p9q1": "pkv1", "p9c0": "a3", "p9c1": "b3",
        "a1p": "c3", "mg9": "d3", "pr9s": "v3", "bd9": "o0",
        "rh110": "rh110", "rh111": "rh111", "xl110": "xl110", "xl111": "xl111",
        "yp110": "hal0", "yp111": "hal1", "rsc11": "hbl0", "rsl11": "hbl1",
        "k100": "pk0", "k101": "pk1", "v100": "pk2", "v101": "pk3",
        "q100": "pkv0", "q101": "pkv1", "oa100": "wo0", "oa101": "wo1",
        "os100": "l70", "os101": "l71", "vx100": "a3", "vx101": "b3",
        "kk100": "c3", "kk101": "d3", "ke100": "pa0", "ke101": "pa1",
        "kel100": "pb60", "kel101": "pb61", "q2100": "pc60", "q2101": "pc61",
        "oal100": "xl110", "oal101": "xl111", "x100": "x100", "x101": "x101",
        "yfl100": "rh110", "yfl101": "rh111", "yy100": "y70", "yy101": "y71",
        "pf10": "y06", "g210": "hcl0", "a2p": "hcl1", "wl10": "sl4",
        "b10": "xs6",
    }

    def Q(name):
        name = ALIAS.get(name, name)
        if name not in SEMS:
            SEMS[name] = Sem(nc, name)
        return SEMS[name]

    ts = Sem(nc, "ts")
    vs = Sem(nc, "vs")
    asm = Sem(nc, "asm")
    gs = Sem(nc, "gs")
    cc = Sem(nc, "cc")

    pbank = [nc.alloc_psum_tensor(f"pb{i}", [128, 512], F32) for i in range(8)]
    bankfree = [None] * 8

    sbA = nc.alloc_sbuf_tensor
    ident_s = sbA("ident_s", [128, 128], F32)
    ones1f_s = sbA("ones1f_s", [1, 128], F32)
    pihalf = sbA("pihalf", [1, 1], F32)
    epscol = sbA("epscol", [128, 1], F32)
    rowsh = sbA("rowsh", [1, 2 * D], F32)
    triu_s = sbA("triu_s", [128, 128], F32)
    bc127_s = sbA("bc127_s", [128, 128], F32)
    onescol_s = sbA("onescol_s", [128, 1], F32)
    e0m_s = sbA("e0m_s", [128, 1], F32)
    prevm_s = sbA("prevm_s", [8, 1], F32)
    prevo_s = sbA("prevo_s", [8, 1], F32)
    s1c_s = sbA("s1c_s", [128, 32], F32)
    s2c_s = sbA("s2c_s", [128, 32], F32)
    sh1c_s = sbA("sh1c_s", [128, 32], F32)
    sh2c_s = sbA("sh2c_s", [128, 32], F32)
    s3c_s = sbA("s3c_s", [128, 8], F32)
    bob = sbA("bob", [128, D], F32)
    selrow = sbA("selrow", [1, 8], F32)
    self8 = sbA("self8", [128, 8], F32)

    AF32 = Arena(nc, "AF32", F32, 24320)
    AB16 = Arena(nc, "AB16", BF16, 49152)
    st = {}

    with nc.Block() as block:
        def p0_sync(e):
            c0 = Q("c0")
            c0.dma(e.dma_start(out=ident_s[:, :], in_=identf[:, :]))
            c0.dma(e.dma_start(out=s1c_s[:, :], in_=s1c[:, :]))
            c0.dma(e.dma_start(out=s2c_s[:, :], in_=s2c[:, :]))
            c0.dma(e.dma_start(out=sh1c_s[:, :], in_=sh1c[:, :]))
            c0.dma(e.dma_start(out=sh2c_s[:, :], in_=sh2c[:, :]))
            c0.dma(e.dma_start(out=s3c_s[:, :], in_=s3c[:, :]))
            c0.dma(e.dma_start(out=triu_s[:, :], in_=triu128[:, :]))
            c0.dma(e.dma_start(out=bc127_s[:, :], in_=bc127[:, :]))
            c0.dma(e.dma_start(out=e0m_s[:, :], in_=e0m[:, :]))
            c0.dma(e.dma_start(out=prevm_s[0:8, :], in_=prevm8[0:8, :]))
            c0.dma(e.dma_start(out=prevo_s[0:8, :], in_=prevo8[0:8, :]))
            st["c_all"] = c0.dma(e.dma_start(out=selrow[0:1, :],
                                             in_=selmask[0:1, :]))

        def p0_gps(e):
            e.memset(pihalf[:, :], float(np.pi / 2))
            e.memset(ones1f_s[:, :], 1.0)
            e.memset(onescol_s[:, :], 1.0)
            st["c_mem"] = gs.inc(e.memset(epscol[:, :], 1e-5))

        block.sync(p0_sync)
        block.gpsimd(p0_gps)

        AF32.mark()
        cbt = AF32.alloc("cbt", [128, DH])
        sbt = AF32.alloc("sbt", [128, DH])

        p0b_a = asm.n
        p0b_t = ts.n
        p0b_v = vs.n
        p0b_ld = []

        def p0b_sync(e):
            for step in range(8):
                h = step % 4
                if step >= 1:
                    wait(e, (asm, p0b_a + step))
                p0b_ld.append(Q("pr").dma(e.dma_start(
                    out=rowsh[0:1, 0:D], in_=phase[h:h + 1, :])))
            for step in range(2):
                wait(e, (ts, p0b_t + 16 + step))
                p0b_ld.append(Q("pr").dma(e.dma_start(
                    out=rowsh[0:1, D + 512 * step:D + 512 * (step + 1)],
                    in_=bo[0:1, 512 * step:512 * (step + 1)])))

        def p0b_act(e):
            for step in range(8):
                wait(e, p0b_ld[step], st["c_mem"])
                if step >= 1:
                    wait(e, (ts, p0b_t + 2 * step))
                if step < 4:
                    asm.inc(e.activation(out=rowsh[0:1, D:2 * D],
                                         in_=rowsh[0:1, 0:D], func=AF.Sin,
                                         bias=pihalf[0:1, 0:1]))
                else:
                    asm.inc(e.activation(out=rowsh[0:1, D:2 * D],
                                         in_=rowsh[0:1, 0:D], func=AF.Sin))

        block.sync(p0b_sync)
        block.scalar(p0b_act)

        def p0b_tsec(teng):
            g = 0
            for step in range(8):
                wait(teng, (asm, p0b_a + step + 1), st["c_all"],
                     (gs, st["c_mem"][1]))
                for half in range(2):
                    if g >= 4:
                        wait(teng, (vs, p0b_v + g - 4 + 1))
                    ts.inc(teng.matmul(
                        pbank[g % 4][:, :], ones1f_s[0:1, :],
                        rowsh[0:1, D + 512 * half:D + 512 * (half + 1)],
                        start=True, stop=True))
                    g += 1
            for step in range(2):
                wait(teng, p0b_ld[8 + step])
                if g >= 4:
                    wait(teng, (vs, p0b_v + g - 4 + 1))
                ts.inc(teng.matmul(
                    pbank[g % 4][:, :], ones1f_s[0:1, :],
                    rowsh[0:1, D + 512 * step:D + 512 * (step + 1)],
                    start=True, stop=True))
                g += 1
            if g >= 4:
                wait(teng, (vs, p0b_v + g - 4 + 1))
            ts.inc(teng.matmul(pbank[g % 4][:, 0:8], ones1f_s[0:1, :],
                               selrow[0:1, :], start=True, stop=True))

        block.tensor(p0b_tsec)

        def p0b_vec(e):
            g = 0
            for step in range(8):
                dstt = cbt if step < 4 else sbt
                h = step % 4
                for half in range(2):
                    wait(e, (ts, p0b_t + g + 1))
                    vs.inc(e.tensor_copy(
                        out=dstt[:, 1024 * h + 512 * half:
                                 1024 * h + 512 * (half + 1)],
                        in_=pbank[g % 4][:, :]))
                    g += 1
            for step in range(2):
                wait(e, (ts, p0b_t + g + 1))
                vs.inc(e.tensor_copy(out=bob[:, 512 * step:512 * (step + 1)],
                                     in_=pbank[g % 4][:, :]))
                g += 1
            wait(e, (ts, p0b_t + g + 1))
            vs.inc(e.tensor_copy(out=self8[:, :], in_=pbank[g % 4][:, 0:8]))

        block.vector(p0b_vec)
        for i in range(4):
            bankfree[i] = (vs, p0b_v + 19)
        assert ts.n == p0b_t + 19 and vs.n == p0b_v + 19

        AB16.mark()
        xnTh = AB16.alloc("xnTh", [128, 8 * NT * 128])
        xnTl = AB16.alloc("xnTl", [128, 8 * NT * 128])
        AF32.mark()
        xin = [AF32.alloc(f"xin{i}", [128, D]) for i in range(2)]
        xnb = [AF32.alloc(f"xnb{i}", [128, D]) for i in range(2)]
        strash = AF32.alloc("strash", [128, D])
        c_sq = AF32.alloc("c_sq", [128, 16])
        c_sum = AF32.alloc("c_sum", [128, 16])
        c_mu = AF32.alloc("c_mu", [128, 16])
        c_mu2 = AF32.alloc("c_mu2", [128, 16])
        c_sq2 = AF32.alloc("c_sq2", [128, 16])
        c_var = AF32.alloc("c_var", [128, 16])
        c_std = AF32.alloc("c_std", [128, 16])
        c_rstd = AF32.alloc("c_rstd", [128, 16])
        c_nmr = AF32.alloc("c_nmr", [128, 16])
        c_ngm = AF32.alloc("c_ngm", [128, 16])
        fil1 = AF32.alloc("fil1", [128, 16])
        ttmp = [AF32.alloc(f"ttmp{i}", [128, 128]) for i in range(2)]

        p1_ld = []
        p1a = asm.n
        p1v = vs.n
        p1t = ts.n
        t_stats = {}
        t_nm = {}
        t_ev = {}
        for tt in range(NT):
            p1_ld.append((Q(f"x{tt % 2}"), Q(f"x{tt % 2}").n + 16 * (tt // 2 + 1)))

        def p1_evac(e, tt, dc):
            col = dc * (NT * 128) + tt * 128
            tb = ttmp[dc % 2]
            wait(e, (ts, p1t + 8 * tt + dc + 1))
            e.tensor_copy(out=tb[:, :],
                          in_=pbank[4 + (8 * tt + dc) % 4][:, 0:128])
            e.tensor_copy(out=xnTh[:, col:col + 128], in_=tb[:, :])
            t_ev[(tt, dc)] = vs.inc(e.tensor_tensor(
                out=xnTl[:, col:col + 128], in0=tb[:, :],
                in1=xnTh[:, col:col + 128], op=ALU.subtract))

        def p1_fill(e, n):
            for _ in range(n):
                e.tensor_copy(out=fil1[:, 0:1], in_=epscol[:, 0:1])

        def p1_vec(e):
            for tt in range(NT):
                xi = xin[tt % 2]
                wait(e, p1_ld[tt], (asm, p1a + 3 * tt + 1))
                if dbg and tt >= 1:
                    wait(e, (Q("dbg1"), 16 * 8 * tt))
                e.reduce_sum(out=c_sum[:, 0:1], in_=xi[:, :], axis=AX)
                e.tensor_scalar(out=c_sq2[:, 0:1], in0=c_sq[:, 0:1],
                                scalar1=1.0 / D, scalar2=None, op0=ALU.mult)
                if tt >= 1:
                    for dc in range(4):
                        p1_evac(e, tt - 1, dc)
                else:
                    p1_fill(e, 4)
                e.tensor_scalar(out=c_mu[:, 0:1], in0=c_sum[:, 0:1],
                                scalar1=1.0 / D, scalar2=None, op0=ALU.mult)
                e.tensor_scalar(out=c_ngm[:, 0:1], in0=c_sum[:, 0:1],
                                scalar1=-1.0 / D, scalar2=None, op0=ALU.mult)
                if tt >= 1:
                    for dc in range(4, 6):
                        p1_evac(e, tt - 1, dc)
                else:
                    p1_fill(e, 4)
                e.tensor_tensor(out=c_mu2[:, 0:1], in0=c_mu[:, 0:1],
                                in1=c_mu[:, 0:1], op=ALU.mult)
                if tt >= 1:
                    for dc in range(6, 8):
                        p1_evac(e, tt - 1, dc)
                else:
                    p1_fill(e, 4)
                t_stats[tt] = vs.inc(e.tensor_tensor(
                    out=c_var[:, 0:1], in0=c_sq2[:, 0:1],
                    in1=c_mu2[:, 0:1], op=ALU.subtract))
                wait(e, (asm, p1a + 3 * tt + 2))
                e.reciprocal(out=c_rstd[:, 0:1], in_=c_std[:, 0:1])
                p1_fill(e, 4)
                t_nm[tt] = vs.inc(e.scalar_tensor_tensor(
                    out=c_nmr[:, 0:1], in0=c_ngm[:, 0:1], scalar=1.0,
                    in1=c_rstd[:, 0:1], op0=ALU.mult, op1=ALU.mult))
            for dc in range(8):
                p1_evac(e, NT - 1, dc)

        block.vector(p1_vec)

        def p1_dbgpark(e, tt):
            wait(e, t_nm[tt])
            cols = [c_sum, c_sq2, c_mu, c_mu2, c_var, c_std, c_rstd, c_nmr]
            for k, cc_ in enumerate(cols):
                Q("dbg1").dma(e.dma_start(out=pdbg[tt, :, k:k + 1],
                                          in_=cc_[:, 0:1],
                                          allow_slow_non_contiguous=True))

        def p1_sync(e):
            for tt in range(NT):
                if tt >= 2:
                    wait(e, (asm, p1a + 3 * (tt - 2) + 3))
                Q(f"x{tt % 2}").dma(e.dma_start(
                    out=xin[tt % 2][:, :], in_=x_rows[128 * tt:128 * (tt + 1), :]))
                if dbg and tt >= 1:
                    p1_dbgpark(e, tt - 1)
            if dbg:
                p1_dbgpark(e, NT - 1)

        block.sync(p1_sync)

        def p1_act(e):
            for tt in range(NT):
                wait(e, p1_ld[tt], st["c_mem"])
                if tt >= 1:
                    wait(e, t_stats[tt - 1])
                asm.inc(e.activation(out=strash[:, :], in_=xin[tt % 2][:, :],
                                     func=AF.Square, accum_out=c_sq[:, 0:1]))
                wait(e, t_stats[tt])
                asm.inc(e.activation(out=c_std[:, 0:1], in_=c_var[:, 0:1],
                                     func=AF.Sqrt, bias=epscol[0:128, 0:1]))
                wait(e, t_nm[tt])
                if tt >= 2:
                    wait(e, (ts, p1t + 8 * (tt - 2) + 8))
                asm.inc(e.activation(out=xnb[tt % 2][:, :], in_=xin[tt % 2][:, :],
                                     func=AF.Identity,
                                     bias=c_nmr[0:128, 0:1],
                                     scale=c_rstd[0:128, 0:1]))

        block.scalar(p1_act)

        def p1_tsec(teng):
            for i in range(4, 8):
                wait(teng, bankfree[i])
                bankfree[i] = None
            for tt in range(NT):
                for dc in range(8):
                    g = 8 * tt + dc
                    wait(teng, (asm, p1a + 3 * tt + 3))
                    if g >= 4:
                        ptt, pdc = divmod(g - 4, 8)
                        wait(teng, t_ev[(ptt, pdc)])
                    ts.inc(teng.transpose(
                        pbank[4 + g % 4][:, 0:128],
                        xnb[tt % 2][:, 128 * dc:128 * (dc + 1)], ident_s[:, :]))

        block.tensor(p1_tsec)
        for i in range(4, 8):
            bankfree[i] = (vs, vs.n)
        st["p1_done"] = vs.tok()
        assert vs.n == p1v + 10 * NT
        assert ts.n == p1t + 8 * NT
        assert asm.n == p1a + 3 * NT

        def p1_park(e):
            wait(e, st["p1_done"])
            for dc in range(8):
                Q("p1pk").dma(e.dma_start(
                    out=pxnTh[dc, :, :],
                    in_=xnTh[:, dc * NT * 128:(dc + 1) * NT * 128]))
                Q("p1pk").dma(e.dma_start(
                    out=pxnTl[dc, :, :],
                    in_=xnTl[:, dc * NT * 128:(dc + 1) * NT * 128]))
            st["p1_park"] = Q("p1pk").tok()

        block.sync(p1_park)
        AF32.pop()

        AF32.mark()
        wtmp = [AF32.alloc(f"wtmp{i}", [128, 8 * 512]) for i in range(2)]
        qst = [AF32.alloc(f"qst{i}", [128, 512]) for i in range(2)]
        rst = [AF32.alloc(f"rst{i}", [128, 512]) for i in range(4)]
        biasb = AF32.alloc("biasb", [128, DH])
        AB16.mark()
        whh = [AB16.alloc(f"whh{i}", [128, 8 * 512]) for i in range(2)]
        wll = [AB16.alloc(f"wll{i}", [128, 8 * 512]) for i in range(2)]

        NPP = 8 + 8 * NT
        p2g = gs.n
        p2t = ts.n
        p2v = vs.n
        p2_wld = []
        p2_bld = []
        WSRC = {0: Wq, 1: Wk, 2: Wv}
        BIN = {0: bq, 1: bk, 2: bv}

        def p2G(pj, fc, tt):
            return pj * NPP + 8 + fc * NT + tt

        def p2_pktok(pj, fc, tt):
            gi = fc * NT + tt
            sl = gi % 2
            if pj < 2:
                return (Q(f"pk{2 * sl + 1}"), 16 * (pj * 4 * NT + gi // 2 + 1))
            return (Q(f"pkv{sl}"), 16 * (gi // 2 + 1))

        def p2_parks(e, pj, fc):
            for tt in range(NT):
                gi = fc * NT + tt
                G = p2G(pj, fc, tt)
                wait(e, (vs, p2v + G + 1))
                sl = gi % 2
                if pj < 2:
                    dst0 = (pqr, pqi)[0] if pj == 0 else pkr
                    PA, PB = (pqr, pqi) if pj == 0 else (pkr, pki)
                    Q(f"pk{2 * sl}").dma(e.dma_start(
                        out=PA[128 * tt:128 * (tt + 1), 512 * fc:512 * (fc + 1)],
                        in_=rst[2 * sl][:, :]))
                    Q(f"pk{2 * sl + 1}").dma(e.dma_start(
                        out=PB[128 * tt:128 * (tt + 1), 512 * fc:512 * (fc + 1)],
                        in_=rst[2 * sl + 1][:, :]))
                else:
                    Q(f"pkv{sl}").dma(e.dma_start(
                        out=pvv[128 * tt:128 * (tt + 1), 512 * fc:512 * (fc + 1)],
                        in_=qst[sl][:, :]))

        def p2_sync(e):
            for pj in range(3):
                for c in range(8):
                    bi = pj * 8 + c
                    if bi >= 1:
                        pgj, pc = divmod(bi - 1, 8)
                        wait(e, (ts, p2t + pgj * NPP + pc + 1))
                    else:
                        wait(e, (ts, p2t))
                    p2_bld.append(Q("br2").dma(e.dma_start(
                        out=rowsh[0:1, D:D + 512],
                        in_=BIN[pj][0:1, 512 * c:512 * (c + 1)])))
                for fc in range(8):
                    idx = pj * 8 + fc
                    if idx >= 2:
                        wait(e, (gs, p2g + idx - 1))
                    p2_wld.append(Q(f"wt{idx % 2}").dma(e.dma_start(
                        out=wtmp[idx % 2][:, :],
                        in_=WSRC[pj][:, 512 * fc:512 * (fc + 1)].rearrange(
                            "(kc p) n -> p kc n", p=128))))
                    if fc >= 2:
                        p2_parks(e, pj, fc - 2)
                p2_parks(e, pj, 6)
                p2_parks(e, pj, 7)
            st["p2_park"] = ([Q(f"pk{i}").tok() for i in range(4)] +
                             [Q("pkv0").tok(), Q("pkv1").tok()])

        def p2_gps(e):
            for idx in range(24):
                pj, fc = divmod(idx, 8)
                wait(e, p2_wld[idx])
                if idx >= 2:
                    pgj, pfc = divmod(idx - 2, 8)
                    wait(e, (ts, p2t + p2G(pgj, pfc, NT - 1) + 1))
                e.tensor_copy(out=whh[idx % 2][:, :], in_=wtmp[idx % 2][:, :])
                e.tensor_tensor(out=wtmp[idx % 2][:, :], in0=wtmp[idx % 2][:, :],
                                in1=whh[idx % 2][:, :], op=ALU.subtract)
                gs.inc(e.tensor_copy(out=wll[idx % 2][:, :],
                                     in_=wtmp[idx % 2][:, :]))

        block.sync(p2_sync)
        block.gpsimd(p2_gps)

        def p2_tsec(teng):
            for i in range(4):
                wait(teng, bankfree[i])
                bankfree[i] = None
            G = 0
            for pj in range(3):
                for c in range(8):
                    wait(teng, p2_bld[pj * 8 + c])
                    if G >= 4:
                        wait(teng, (vs, p2v + G - 4 + 1))
                    ts.inc(teng.matmul(pbank[G % 4][:, :], ones1f_s[0:1, :],
                                       rowsh[0:1, D:D + 512],
                                       start=True, stop=True))
                    G += 1
                for fc in range(8):
                    idx = pj * 8 + fc
                    for tt in range(NT):
                        wait(teng, (gs, p2g + idx + 1), st["p1_done"])
                        if G >= 4:
                            wait(teng, (vs, p2v + G - 4 + 1))
                        wh, wl = whh[idx % 2], wll[idx % 2]
                        last = None
                        for kc in range(8):
                            colx = kc * (NT * 128) + tt * 128
                            lh = xnTh[:, colx:colx + 128]
                            ll = xnTl[:, colx:colx + 128]
                            rh = wh[:, 512 * kc:512 * (kc + 1)]
                            rl = wl[:, 512 * kc:512 * (kc + 1)]
                            teng.matmul(pbank[G % 4][:, :], lh, rh,
                                        start=(kc == 0), stop=False)
                            teng.matmul(pbank[G % 4][:, :], lh, rl,
                                        start=False, stop=False)
                            last = teng.matmul(pbank[G % 4][:, :], ll, rh,
                                               start=False, stop=(kc == 7))
                        ts.inc(last)
                        G += 1

        block.tensor(p2_tsec)

        def p2_vec(e):
            G = 0
            for pj in range(3):
                for c in range(8):
                    wait(e, (ts, p2t + G + 1))
                    vs.inc(e.tensor_copy(out=biasb[:, 512 * c:512 * (c + 1)],
                                         in_=pbank[G % 4][:, :]))
                    G += 1
                for fc in range(8):
                    for tt in range(NT):
                        gi = fc * NT + tt
                        sl = gi % 2
                        wait(e, (ts, p2t + G + 1))
                        if gi >= 2:
                            wait(e, p2_pktok(pj, (gi - 2) // NT, (gi - 2) % NT))
                        if pj < 2:
                            e.tensor_tensor(out=qst[sl][:, :],
                                            in0=pbank[G % 4][:, :],
                                            in1=biasb[:, 512 * fc:512 * (fc + 1)],
                                            op=ALU.add)
                            e.tensor_tensor(out=rst[2 * sl][:, :],
                                            in0=qst[sl][:, :],
                                            in1=cbt[:, 512 * fc:512 * (fc + 1)],
                                            op=ALU.mult)
                            vs.inc(e.tensor_tensor(
                                out=rst[2 * sl + 1][:, :], in0=qst[sl][:, :],
                                in1=sbt[:, 512 * fc:512 * (fc + 1)],
                                op=ALU.mult))
                        else:
                            vs.inc(e.tensor_tensor(
                                out=qst[sl][:, :], in0=pbank[G % 4][:, :],
                                in1=biasb[:, 512 * fc:512 * (fc + 1)],
                                op=ALU.add))
                        G += 1

        block.vector(p2_vec)
        for i in range(4):
            bankfree[i] = (vs, p2v + 3 * NPP)
        assert ts.n == p2t + 3 * NPP
        assert vs.n == p2v + 3 * NPP
        AB16.pop()
        AF32.pop()
        AF32.pop()
        AB16.pop()

        _p3(block, locals())
        _p4(block, locals())
        _p6(block, locals())
        _p7(block, locals())
        _p8(block, locals())
        _p9(block, locals())
        _p11(block, locals())
        _p10(block, locals())

    return nc


def _p3(block, X):
    g = X
    nc = g["nc"]; NT = g["NT"]; st = g["st"]; Q = g["Q"]
    ts = g["ts"]; vs = g["vs"]; asm = g["asm"]
    pbank = g["pbank"]; bankfree = g["bankfree"]
    AF32 = g["AF32"]; AB16 = g["AB16"]
    ident_s = g["ident_s"]; epscol = g["epscol"]
    pqr = g["pqr"]; pqi = g["pqi"]; pkr = g["pkr"]; pki = g["pki"]; pvv = g["pvv"]
    poutTh = g["poutTh"]; poutTl = g["poutTl"]

    AF32.mark()
    qrb = AF32.alloc("qrb", [128, DH])
    qib = AF32.alloc("qib", [128, DH])
    krb = AF32.alloc("krb", [128, DH])
    kib = AF32.alloc("kib", [128, DH])
    vb = AF32.alloc("vb", [128, DH])
    outh = AF32.alloc("outh", [128, 2 * D])
    scr = AF32.alloc("scr", [128, 16])
    sci = AF32.alloc("sci", [128, 16])
    ssum = AF32.alloc("ssum", [128, 16])
    sexp = AF32.alloc("sexp", [128, 16])
    attw4 = AF32.alloc("attw4", [128, 16])
    c_mx = AF32.alloc("c_mx", [128, 16])
    c_nm = AF32.alloc("c_nm", [128, 16])
    c_sm4 = AF32.alloc("c_sm4", [128, 16])
    c_rs4 = AF32.alloc("c_rs4", [128, 16])
    fil3 = AF32.alloc("fil3", [128, 16])
    AB16.mark()
    oth = AB16.alloc("oth", [128, DH])
    otl = AB16.alloc("otl", [128, DH])

    p3v = vs.n
    p3a = asm.n
    p3t = ts.n
    lds = {}
    for tt in range(NT):
        for i, nm in enumerate(("a3", "b3", "c3", "d3", "v3")):
            lds[(tt, i)] = (Q(nm), Q(nm).n + 16 * (tt + 1))
    t_sc = {}
    t_nmx = {}
    t_rcp = {}
    t_ap1 = {}
    t_ap2 = {}
    t_lo = {}

    def atok(tt, k):
        return (asm, p3a + 10 * tt + k)

    def qpark(tt):
        return (Q(f"o{tt % 2}"), 32 * (tt // 2 + 1))

    pairs = [(h, gg, ri) for ri in range(2) for h in range(NH) for gg in range(NH)]

    def p3_lo_evac(e, tt, grp):
        wait(e, atok(tt, 3 + grp))
        if grp == 0 and tt >= 1:
            wait(e, qpark(tt - 1))
        t_lo[(tt, grp)] = vs.inc(e.tensor_tensor(
            out=otl[:, 512 * grp:512 * (grp + 1)],
            in0=pbank[4 + (8 * tt + grp) % 4][:, :],
            in1=oth[:, 512 * grp:512 * (grp + 1)], op=ALU.subtract))

    def p3_vec(e):
        for tt in range(NT):
            if tt >= 1:
                for grp in range(2):
                    p3_lo_evac(e, tt - 1, grp)
            wait(e, lds[(tt, 0)], lds[(tt, 1)], lds[(tt, 2)], lds[(tt, 3)])
            if tt >= 1:
                wait(e, (ts, p3t + 8 * (tt - 1) + 6))
            ins = None
            for (h, gg, ri) in pairs:
                a = qrb if ri == 0 else qib
                b = krb if ri == 0 else kib
                dst = scr if ri == 0 else sci
                ins = e.scalar_tensor_tensor(
                    out=outh[:, 0:1024], in0=a[:, 1024 * h:1024 * (h + 1)],
                    scalar=1.0, in1=b[:, 1024 * gg:1024 * (gg + 1)],
                    op0=ALU.mult, op1=ALU.mult,
                    accum_out=dst[:, 4 * h + gg:4 * h + gg + 1])
            t_sc[tt] = vs.inc(ins)
            if tt >= 1:
                for grp in range(2, 6):
                    p3_lo_evac(e, tt - 1, grp)
            else:
                for _ in range(4):
                    e.tensor_copy(out=fil3[:, 0:1], in_=epscol[:, 0:1])
            for h in range(NH):
                e.tensor_tensor(out=ssum[:, 4 * h:4 * (h + 1)],
                                in0=scr[:, 4 * h:4 * (h + 1)],
                                in1=sci[:, 4 * h:4 * (h + 1)], op=ALU.add)
            e.tensor_copy(out=fil3[:, 0:1], in_=epscol[:, 0:1])
            for h in range(NH):
                e.tensor_reduce(out=c_mx[:, h:h + 1],
                                in_=ssum[:, 4 * h:4 * (h + 1)], axis=AX,
                                op=ALU.max)
            if tt >= 1:
                for grp in range(6, 8):
                    p3_lo_evac(e, tt - 1, grp)
                for _ in range(2):
                    e.tensor_copy(out=fil3[:, 0:1], in_=epscol[:, 0:1])
            else:
                for _ in range(4):
                    e.tensor_copy(out=fil3[:, 0:1], in_=epscol[:, 0:1])
            t_nmx[tt] = vs.inc(e.tensor_scalar(
                out=c_nm[:, 0:4], in0=c_mx[:, 0:4],
                scalar1=-1.0 / 32.0, scalar2=None, op0=ALU.mult))
            wait(e, atok(tt, 1))
            t_rcp[tt] = vs.inc(e.reciprocal(out=c_rs4[:, 0:4],
                                            in_=c_sm4[:, 0:4]))
            wait(e, lds[(tt, 4)], atok(tt, 2))
            for hp in range(2):
                if hp == 0 and tt >= 1:
                    wait(e, (ts, p3t + 8 * tt))
                if hp == 1:
                    wait(e, (ts, p3t + 8 * tt + 4))
                ins = None
                for h2 in range(2):
                    h = 2 * hp + h2
                    dst = outh[:, 1024 * h2:1024 * (h2 + 1)]
                    e.tensor_scalar(out=dst, in0=vb[:, 0:1024],
                                    scalar1=attw4[:, 4 * h:4 * h + 1],
                                    scalar2=None, op0=ALU.mult)
                    for gg in range(1, NH):
                        ins = e.scalar_tensor_tensor(
                            out=dst, in0=vb[:, 1024 * gg:1024 * (gg + 1)],
                            scalar=attw4[:, 4 * h + gg:4 * h + gg + 1],
                            in1=dst, op0=ALU.mult, op1=ALU.add)
                if hp == 0:
                    t_ap1[tt] = vs.inc(ins)
                else:
                    t_ap2[tt] = vs.inc(ins)
        for grp in range(8):
            p3_lo_evac(e, NT - 1, grp)

    block.vector(p3_vec)

    def p3_park1(e, tt):
        wait(e, t_lo[(tt, 7)], atok(tt, 10))
        Q(f"o{tt % 2}").dma(e.dma_start(out=poutTh[tt, :, :], in_=oth[:, :]))
        Q(f"o{tt % 2}").dma(e.dma_start(out=poutTl[tt, :, :], in_=otl[:, :]))

    def p3_sync(e):
        wait(e, *st["p2_park"])
        for tt in range(NT):
            if tt >= 1:
                wait(e, t_sc[tt - 1])
            Q("a3").dma(e.dma_start(
                out=qrb[:, :], in_=pqr[128 * tt:128 * (tt + 1), :]))
            Q("b3").dma(e.dma_start(
                out=qib[:, :], in_=pqi[128 * tt:128 * (tt + 1), :]))
            Q("c3").dma(e.dma_start(
                out=krb[:, :], in_=pkr[128 * tt:128 * (tt + 1), :]))
            Q("d3").dma(e.dma_start(
                out=kib[:, :], in_=pki[128 * tt:128 * (tt + 1), :]))
            if tt >= 1:
                wait(e, t_ap2[tt - 1])
            Q("v3").dma(e.dma_start(
                out=vb[:, :], in_=pvv[128 * tt:128 * (tt + 1), :]))
            if tt >= 1:
                p3_park1(e, tt - 1)
        p3_park1(e, NT - 1)
        st["p3_park"] = [Q("o0").tok(), Q("o1").tok()]

    block.sync(p3_sync)

    def p3_tsec(teng):
        for i in range(4, 8):
            wait(teng, bankfree[i])
            bankfree[i] = None
        for tt in range(NT):
            for grp in range(8):
                gidx = 8 * tt + grp
                wait(teng, t_ap1[tt] if grp < 4 else t_ap2[tt])
                if gidx >= 4:
                    ptt, pgrp = divmod(gidx - 4, 8)
                    wait(teng, t_lo[(ptt, pgrp)])
                last = None
                for c in range(4):
                    blk = grp * 4 + c
                    h = blk // 8
                    dc = blk % 8
                    lcol = 128 * (8 * (h % 2) + dc)
                    last = teng.transpose(
                        pbank[4 + gidx % 4][:, 128 * c:128 * (c + 1)],
                        outh[:, lcol:lcol + 128],
                        ident_s[:, :])
                ts.inc(last)

    block.tensor(p3_tsec)

    def p3_act(e):
        for tt in range(NT):
            wait(e, t_nmx[tt])
            if tt >= 1:
                wait(e, t_rcp[tt - 1])
            ins = None
            for h in range(NH):
                ins = e.activation(out=sexp[:, 4 * h:4 * (h + 1)],
                                   in_=ssum[:, 4 * h:4 * (h + 1)],
                                   func=AF.Exp, scale=1.0 / 32.0,
                                   bias=c_nm[0:128, h:h + 1],
                                   accum_out=c_sm4[:, h:h + 1])
            asm.inc(ins)
            wait(e, t_rcp[tt])
            if tt >= 1:
                wait(e, t_ap2[tt - 1])
            ins = None
            for h in range(NH):
                ins = e.activation(out=attw4[:, 4 * h:4 * (h + 1)],
                                   in_=sexp[:, 4 * h:4 * (h + 1)],
                                   func=AF.Identity,
                                   scale=c_rs4[0:128, h:h + 1])
            asm.inc(ins)
            for grp in range(8):
                wait(e, (ts, p3t + 8 * tt + grp + 1))
                if grp == 0 and tt >= 1:
                    wait(e, qpark(tt - 1))
                asm.inc(e.activation(
                    out=oth[:, 512 * grp:512 * (grp + 1)],
                    in_=pbank[4 + (8 * tt + grp) % 4][:, :], func=AF.Identity))

    block.scalar(p3_act)
    for i in range(4, 8):
        bankfree[i] = (vs, vs.n)
    assert vs.n == p3v + 13 * NT
    assert asm.n == p3a + 10 * NT
    assert ts.n == p3t + 8 * NT
    AB16.pop()
    AF32.pop()


def _p4(block, X):
    g = X
    NT = g["NT"]; NTG = g["NTG"]; TC4 = g["TC4"]; st = g["st"]; Q = g["Q"]
    ts = g["ts"]; vs = g["vs"]; gs = g["gs"]; cc = g["cc"]
    pbank = g["pbank"]; bankfree = g["bankfree"]
    AF32 = g["AF32"]; AB16 = g["AB16"]
    Wo = g["Wo"]; poutTh = g["poutTh"]; poutTl = g["poutTl"]
    pagin = g["pagin"]; pagout = g["pagout"]

    AF32.mark()
    wotmp = [AF32.alloc(f"wotmp{i}", [128, 512]) for i in range(2)]
    agst = [AF32.alloc(f"agst{i}", [128, 512]) for i in range(2)]
    AB16.mark()
    woh = AB16.alloc("woh", [128, 32 * 512])
    wol = AB16.alloc("wol", [128, 32 * 512])
    slh = AB16.alloc("slh", [128, 8 * 512])
    sll = AB16.alloc("sll", [128, 8 * 512])

    NG4 = NTG * NH
    p4g = gs.n
    p4t = ts.n
    p4v = vs.n
    p4_wld = []
    p4_sld = {}
    p4_pk = {}

    def p4_sync(e):
        wait(e, *st["p3_park"])
        for half in range(2):
            for m in range(32):
                mi = 32 * half + m
                if mi >= 2:
                    wait(e, (gs, p4g + mi - 1))
                if half == 1 and m < 2:
                    wait(e, (ts, p4t + NG4))
                p4_wld.append(Q(f"wo{mi % 2}").dma(e.dma_start(
                    out=wotmp[mi % 2][:, :],
                    in_=Wo[128 * m:128 * (m + 1), 512 * half:512 * (half + 1)])))
            for gidx in range(NG4):
                gg = NG4 * half + gidx
                tg, h = divmod(gidx, NH)
                if gg >= 1:
                    wait(e, (ts, p4t + gg))
                for ti in range(4):
                    tt = 4 * tg + ti
                    Q("sl4").dma(e.dma_start(
                        out=slh[:, :].rearrange("p (k t) -> p k t", k=8)[
                            :, :, 128 * ti:128 * (ti + 1)],
                        in_=poutTh[tt, :, :].rearrange("p (f t) -> p f t", f=32)[
                            :, 8 * h:8 * (h + 1), :]))
                    p4_sld[gg] = Q("sl4").dma(e.dma_start(
                        out=sll[:, :].rearrange("p (k t) -> p k t", k=8)[
                            :, :, 128 * ti:128 * (ti + 1)],
                        in_=poutTl[tt, :, :].rearrange("p (f t) -> p f t", f=32)[
                            :, 8 * h:8 * (h + 1), :]))
                if gg >= 1:
                    pg = gg - 1
                    ptg, ph = divmod(pg % NG4, NH)
                    phf = pg // NG4
                    wait(e, (vs, p4v + pg + 1))
                    p4_pk[pg] = Q(f"ag{pg % 2}").dma(e.dma_start(
                        out=pagin[ph * TC4 + 128 * ptg:ph * TC4 + 128 * (ptg + 1),
                                  512 * phf:512 * (phf + 1)],
                        in_=agst[pg % 2][:, :]))
        pg = 2 * NG4 - 1
        ptg, ph = divmod(pg % NG4, NH)
        phf = pg // NG4
        wait(e, (vs, p4v + pg + 1))
        p4_pk[pg] = Q(f"ag{pg % 2}").dma(e.dma_start(
            out=pagin[ph * TC4 + 128 * ptg:ph * TC4 + 128 * (ptg + 1),
                      512 * phf:512 * (phf + 1)],
            in_=agst[pg % 2][:, :]))
        st["p4_park"] = [Q("ag0").tok(), Q("ag1").tok()]

    def p4_gps(e):
        for half in range(2):
            for m in range(32):
                mi = 32 * half + m
                wait(e, p4_wld[mi])
                e.tensor_copy(out=woh[:, 512 * m:512 * (m + 1)],
                              in_=wotmp[mi % 2][:, :])
                e.tensor_tensor(out=wotmp[mi % 2][:, :], in0=wotmp[mi % 2][:, :],
                                in1=woh[:, 512 * m:512 * (m + 1)],
                                op=ALU.subtract)
                gs.inc(e.tensor_copy(out=wol[:, 512 * m:512 * (m + 1)],
                                     in_=wotmp[mi % 2][:, :]))

    block.sync(p4_sync)
    block.gpsimd(p4_gps)

    def p4_tsec(teng):
        for i in range(4):
            wait(teng, bankfree[i])
            bankfree[i] = None
        for half in range(2):
            for gidx in range(NG4):
                gg = NG4 * half + gidx
                wait(teng, (gs, p4g + 32 * (half + 1)), p4_sld[gg])
                if gg >= 2:
                    wait(teng, (vs, p4v + gg - 1))
                bi = pbank[gg % 4]
                last = None
                for r in range(4):
                    for k in range(8):
                        m = r * 8 + k
                        first = (m == 0)
                        lastm = (m == 31)
                        lh = slh[:, 512 * k + r:512 * k + r + 509:4]
                        ll = sll[:, 512 * k + r:512 * k + r + 509:4]
                        rh = woh[:, 512 * m:512 * (m + 1)]
                        rl = wol[:, 512 * m:512 * (m + 1)]
                        teng.matmul(bi[:, :], lh, rh, start=first, stop=False)
                        teng.matmul(bi[:, :], lh, rl, start=False, stop=False)
                        last = teng.matmul(bi[:, :], ll, rh, start=False,
                                           stop=lastm)
                ts.inc(last)

    block.tensor(p4_tsec)

    def p4_vec(e):
        for gg in range(2 * NG4):
            wait(e, (ts, p4t + gg + 1))
            if gg >= 2:
                wait(e, p4_pk[gg - 2])
            vs.inc(e.tensor_copy(out=agst[gg % 2][:, :],
                                 in_=pbank[gg % 4][:, :]))

    block.vector(p4_vec)
    for i in range(4):
        bankfree[i] = (vs, p4v + 2 * NG4)
    assert ts.n == p4t + 2 * NG4
    assert vs.n == p4v + 2 * NG4
    st["p4_tend"] = ts.tok()
    st["p4_vend"] = vs.tok()

    def p5_gps(e):
        wait(e, *st["p4_park"])
        cc.inc(e.collective_compute(
            "AllGather", ALU.bypass,
            replica_groups=[list(range(W))],
            ins=[pagin[:, :]], outs=[pagout[:, :]]))

    block.gpsimd(p5_gps)
    AB16.pop()
    AF32.pop()


def _p6(block, X):
    g = X
    NT = g["NT"]; NTG = g["NTG"]; st = g["st"]; Q = g["Q"]
    ts = g["ts"]; vs = g["vs"]; asm = g["asm"]
    pbank = g["pbank"]; bankfree = g["bankfree"]
    AF32 = g["AF32"]; AB16 = g["AB16"]
    ident_s = g["ident_s"]
    w1T = g["w1T"]; w2T = g["w2T"]; h1T = g["h1T"]; h2T = g["h2T"]; w3T = g["w3T"]
    s1c_s = g["s1c_s"]; s2c_s = g["s2c_s"]
    sh1c_s = g["sh1c_s"]; sh2c_s = g["sh2c_s"]; s3c_s = g["s3c_s"]
    pxnTh = g["pxnTh"]; pxnTl = g["pxnTl"]
    phah = g["phah"]; phal = g["phal"]
    phbh = g["phbh"]; phbl = g["phbl"]
    phch = g["phcl"] if False else g["phch"]; phcl = g["phcl"]
    py0ffn = g["py0ffn"]

    AF32.mark()
    t1f = [AF32.alloc(f"t1f{i}", [128, 512]) for i in range(2)]
    haf = [AF32.alloc(f"haf{i}", [128, 512]) for i in range(2)]
    y0sm = [AF32.alloc(f"y0sm{i}", [128, 512]) for i in range(2)]
    y0r = [AF32.alloc(f"y0r{i}", [128, D]) for i in range(4)]
    AB16.mark()
    xnslh = AB16.alloc("xnslh", [128, 8 * 512])
    xnsll = AB16.alloc("xnsll", [128, 8 * 512])
    w12s = [AB16.alloc(f"w12s{i}", [128, 8 * 128]) for i in range(8)]
    h1s = [AB16.alloc(f"h1s{i}", [128, 8 * 128]) for i in range(4)]
    h2s = [AB16.alloc(f"h2s{i}", [128, 8 * 128]) for i in range(4)]
    w3s = [AB16.alloc(f"w3s{i}", [128, 8 * 128]) for i in range(4)]
    hald = [AB16.alloc(f"hald{i}", [128, 512]) for i in range(8)]
    hbld = [AB16.alloc(f"hbld{i}", [128, 512]) for i in range(8)]
    hcld = [AB16.alloc(f"hcld{i}", [128, 512]) for i in range(8)]
    pstg = [AB16.alloc(f"pstg{i}", [128, 512]) for i in range(4)]

    TPS = 360
    VPS = 112
    APS = 96
    p6 = dict(t=ts.n, v=vs.n, a=asm.n)

    def tP(ss, k):
        return (ts, p6["t"] + TPS * ss + k)

    def vP(ss, k):
        return (vs, p6["v"] + VPS * ss + k)

    def aP(ss, k):
        return (asm, p6["a"] + APS * ss + k)

    T_w12 = lambda oc, j: 2 * oc + j + 1
    T_h1 = lambda og, ic: 64 + 32 * og + ic + 1
    T_h2 = lambda og, ic: 192 + 32 * og + ic + 1
    T_w3 = lambda ic: 320 + ic + 1
    T_tr = lambda o3: 352 + o3 + 1
    V_w12 = lambda oc: oc + 1
    V_h1 = lambda og, b: 32 + 8 * og + b + 1
    V_h2 = lambda og, b: 64 + 8 * og + b + 1
    V_w3 = lambda o3: 97 if o3 == 0 else 96 + 2 * o3
    V_tr = lambda o3: 112 if o3 == 7 else 99 + 2 * o3
    A_w12 = lambda oc: oc + 1
    A_h1 = lambda og, b: 32 + 8 * og + b + 1
    A_h2 = lambda og, b: 64 + 8 * og + b + 1

    def q_xn(ss):
        return (Q("xs6"), 32 * (ss + 1))

    def q_w12(ss, oc):
        return (Q(f"wa{oc % 4}"), 32 * (8 * ss + oc // 4 + 1))

    def q_h1s(ss, og, ic):
        gi = 32 * og + ic
        return (Q(f"h1w{gi % 4}"), 16 * (32 * ss + gi // 4 + 1))

    def q_h2s(ss, og, ic):
        gi = 32 * og + ic
        return (Q(f"h2w{gi % 4}"), 16 * (32 * ss + gi // 4 + 1))

    def q_w3s(ss, ic):
        return (Q(f"w3w{ic % 2}"), 16 * (16 * ss + ic // 2 + 1))

    def q_hal(ss, og, ic):
        gi = 32 * og + ic
        return (Q(f"hal{gi % 4}"), 32 * (32 * ss + gi // 4 + 1))

    def q_hbl(ss, og, ic):
        gi = 32 * og + ic
        return (Q(f"hbl{gi % 4}"), 32 * (32 * ss + gi // 4 + 1))

    def q_hcl(ss, ic):
        return (Q(f"hcl{ic % 2}"), 32 * (16 * ss + ic // 2 + 1))

    def q_pa(ss, oc):
        return (Q(f"pa{oc % 2}"), 32 * (16 * ss + oc // 2 + 1))

    def q_pb(ss, o):
        return (Q(f"pb6{o % 2}"), 32 * (16 * ss + o // 2 + 1))

    def q_pc(ss, o):
        return (Q(f"pc6{o % 2}"), 32 * (16 * ss + o // 2 + 1))

    def q_y0(ss):
        return (Q("y06"), 64 * (ss + 1))

    def p6_gps(e):
        wait(e, st["p4_tend"])
        for ss in range(NTG):
            for oc in range(32):
                if oc >= 4:
                    wait(e, tP(ss, T_w12(oc - 4, 1)))
                elif ss >= 1:
                    wait(e, tP(ss - 1, T_w12(28 + oc, 1)))
                Q(f"wa{oc % 4}").dma(e.dma_start(
                    out=w12s[2 * (oc % 4)][:, :],
                    in_=w1T[:, 128 * oc:128 * (oc + 1)].rearrange(
                        "(k p) n -> p k n", p=128)))
                Q(f"wa{oc % 4}").dma(e.dma_start(
                    out=w12s[2 * (oc % 4) + 1][:, :],
                    in_=w2T[:, 128 * oc:128 * (oc + 1)].rearrange(
                        "(k p) n -> p k n", p=128)))
            for og in range(4):
                for ic in range(32):
                    gi = 32 * og + ic
                    if gi >= 4:
                        wait(e, tP(ss, T_h1(*divmod(gi - 4, 32))))
                    elif ss >= 1:
                        wait(e, tP(ss - 1, T_h1(3, 28 + gi)))
                    Q(f"h1w{gi % 4}").dma(e.dma_start(
                        out=h1s[gi % 4][:, :],
                        in_=h1T[128 * ic:128 * (ic + 1),
                                1024 * og:1024 * (og + 1)]))
            for og in range(4):
                for ic in range(32):
                    gi = 32 * og + ic
                    if gi >= 4:
                        wait(e, tP(ss, T_h2(*divmod(gi - 4, 32))))
                    elif ss >= 1:
                        wait(e, tP(ss - 1, T_h2(3, 28 + gi)))
                    Q(f"h2w{gi % 4}").dma(e.dma_start(
                        out=h2s[gi % 4][:, :],
                        in_=h2T[128 * ic:128 * (ic + 1),
                                1024 * og:1024 * (og + 1)]))
            for ic in range(32):
                if ic >= 2:
                    wait(e, tP(ss, T_w3(ic - 2)))
                elif ss >= 1:
                    wait(e, tP(ss - 1, T_w3(30 + ic)))
                Q(f"w3w{ic % 2}").dma(e.dma_start(
                    out=w3s[ic % 2][:, :],
                    in_=w3T[128 * ic:128 * (ic + 1), :]))

    block.gpsimd(p6_gps)

    def p6_qs(e):
        for ss in range(NTG):
            for oc in range(32):
                wait(e, vP(ss, V_w12(oc)))
                sl = 2 * (oc % 2)
                Q(f"pa{oc % 2}").dma(e.dma_start(out=phah[oc, :, :],
                                                 in_=pstg[sl][:, :]))
                Q(f"pa{oc % 2}").dma(e.dma_start(out=phal[oc, :, :],
                                                 in_=pstg[sl + 1][:, :]))
            for o in range(32):
                wait(e, vP(ss, V_h1(o // 8, o % 8)))
                sl = 2 * (o % 2)
                Q(f"pb6{o % 2}").dma(e.dma_start(out=phbh[o, :, :],
                                                 in_=pstg[sl][:, :]))
                Q(f"pb6{o % 2}").dma(e.dma_start(out=phbl[o, :, :],
                                                 in_=pstg[sl + 1][:, :]))
            for o in range(32):
                wait(e, vP(ss, V_h2(o // 8, o % 8)))
                sl = 2 * (o % 2)
                Q(f"pc6{o % 2}").dma(e.dma_start(out=phch[o, :, :],
                                                 in_=pstg[sl][:, :]))
                Q(f"pc6{o % 2}").dma(e.dma_start(out=phcl[o, :, :],
                                                 in_=pstg[sl + 1][:, :]))
            wait(e, vP(ss, V_tr(7)))
            for ttl in range(4):
                Q("y06").dma(e.dma_start(
                    out=py0ffn[512 * ss + 128 * ttl:512 * ss + 128 * (ttl + 1), :],
                    in_=y0r[ttl][:, :]))
        st["p6_park"] = Q("y06").tok()

    block.sync(p6_qs)

    def p6_act(e):
        for ss in range(NTG):
            if ss >= 1:
                wait(e, tP(ss - 1, T_w12(31, 1)))
            else:
                wait(e, st["p1_park"], st["p4_tend"], *st["p3_park"])
            Q("xs6").dma(e.dma_start(
                out=xnslh[:, :].rearrange("p (k t) -> p k t", k=8),
                in_=pxnTh[:, :, 512 * ss:512 * (ss + 1)].rearrange(
                    "k p t -> p k t")))
            Q("xs6").dma(e.dma_start(
                out=xnsll[:, :].rearrange("p (k t) -> p k t", k=8),
                in_=pxnTl[:, :, 512 * ss:512 * (ss + 1)].rearrange(
                    "k p t -> p k t")))
            for oc in range(32):
                wait(e, tP(ss, T_w12(oc, 0)))
                if oc >= 2:
                    wait(e, vP(ss, V_w12(oc - 2)))
                asm.inc(e.activation(out=t1f[oc % 2][:, :],
                                     in_=pbank[4 * (oc % 2)][:, :], func=AF.Silu,
                                     scale=s1c_s[:, oc:oc + 1]))
            for og in range(4):
                for ic in range(32):
                    gi = 32 * og + ic
                    wait(e, q_pa(ss, ic))
                    if gi >= 4:
                        wait(e, tP(ss, T_h1(*divmod(gi - 4, 32))))
                    sl = 2 * (gi % 4)
                    Q(f"hal{gi % 4}").dma(e.dma_start(out=hald[sl][:, :],
                                                      in_=phah[ic, :, :]))
                    Q(f"hal{gi % 4}").dma(e.dma_start(out=hald[sl + 1][:, :],
                                                      in_=phal[ic, :, :]))
                wait(e, tP(ss, T_h1(og, 31)))
                for b in range(8):
                    gi2 = 8 * og + b
                    if gi2 >= 2:
                        wait(e, vP(ss, 32 + gi2 - 1))
                    asm.inc(e.activation(out=t1f[b % 2][:, :],
                                         in_=pbank[b][:, :], func=AF.Gelu,
                                         scale=sh1c_s[:, 8 * og + b:
                                                      8 * og + b + 1]))
            for og in range(4):
                for ic in range(32):
                    gi = 32 * og + ic
                    wait(e, q_pb(ss, ic))
                    if gi >= 4:
                        wait(e, tP(ss, T_h2(*divmod(gi - 4, 32))))
                    sl = 2 * (gi % 4)
                    Q(f"hbl{gi % 4}").dma(e.dma_start(out=hbld[sl][:, :],
                                                      in_=phbh[ic, :, :]))
                    Q(f"hbl{gi % 4}").dma(e.dma_start(out=hbld[sl + 1][:, :],
                                                      in_=phbl[ic, :, :]))
                wait(e, tP(ss, T_h2(og, 31)))
                for b in range(8):
                    gi2 = 8 * og + b
                    if gi2 >= 2:
                        wait(e, vP(ss, 64 + gi2 - 1))
                    asm.inc(e.activation(out=t1f[b % 2][:, :],
                                         in_=pbank[b][:, :], func=AF.Gelu,
                                         scale=sh2c_s[:, 8 * og + b:
                                                      8 * og + b + 1]))
            for ic in range(32):
                wait(e, q_pc(ss, ic))
                if ic >= 2:
                    wait(e, tP(ss, T_w3(ic - 2)))
                sl = 2 * (ic % 2)
                Q(f"hcl{ic % 2}").dma(e.dma_start(out=hcld[sl][:, :],
                                                  in_=phch[ic, :, :]))
                Q(f"hcl{ic % 2}").dma(e.dma_start(out=hcld[sl + 1][:, :],
                                                  in_=phcl[ic, :, :]))

    block.scalar(p6_act)

    def p6_tsec(teng):
        for i in range(8):
            wait(teng, bankfree[i])
            bankfree[i] = None
        for ss in range(NTG):
            for oc in range(32):
                bA = pbank[4 * (oc % 2)]
                bB = pbank[4 * (oc % 2) + 1]
                wait(teng, q_w12(ss, oc), q_xn(ss))
                if oc >= 2:
                    wait(teng, vP(ss, V_w12(oc - 2)), aP(ss, A_w12(oc - 2)))
                elif ss >= 1:
                    wait(teng, vP(ss - 1, VPS), aP(ss - 1, APS))
                last = None
                for kc in range(8):
                    lw1 = w12s[2 * (oc % 4)][:, 128 * kc:128 * (kc + 1)]
                    teng.matmul(bA[:, :], lw1, xnslh[:, 512 * kc:512 * (kc + 1)],
                                start=(kc == 0), stop=False)
                    last = teng.matmul(bA[:, :], lw1,
                                       xnsll[:, 512 * kc:512 * (kc + 1)],
                                       start=False, stop=(kc == 7))
                ts.inc(last)
                last = None
                for kc in range(8):
                    lw2 = w12s[2 * (oc % 4) + 1][:, 128 * kc:128 * (kc + 1)]
                    teng.matmul(bB[:, :], lw2, xnslh[:, 512 * kc:512 * (kc + 1)],
                                start=(kc == 0), stop=False)
                    last = teng.matmul(bB[:, :], lw2,
                                       xnsll[:, 512 * kc:512 * (kc + 1)],
                                       start=False, stop=(kc == 7))
                ts.inc(last)
            for stg in range(2):
                Tfn = (T_h1, T_h2)[stg]
                Afn = (A_w12, A_h1)[stg]
                for og in range(4):
                    if og == 0 and stg == 0:
                        wait(teng, aP(ss, A_w12(31)), vP(ss, V_w12(31)))
                    if og == 0 and stg == 1:
                        wait(teng, aP(ss, A_h1(3, 7)), vP(ss, V_h1(3, 7)))
                    if og >= 1:
                        wait(teng, aP(ss, (A_h1, A_h2)[stg](og - 1, 7)))
                    for ic in range(32):
                        gi = 32 * og + ic
                        if stg == 0:
                            wait(teng, q_h1s(ss, og, ic), q_hal(ss, og, ic))
                            slab = h1s[gi % 4]
                            ldp = hald
                        else:
                            wait(teng, q_h2s(ss, og, ic), q_hbl(ss, og, ic))
                            slab = h2s[gi % 4]
                            ldp = hbld
                        sl = 2 * (gi % 4)
                        last = None
                        for o in range(8):
                            lw = slab[:, 128 * o:128 * (o + 1)]
                            teng.matmul(pbank[o][:, :], lw, ldp[sl][:, :],
                                        start=(ic == 0), stop=False)
                            last = teng.matmul(pbank[o][:, :], lw,
                                               ldp[sl + 1][:, :],
                                               start=False, stop=(ic == 31))
                        ts.inc(last)
            for ic in range(32):
                if ic == 0:
                    wait(teng, aP(ss, A_h2(3, 7)), vP(ss, V_h2(3, 7)))
                wait(teng, q_w3s(ss, ic), q_hcl(ss, ic))
                sl = 2 * (ic % 2)
                last = None
                for o3 in range(8):
                    lw = w3s[ic % 2][:, 128 * o3:128 * (o3 + 1)]
                    teng.matmul(pbank[o3][:, :], lw, hcld[sl][:, :],
                                start=(ic == 0), stop=False)
                    last = teng.matmul(pbank[o3][:, :], lw, hcld[sl + 1][:, :],
                                       start=False, stop=(ic == 31))
                ts.inc(last)
            for o3 in range(8):
                wait(teng, vP(ss, V_w3(o3)))
                if o3 >= 2:
                    wait(teng, vP(ss, V_tr(o3 - 2)))
                last = None
                for ttl in range(4):
                    last = teng.transpose(
                        pbank[o3][:, 128 * ttl:128 * (ttl + 1)],
                        y0sm[o3 % 2][:, 128 * ttl:128 * (ttl + 1)], ident_s[:, :])
                ts.inc(last)

    block.tensor(p6_tsec)

    def p6_vec(e):
        for ss in range(NTG):
            for oc in range(32):
                wait(e, tP(ss, T_w12(oc, 1)), aP(ss, A_w12(oc)))
                sl = 2 * (oc % 2)
                if oc >= 2:
                    wait(e, q_pa(ss, oc - 2))
                elif ss >= 1:
                    wait(e, q_pc(ss - 1, 30 + oc))
                e.scalar_tensor_tensor(out=haf[oc % 2][:, :],
                                       in0=pbank[4 * (oc % 2) + 1][:, :],
                                       scalar=s2c_s[:, oc:oc + 1],
                                       in1=t1f[oc % 2][:, :],
                                       op0=ALU.mult, op1=ALU.mult)
                e.tensor_copy(out=pstg[sl][:, :], in_=haf[oc % 2][:, :])
                vs.inc(e.tensor_tensor(out=pstg[sl + 1][:, :],
                                       in0=haf[oc % 2][:, :],
                                       in1=pstg[sl][:, :], op=ALU.subtract))
            for stgi, (Vp, Ap, qprev, qpark_prev) in enumerate((
                    (V_h1, A_h1, q_pa, q_pb), (V_h2, A_h2, q_pb, q_pc))):
                for og in range(4):
                    for b in range(8):
                        gi2 = 8 * og + b
                        wait(e, aP(ss, Ap(og, b)))
                        sl = 2 * (gi2 % 2)
                        if gi2 >= 2:
                            wait(e, qpark_prev(ss, gi2 - 2))
                        else:
                            wait(e, qprev(ss, 30 + gi2))
                        e.tensor_copy(out=pstg[sl][:, :], in_=t1f[b % 2][:, :])
                        vs.inc(e.tensor_tensor(out=pstg[sl + 1][:, :],
                                               in0=t1f[b % 2][:, :],
                                               in1=pstg[sl][:, :],
                                               op=ALU.subtract))
            def w3ev(o3):
                wait(e, tP(ss, T_w3(31)))
                if o3 >= 2:
                    wait(e, tP(ss, T_tr(o3 - 2)))
                vs.inc(e.tensor_scalar(out=y0sm[o3 % 2][:, :],
                                       in0=pbank[o3][:, :],
                                       scalar1=s3c_s[:, o3:o3 + 1],
                                       scalar2=None, op0=ALU.mult))

            def trev(o3):
                wait(e, tP(ss, T_tr(o3)))
                if o3 == 0 and ss >= 1:
                    wait(e, q_y0(ss - 1))
                ins = None
                for ttl in range(4):
                    ins = e.tensor_copy(
                        out=y0r[ttl][:, 128 * o3:128 * (o3 + 1)],
                        in_=pbank[o3][:, 128 * ttl:128 * (ttl + 1)])
                vs.inc(ins)

            for o3 in range(8):
                w3ev(o3)
                if o3 >= 1:
                    trev(o3 - 1)
            trev(7)

    block.vector(p6_vec)
    for i in range(8):
        bankfree[i] = (vs, p6["v"] + VPS * NTG)
    assert ts.n == p6["t"] + TPS * NTG, (ts.n, p6["t"])
    assert vs.n == p6["v"] + VPS * NTG
    assert asm.n == p6["a"] + APS * NTG
    AB16.pop()
    AF32.pop()


def _p7(block, X):
    g = X
    NT = g["NT"]; TC4 = g["TC4"]; st = g["st"]; Q = g["Q"]
    vs = g["vs"]; cc = g["cc"]
    AF32 = g["AF32"]
    x_rows = g["x_rows"]; pagout = g["pagout"]; py0ffn = g["py0ffn"]
    px1 = g["px1"]; y = g["y"]; bob = g["bob"]; self8 = g["self8"]

    AF32.mark()
    xa = [AF32.alloc(f"xa{i}", [128, D]) for i in range(2)]
    agc = [AF32.alloc(f"agc{i}", [128, 8 * D]) for i in range(2)]
    ffb = [AF32.alloc(f"ffb{i}", [128, D]) for i in range(2)]
    x1b = [AF32.alloc(f"x1b{i}", [128, D]) for i in range(2)]

    p7v = vs.n
    p7_ld = {}

    def p7_outpark(e, tt):
        wait(e, (vs, p7v + tt + 1))
        Q(f"y7{tt % 2}").dma(e.dma_start(
            out=px1[128 * tt:128 * (tt + 1), :], in_=x1b[tt % 2][:, :]))

    def p7_sync(e):
        wait(e, cc.tok(), st["p6_park"])
        for tt in range(NT):
            if tt >= 2:
                wait(e, (vs, p7v + (tt - 2) + 1))
            sem = Q(f"l7{tt % 2}")
            sem.dma(e.dma_start(
                out=xa[tt % 2][:, :], in_=x_rows[128 * tt:128 * (tt + 1), :]))
            R = TC4 // 128
            for b in range(2):
                for h in range(NH):
                    srcrow = ((4 * b + tt // R) * (NH * TC4) + h * TC4 +
                              128 * (tt % R))
                    sem.dma(e.dma_start(
                        out=agc[tt % 2][:, D * (4 * b + h):D * (4 * b + h + 1)],
                        in_=pagout[srcrow:srcrow + 128, :]))
            p7_ld[tt] = sem.dma(e.dma_start(
                out=ffb[tt % 2][:, :], in_=py0ffn[128 * tt:128 * (tt + 1), :]))
            if tt >= 2:
                p7_outpark(e, tt - 2)
        p7_outpark(e, NT - 2)
        p7_outpark(e, NT - 1)
        st["p7_park"] = [Q("y70").tok(), Q("y71").tok()]

    def p7_vec(e):
        for tt in range(NT):
            wait(e, p7_ld[tt])
            if tt >= 2:
                wait(e, (Q(f"y7{tt % 2}"), 16 * (tt // 2)))
            xb = x1b[tt % 2]
            ag = agc[tt % 2]
            e.tensor_scalar(out=xb[:, :], in0=ag[:, 0:D],
                            scalar1=self8[:, 0:1], scalar2=None, op0=ALU.mult)
            for i in range(1, 8):
                e.scalar_tensor_tensor(out=xb[:, :], in0=ag[:, D * i:D * (i + 1)],
                                       scalar=self8[:, i:i + 1], in1=xb[:, :],
                                       op0=ALU.mult, op1=ALU.add)
            e.tensor_tensor(out=xb[:, :], in0=xb[:, :], in1=xa[tt % 2][:, :],
                            op=ALU.add)
            e.tensor_tensor(out=xb[:, :], in0=xb[:, :], in1=bob[:, :],
                            op=ALU.add)
            vs.inc(e.tensor_tensor(out=xb[:, :], in0=xb[:, :],
                                   in1=ffb[tt % 2][:, :], op=ALU.add))

    block.sync(p7_sync)
    block.vector(p7_vec)

    def p7_fin(e):
        wait(e, *st["p7_park"])

    block.sync(p7_fin)
    AF32.pop()


def host_inputs(n_tok, core_id, full):
    TC = n_tok // W
    j = core_id

    def blockscale(s, noc):
        return np.ascontiguousarray(s.reshape(noc, 128).T).astype(np.float32)

    sel = np.zeros((1, 8), np.float32)
    sel[0, 4 * (j % 2) + (j // 2)] = 1.0
    d = {
        "x_rows": full["L0_attn_x"][0, j * TC:(j + 1) * TC, :],
        "phase": full["L0_attn_phase"],
        "Wq": full["L0_attn_Wq"], "Wk": full["L0_attn_Wk"], "Wv": full["L0_attn_Wv"],
        "bq": full["L0_attn_bq"].reshape(1, -1),
        "bk": full["L0_attn_bk"].reshape(1, -1),
        "bv": full["L0_attn_bv"].reshape(1, -1),
        "Wo": full["L0_attn_Wo"], "bo": full["L0_attn_bo"].reshape(1, -1),
        "w1T": np.ascontiguousarray(full["L0_ffn_w1_w"].T),
        "w2T": np.ascontiguousarray(full["L0_ffn_w2_w"].T),
        "h1T": np.ascontiguousarray(full["L0_ffn_h1_w"].T),
        "h2T": np.ascontiguousarray(full["L0_ffn_h2_w"].T),
        "w3T": np.ascontiguousarray(full["L0_ffn_w3_w"].T),
        "s1c": blockscale(full["L0_ffn_w1_s"], 32),
        "s2c": blockscale(full["L0_ffn_w2_s"], 32),
        "sh1c": blockscale(full["L0_ffn_h1_s"], 32),
        "sh2c": blockscale(full["L0_ffn_h2_s"], 32),
        "s3c": blockscale(full["L0_ffn_w3_s"], 8),
        "identf": np.eye(128, dtype=np.float32),
        "selmask": sel,
        "wq1T": np.ascontiguousarray(full["L1_attn_Wq1"].T),
        "wk1T": np.ascontiguousarray(full["L1_attn_Wk1"].T),
        "wk2T": np.ascontiguousarray(full["L1_attn_Wk2"].T),
        "wpjT": np.ascontiguousarray(full["L1_attn_Wproj"].T),
        "bq1r": full["L1_attn_bq1"].reshape(1, -1),
        "bk1r": full["L1_attn_bk1"].reshape(1, -1),
        "bk2r": full["L1_attn_bk2"].reshape(1, -1),
        "bpjr": full["L1_attn_bproj"].reshape(1, -1),
        "rhT": np.ascontiguousarray(full["L1_ffn_rhs"].T),
        "prevm8": (np.arange(8) < j).astype(np.float32).reshape(8, 1),
        "prevo8": (np.arange(8) == j - 1).astype(np.float32).reshape(8, 1),
        "e0m": (np.arange(128) != 0).astype(np.float32).reshape(128, 1)
        if j == 0 else np.ones((128, 1), np.float32),
        "triu128": np.triu(np.ones((128, 128), np.float32)),
        "bc127": np.vstack([np.zeros((127, 128), np.float32),
                            np.ones((1, 128), np.float32)]),
    }
    return {k: np.ascontiguousarray(np.asarray(v, np.float32)) for k, v in d.items()}


def run(n_tok, full):
    nc = build(n_tok)
    in_maps = [host_inputs(n_tok, j, full) for j in range(W)]
    res = run_bass_kernel_spmd(nc, in_maps, core_ids=list(range(W)))
    out = np.concatenate([res.results[j]["y"] for j in range(W)], axis=0)
    return out.reshape(1, n_tok, D)


def _p8(block, X):
    g = X
    NT = g["NT"]; st = g["st"]; Q = g["Q"]
    ts = g["ts"]; vs = g["vs"]; asm = g["asm"]
    pbank = g["pbank"]; bankfree = g["bankfree"]
    AF32 = g["AF32"]; AB16 = g["AB16"]
    ident_s = g["ident_s"]; epscol = g["epscol"]
    px1 = g["px1"]; pxn1Th = g["pxn1Th"]; pxn1Tl = g["pxn1Tl"]; pxn1 = g["pxn1"]

    AB16.mark()
    xnTh = AB16.alloc("x8Th", [128, 8 * NT * 128])
    xnTl = AB16.alloc("x8Tl", [128, 8 * NT * 128])
    AF32.mark()
    xin = [AF32.alloc(f"x8in{i}", [128, D]) for i in range(2)]
    xnb = [AF32.alloc(f"x8nb{i}", [128, D]) for i in range(2)]
    strash = AF32.alloc("s8tr", [128, D])
    c_sq = AF32.alloc("c8sq", [128, 16])
    c_sum = AF32.alloc("c8sum", [128, 16])
    c_mu = AF32.alloc("c8mu", [128, 16])
    c_mu2 = AF32.alloc("c8mu2", [128, 16])
    c_sq2 = AF32.alloc("c8sq2", [128, 16])
    c_var = AF32.alloc("c8var", [128, 16])
    c_std = AF32.alloc("c8std", [128, 16])
    c_rstd = AF32.alloc("c8rstd", [128, 16])
    c_nmr = AF32.alloc("c8nmr", [128, 16])
    c_ngm = AF32.alloc("c8ngm", [128, 16])
    fil1 = AF32.alloc("f8il", [128, 16])
    ttmp = [AF32.alloc(f"t8mp{i}", [128, 128]) for i in range(2)]

    p8_ld = []
    p8a = asm.n
    p8v = vs.n
    p8t = ts.n
    t_stats = {}
    t_nm = {}
    t_ev = {}
    for tt in range(NT):
        p8_ld.append((Q(f"z{tt % 2}"), Q(f"z{tt % 2}").n + 16 * (tt // 2 + 1)))

    def p8_evac(e, tt, dc):
        col = dc * (NT * 128) + tt * 128
        tb = ttmp[dc % 2]
        wait(e, (ts, p8t + 8 * tt + dc + 1))
        e.tensor_copy(out=tb[:, :], in_=pbank[4 + (8 * tt + dc) % 4][:, 0:128])
        e.tensor_copy(out=xnTh[:, col:col + 128], in_=tb[:, :])
        t_ev[(tt, dc)] = vs.inc(e.tensor_tensor(
            out=xnTl[:, col:col + 128], in0=tb[:, :],
            in1=xnTh[:, col:col + 128], op=ALU.subtract))

    def p8_fill(e, n):
        for _ in range(n):
            e.tensor_copy(out=fil1[:, 0:1], in_=epscol[:, 0:1])

    def p8_vec(e):
        for tt in range(NT):
            xi = xin[tt % 2]
            wait(e, p8_ld[tt], (asm, p8a + 3 * tt + 1))
            e.reduce_sum(out=c_sum[:, 0:1], in_=xi[:, :], axis=AX)
            e.tensor_scalar(out=c_sq2[:, 0:1], in0=c_sq[:, 0:1],
                            scalar1=1.0 / D, scalar2=None, op0=ALU.mult)
            if tt >= 1:
                for dc in range(4):
                    p8_evac(e, tt - 1, dc)
            else:
                p8_fill(e, 4)
            e.tensor_scalar(out=c_mu[:, 0:1], in0=c_sum[:, 0:1],
                            scalar1=1.0 / D, scalar2=None, op0=ALU.mult)
            e.tensor_scalar(out=c_ngm[:, 0:1], in0=c_sum[:, 0:1],
                            scalar1=-1.0 / D, scalar2=None, op0=ALU.mult)
            if tt >= 1:
                for dc in range(4, 6):
                    p8_evac(e, tt - 1, dc)
            else:
                p8_fill(e, 4)
            e.tensor_tensor(out=c_mu2[:, 0:1], in0=c_mu[:, 0:1],
                            in1=c_mu[:, 0:1], op=ALU.mult)
            if tt >= 1:
                for dc in range(6, 8):
                    p8_evac(e, tt - 1, dc)
            else:
                p8_fill(e, 4)
            t_stats[tt] = vs.inc(e.tensor_tensor(
                out=c_var[:, 0:1], in0=c_sq2[:, 0:1],
                in1=c_mu2[:, 0:1], op=ALU.subtract))
            wait(e, (asm, p8a + 3 * tt + 2))
            e.reciprocal(out=c_rstd[:, 0:1], in_=c_std[:, 0:1])
            p8_fill(e, 4)
            t_nm[tt] = vs.inc(e.scalar_tensor_tensor(
                out=c_nmr[:, 0:1], in0=c_ngm[:, 0:1], scalar=1.0,
                in1=c_rstd[:, 0:1], op0=ALU.mult, op1=ALU.mult))
        for dc in range(8):
            p8_evac(e, NT - 1, dc)

    block.vector(p8_vec)

    def p8_sync(e):
        wait(e, *st["p7_park"])
        for tt in range(NT):
            if tt >= 2:
                wait(e, (asm, p8a + 3 * (tt - 2) + 3))
            Q(f"z{tt % 2}").dma(e.dma_start(
                out=xin[tt % 2][:, :], in_=px1[128 * tt:128 * (tt + 1), :]))
            if tt >= 1:
                wait(e, (asm, p8a + 3 * (tt - 1) + 3))
                Q("p8r").dma(e.dma_start(
                    out=pxn1[1 + 128 * (tt - 1):1 + 128 * tt, :],
                    in_=xnb[(tt - 1) % 2][:, :]))
        wait(e, (asm, p8a + 3 * (NT - 1) + 3))
        Q("p8r").dma(e.dma_start(
            out=pxn1[1 + 128 * (NT - 1):1 + 128 * NT, :],
            in_=xnb[(NT - 1) % 2][:, :]))
        st["p8_rows"] = Q("p8r").tok()

    block.sync(p8_sync)

    def p8_act(e):
        for tt in range(NT):
            wait(e, p8_ld[tt], st["c_mem"])
            if tt >= 1:
                wait(e, t_stats[tt - 1])
            asm.inc(e.activation(out=strash[:, :], in_=xin[tt % 2][:, :],
                                 func=AF.Square, accum_out=c_sq[:, 0:1]))
            wait(e, t_stats[tt])
            asm.inc(e.activation(out=c_std[:, 0:1], in_=c_var[:, 0:1],
                                 func=AF.Sqrt, bias=epscol[0:128, 0:1]))
            wait(e, t_nm[tt])
            if tt >= 2:
                wait(e, (ts, p8t + 8 * (tt - 2) + 8),
                     (Q("p8r"), 16 * (tt - 1)))
            asm.inc(e.activation(out=xnb[tt % 2][:, :], in_=xin[tt % 2][:, :],
                                 func=AF.Identity,
                                 bias=c_nmr[0:128, 0:1],
                                 scale=c_rstd[0:128, 0:1]))

    block.scalar(p8_act)

    def p8_tsec(teng):
        for i in range(4, 8):
            wait(teng, bankfree[i])
            bankfree[i] = None
        for tt in range(NT):
            for dc in range(8):
                gdx = 8 * tt + dc
                wait(teng, (asm, p8a + 3 * tt + 3))
                if gdx >= 4:
                    ptt, pdc = divmod(gdx - 4, 8)
                    wait(teng, t_ev[(ptt, pdc)])
                ts.inc(teng.transpose(
                    pbank[4 + gdx % 4][:, 0:128],
                    xnb[tt % 2][:, 128 * dc:128 * (dc + 1)], ident_s[:, :]))

    block.tensor(p8_tsec)
    for i in range(4, 8):
        bankfree[i] = (vs, vs.n)
    st["p8_done"] = vs.tok()
    assert vs.n == p8v + 10 * NT
    assert ts.n == p8t + 8 * NT
    assert asm.n == p8a + 3 * NT

    def p8_park(e):
        wait(e, st["p8_done"])
        for dc in range(8):
            Q("p8pk").dma(e.dma_start(
                out=pxn1Th[dc, :, :],
                in_=xnTh[:, dc * NT * 128:(dc + 1) * NT * 128]))
            Q("p8pk").dma(e.dma_start(
                out=pxn1Tl[dc, :, :],
                in_=xnTl[:, dc * NT * 128:(dc + 1) * NT * 128]))
        st["p8_park"] = Q("p8pk").tok()

    block.sync(p8_park)
    AF32.pop()
    AB16.pop()


def _p9(block, X):
    g = X
    NT = g["NT"]; NTG = g["NTG"]; TC = g["TC"]; st = g["st"]; Q = g["Q"]
    ts = g["ts"]; vs = g["vs"]; asm = g["asm"]; gs = g["gs"]; cc = g["cc"]
    pbank = g["pbank"]; bankfree = g["bankfree"]
    AF32 = g["AF32"]; AB16 = g["AB16"]
    ones1f_s = g["ones1f_s"]; onescol_s = g["onescol_s"]
    prevm_s = g["prevm_s"]; prevo_s = g["prevo_s"]; epscol = g["epscol"]
    pxn1Th = g["pxn1Th"]; pxn1Tl = g["pxn1Tl"]; pxn1 = g["pxn1"]
    pq1 = g["pq1"]; poar = g["poar"]; pk1 = g["pk1"]; pkv1 = g["pkv1"]
    pk21 = g["pk21"]
    pag1in = g["pag1in"]; pag1out = g["pag1out"]; prow9 = g["prow9"]
    pprefs = g["pprefs"]
    WT = {0: g["wk1T"], 1: g["wq1T"], 2: g["wk2T"]}
    BR = {0: g["bk1r"], 1: g["bq1r"], 2: g["bk2r"]}
    C0232 = float(np.float32(0.02) / 32.0)

    AF32.mark()
    wtmp9 = AF32.alloc("wtmp9", [128, 4096])
    b9r = AF32.alloc("b9r", [128, 1024])
    ebuf = [AF32.alloc(f"e9b{i}", [128, 1024]) for i in range(2)]
    xnr9 = [AF32.alloc(f"x9r{i}", [128, 1024]) for i in range(2)]
    kvb = [AF32.alloc(f"k9v{i}", [128, 1024]) for i in range(2)]
    g9 = AF32.alloc("g9", [128, 4096])
    r9 = AF32.alloc("r9", [128, 8192])
    AB16.mark()
    w9h = AB16.alloc("w9h", [128, 8192])
    w9l = AB16.alloc("w9l", [128, 8192])
    xslh = [AB16.alloc(f"xs9h{i}", [128, 4096]) for i in range(2)]
    xsll = [AB16.alloc(f"xs9l{i}", [128, 4096]) for i in range(2)]
    scrh = AB16.alloc("scr9h", [128, 1024])
    scrl = AB16.alloc("scr9l", [128, 1024])
    brow = AB16.alloc("b9row", [128, 2048])
    bndh = AB16.alloc("bnd9h", [128, 16])
    bndl = AB16.alloc("bnd9l", [128, 16])

    p9t = ts.n
    p9a = asm.n
    p9v = vs.n
    p9g = gs.n
    B9 = {n: Q(n).n for n in (
        "wl9", "b9", "x90", "x91", "kr90", "kr91", "p9k0", "p9k1",
        "p9v0", "p9v1", "p9q0", "p9q1", "p9c0", "p9c1", "a1p", "mg9",
        "pr9s", "bd9")}

    def R9(k):
        return r9[0:1, 1024 * k:1024 * (k + 1)]

    def T9mm(pj, tt, ob):
        if pj == 0:
            return (ts, p9t + 6 * tt + ob + 1)
        if pj == 1:
            return (ts, p9t + 6 * NT + 2 * tt + ob + 1)
        return (ts, p9t + 8 * NT + 8 + 2 * tt + ob + 1)

    def T9tot(tt, q):
        return (ts, p9t + 6 * tt + 2 + q + 1)

    def T9msk(q):
        return (ts, p9t + 8 * NT + q + 1)

    def A9(pj, tt, ob):
        return (asm, p9a + 2 * NT * pj + 2 * tt + ob + 1)

    def V9kv(tt):
        return (vs, p9v + tt + 1)
    V9totrow = (vs, p9v + NT + 1)

    def V9msk(q):
        return (vs, p9v + NT + 1 + q + 1)
    V9rows = (vs, p9v + NT + 10)

    def V9scr(ss):
        return (vs, p9v + NT + 10 + ss + 1)

    def G9sp(pj, ob):
        return (gs, p9g + 2 * pj + ob + 1)

    def q_w9(pj, ob):
        return (Q("wl9"), B9["wl9"] + 16 * (2 * pj + ob + 1))

    def q_x9(pj, ss):
        sl = ss % 2
        return (Q(f"x9{sl}"), B9[f"x9{sl}"] +
                32 * (pj * ((NTG + 1 - sl) // 2) + ss // 2 + 1))

    def q_b9(pj):
        return (Q("b9"), B9["b9"] + 16 * (pj + 1))

    def q_xr9(tt):
        return (Q(f"kr9{tt % 2}"), B9[f"kr9{tt % 2}"] + 16 * (tt // 2 + 1))

    def q_pk(tt):
        return (Q(f"p9k{tt % 2}"), B9[f"p9k{tt % 2}"] + 16 * (tt // 2 + 1))

    def q_pq(tt):
        return (Q(f"p9q{tt % 2}"), B9[f"p9q{tt % 2}"] + 16 * (tt // 2 + 1))

    q_gath = (Q("mg9"), B9["mg9"] + 64)
    q_bd = (Q("bd9"), B9["bd9"] + 32)

    def p9_sync(e):
        for pj in range(3):
            if pj >= 1:
                wait(e, T9mm(pj - 1, NT - 1, 1))
            else:
                wait(e, st["p8_park"], st["p8_rows"])
            for ob in range(2):
                if not (pj == 0 and ob == 0):
                    wait(e, G9sp(pj, ob - 1) if ob == 1 else G9sp(pj - 1, 1))
                Q("wl9").dma(e.dma_start(
                    out=wtmp9[:, :],
                    in_=WT[pj][:, 512 * ob:512 * (ob + 1)].rearrange(
                        "(kc p) n -> p kc n", p=128)))
            Q("b9").dma(e.dma_start(out=b9r[0:1, :], in_=BR[pj][0:1, :]))
            for ss in range(NTG):
                sl = ss % 2
                if ss >= 2:
                    wait(e, T9mm(pj, 4 * (ss - 2) + 3, 1))
                if pj == 2 and ss >= 1:
                    wait(e, V9scr(ss - 1))
                Q(f"x9{sl}").dma(e.dma_start(
                    out=xslh[sl][:, :].rearrange("p (k t) -> p k t", k=8),
                    in_=pxn1Th[:, :, 512 * ss:512 * (ss + 1)].rearrange(
                        "k p t -> p k t")))
                Q(f"x9{sl}").dma(e.dma_start(
                    out=xsll[sl][:, :].rearrange("p (k t) -> p k t", k=8),
                    in_=pxn1Tl[:, :, 512 * ss:512 * (ss + 1)].rearrange(
                        "k p t -> p k t")))
                for tl in range(4):
                    tt = 4 * ss + tl
                    if pj == 0:
                        if tt >= 2:
                            wait(e, V9kv(tt - 2))
                        Q(f"kr9{tt % 2}").dma(e.dma_start(
                            out=xnr9[tt % 2][:, :],
                            in_=pxn1[1 + 128 * tt:1 + 128 * (tt + 1), :]))
                    if tt >= 1:
                        ptt = tt - 1
                        if pj == 0:
                            wait(e, A9(0, ptt, 1))
                            Q(f"p9k{ptt % 2}").dma(e.dma_start(
                                out=pk1[128 * ptt:128 * (ptt + 1), :],
                                in_=ebuf[ptt % 2][:, :]))
                            wait(e, V9kv(ptt))
                            Q(f"p9v{ptt % 2}").dma(e.dma_start(
                                out=pkv1[128 * ptt:128 * (ptt + 1), :],
                                in_=kvb[ptt % 2][:, :]))
                        elif pj == 1:
                            wait(e, A9(1, ptt, 1))
                            Q(f"p9q{ptt % 2}").dma(e.dma_start(
                                out=pq1[1 + 128 * ptt:1 + 128 * (ptt + 1), :],
                                in_=ebuf[ptt % 2][:, :]))
                        else:
                            wait(e, A9(2, ptt, 1))
                            Q(f"p9c{ptt % 2}").dma(e.dma_start(
                                out=pk21[128 * ptt:128 * (ptt + 1), :],
                                in_=ebuf[ptt % 2][:, :]))
            tt = NT - 1
            if pj == 0:
                wait(e, A9(0, tt, 1))
                Q(f"p9k{tt % 2}").dma(e.dma_start(
                    out=pk1[128 * tt:128 * (tt + 1), :], in_=ebuf[tt % 2][:, :]))
                wait(e, V9kv(tt))
                Q(f"p9v{tt % 2}").dma(e.dma_start(
                    out=pkv1[128 * tt:128 * (tt + 1), :], in_=kvb[tt % 2][:, :]))
            elif pj == 1:
                wait(e, A9(1, tt, 1))
                Q(f"p9q{tt % 2}").dma(e.dma_start(
                    out=pq1[1 + 128 * tt:1 + 128 * (tt + 1), :],
                    in_=ebuf[tt % 2][:, :]))
                wait(e, V9totrow)
                Q("a1p").dma(e.dma_start(out=pag1in[0:1, :],
                                         in_=g9[0:1, 0:1024]))
                Q("a1p").dma(e.dma_start(out=pag1in[1:2, :],
                                         in_=g9[0:1, 1024:2048]))
                wait(e, q_pq(NT - 1))
                Q("a1p").dma(e.dma_start(out=pag1in[2:3, :],
                                         in_=pq1[TC:TC + 1, :]))
                Q("a1p").dma(e.dma_start(out=pag1in[3:4, :],
                                         in_=pxn1[TC:TC + 1, :]))
                wait(e, cc.at(2))
                for qd in range(4):
                    Q("mg9").dma(e.dma_start(
                        out=g9[0:8, 1024 * qd:1024 * (qd + 1)],
                        in_=pag1out[qd::4, :]))
                wait(e, V9rows)
                Q("mg9").dma(e.dma_start(out=pq1[0:1, :], in_=R9(2)))
                Q("mg9").dma(e.dma_start(out=poar[0:1, :], in_=R9(7)))
                Q("pr9s").dma(e.dma_start(out=prow9[0:1, :],
                                          in_=brow[0:1, 0:1024]))
                Q("pr9s").dma(e.dma_start(out=prow9[1:2, :],
                                          in_=brow[0:1, 1024:2048]))
                Q("pr9s").dma(e.dma_start(out=pprefs[0:1, :], in_=R9(4)))
                Q("pr9s").dma(e.dma_start(out=pprefs[1:2, :], in_=R9(1)))
                wait(e, (Q("pr9s"), B9["pr9s"] + 32))
                Q("bd9").dma(e.dma_start(
                    out=bndh[:, 0:8],
                    in_=prow9[0:1, :].rearrange("r (k p) -> p (r k)", p=128),
                    allow_slow_non_contiguous=True))
                Q("bd9").dma(e.dma_start(
                    out=bndl[:, 0:8],
                    in_=prow9[1:2, :].rearrange("r (k p) -> p (r k)", p=128),
                    allow_slow_non_contiguous=True))
        tt = NT - 1
        wait(e, A9(2, tt, 1))
        Q(f"p9c{tt % 2}").dma(e.dma_start(
            out=pk21[128 * tt:128 * (tt + 1), :], in_=ebuf[tt % 2][:, :]))
        st["p9_kpark"] = [Q("p9k0").tok(), Q("p9k1").tok()]
        st["p9_vpark"] = [Q("p9v0").tok(), Q("p9v1").tok()]
        st["p9_qpark"] = [Q("p9q0").tok(), Q("p9q1").tok()]
        st["p9_cpark"] = [Q("p9c0").tok(), Q("p9c1").tok()]
        st["p9_bnd"] = Q("mg9").tok()
        st["p9_prefs"] = (Q("pr9s"), B9["pr9s"] + 64)

    block.sync(p9_sync)

    def p9_gps(e):
        for pj in range(3):
            for ob in range(2):
                if pj == 2 and ob == 0:
                    wait(e, (Q("a1p"), B9["a1p"] + 64))
                    cc.inc(e.collective_compute(
                        "AllGather", ALU.bypass,
                        replica_groups=[list(range(W))],
                        ins=[pag1in[:, :]], outs=[pag1out[:, :]]))
                wait(e, q_w9(pj, ob))
                if pj + ob == 0:
                    wait(e, st["p8_park"])
                e.tensor_copy(out=w9h[:, 4096 * ob:4096 * (ob + 1)],
                              in_=wtmp9[:, :])
                e.tensor_tensor(out=wtmp9[:, :], in0=wtmp9[:, :],
                                in1=w9h[:, 4096 * ob:4096 * (ob + 1)],
                                op=ALU.subtract)
                gs.inc(e.tensor_copy(out=w9l[:, 4096 * ob:4096 * (ob + 1)],
                                     in_=wtmp9[:, :]))

    block.gpsimd(p9_gps)

    def p9_tsec(teng):
        for i in range(4):
            wait(teng, bankfree[i])
            bankfree[i] = None
        for i in range(4, 8):
            wait(teng, bankfree[i])
            bankfree[i] = None
        G = 0
        for pj in range(3):
            for ss in range(NTG):
                sl = ss % 2
                for tl in range(4):
                    tt = 4 * ss + tl
                    for ob in range(2):
                        wait(teng, q_x9(pj, ss), G9sp(pj, ob), q_b9(pj))
                        if pj == 2 and tl == 0:
                            wait(teng, V9scr(ss))
                        bk = pbank[G % 4]
                        if G >= 4:
                            pG = G - 4
                            ppj, prem = divmod(pG, 2 * NT)
                            ptt, pob = divmod(prem, 2)
                            wait(teng, A9(ppj, ptt, pob))
                        last = None
                        for kc in range(8):
                            if pj == 2 and tl == 0:
                                lh = scrh[:, 128 * kc:128 * (kc + 1)]
                                ll = scrl[:, 128 * kc:128 * (kc + 1)]
                            elif pj == 2:
                                c0 = 512 * kc + 128 * tl - 1
                                lh = xslh[sl][:, c0:c0 + 128]
                                ll = xsll[sl][:, c0:c0 + 128]
                            else:
                                c0 = 512 * kc + 128 * tl
                                lh = xslh[sl][:, c0:c0 + 128]
                                ll = xsll[sl][:, c0:c0 + 128]
                            rh = w9h[:, 4096 * ob + 512 * kc:
                                     4096 * ob + 512 * (kc + 1)]
                            rl = w9l[:, 4096 * ob + 512 * kc:
                                     4096 * ob + 512 * (kc + 1)]
                            teng.matmul(bk[:, :], lh, rh,
                                        start=(kc == 0), stop=False)
                            teng.matmul(bk[:, :], lh, rl,
                                        start=False, stop=False)
                            last = teng.matmul(bk[:, :], ll, rh,
                                               start=False, stop=False)
                        last = teng.matmul(bk[:, :], ones1f_s[0:1, :],
                                           b9r[0:1, 512 * ob:512 * (ob + 1)],
                                           start=False, stop=True)
                        ts.inc(last)
                        G += 1
                    if pj == 0:
                        for q in range(2):
                            wait(teng, A9(0, tt, q))
                            ts.inc(teng.matmul(
                                pbank[4 + q][0:1, :], onescol_s[:, 0:1],
                                ebuf[tt % 2][:, 512 * q:512 * (q + 1)],
                                start=(tt == 0), stop=(tt == NT - 1)))
                        for q in range(2):
                            wait(teng, V9kv(tt))
                            ts.inc(teng.matmul(
                                pbank[6 + q][0:1, :], onescol_s[:, 0:1],
                                kvb[tt % 2][:, 512 * q:512 * (q + 1)],
                                start=(tt == 0), stop=(tt == NT - 1)))
            if pj == 1:
                for q in range(8):
                    qd, blk = divmod(q, 2)
                    lm = prevm_s[0:8, 0:1] if qd < 2 else prevo_s[0:8, 0:1]
                    wait(teng, q_gath, V9totrow)
                    if q >= 1:
                        wait(teng, V9msk(q - 1))
                    ts.inc(teng.matmul(
                        pbank[4][0:1, 0:512], lm,
                        g9[0:8, 1024 * qd + 512 * blk:
                           1024 * qd + 512 * (blk + 1)],
                        start=True, stop=True))

    block.tensor(p9_tsec)

    def p9_act(e):
        for pj in range(3):
            for tt in range(NT):
                for ob in range(2):
                    wait(e, T9mm(pj, tt, ob))
                    if tt >= 2:
                        if pj == 0:
                            wait(e, q_pk(tt - 2), V9kv(tt - 2))
                        elif pj == 1:
                            wait(e, q_pq(tt - 2))
                        else:
                            wait(e, (Q(f"p9c{tt % 2}"),
                                     B9[f"p9c{tt % 2}"] + 16 * ((tt - 2) // 2 + 1)))
                    elif pj == 1:
                        wait(e, q_pk(NT - 2 + tt), V9kv(NT - 1))
                    elif pj == 2:
                        wait(e, q_pq(NT - 2 + tt))
                    dst = ebuf[tt % 2][:, 512 * ob:512 * (ob + 1)]
                    bk = pbank[(2 * NT * pj + 2 * tt + ob) % 4]
                    if pj == 0:
                        asm.inc(e.activation(out=dst, in_=bk[:, :],
                                             func=AF.Exp))
                    elif pj == 1:
                        asm.inc(e.activation(out=dst, in_=bk[:, :],
                                             func=AF.Sigmoid))
                    else:
                        asm.inc(e.activation(out=dst, in_=bk[:, :],
                                             func=AF.Sigmoid, scale=C0232))

    block.scalar(p9_act)

    def p9_vec(e):
        for tt in range(NT):
            wait(e, A9(0, tt, 1), q_xr9(tt))
            if tt >= 2:
                wait(e, (Q(f"p9v{tt % 2}"),
                         B9[f"p9v{tt % 2}"] + 16 * ((tt - 2) // 2 + 1)))
            vs.inc(e.tensor_tensor(out=kvb[tt % 2][:, :],
                                   in0=ebuf[tt % 2][:, :],
                                   in1=xnr9[tt % 2][:, :], op=ALU.mult))
        wait(e, T9tot(NT - 1, 3))
        e.tensor_copy(out=g9[0:1, 0:512], in_=pbank[4][0:1, 0:512])
        e.tensor_copy(out=g9[0:1, 512:1024], in_=pbank[5][0:1, 0:512])
        e.tensor_copy(out=g9[0:1, 1024:1536], in_=pbank[6][0:1, 0:512])
        vs.inc(e.tensor_copy(out=g9[0:1, 1536:2048], in_=pbank[7][0:1, 0:512]))
        for q in range(8):
            qd, blk = divmod(q, 2)
            wait(e, T9msk(q))
            vs.inc(e.tensor_copy(
                out=r9[0:1, 1024 * qd + 512 * blk:1024 * qd + 512 * (blk + 1)],
                in_=pbank[4][0:1, 0:512]))
        e.tensor_scalar(out=R9(4), in0=R9(0), scalar1=1e-6, scalar2=None,
                        op0=ALU.add)
        e.reciprocal(out=R9(5), in_=R9(4))
        e.tensor_tensor(out=R9(6), in0=R9(1), in1=R9(5), op=ALU.mult)
        e.tensor_tensor(out=R9(7), in0=R9(6), in1=R9(2), op=ALU.mult)
        e.tensor_copy(out=brow[0:1, 0:1024], in_=R9(3))
        vs.inc(e.tensor_tensor(out=brow[0:1, 1024:2048], in0=R9(3),
                               in1=brow[0:1, 0:1024], op=ALU.subtract))
        for ss in range(NTG):
            sl = ss % 2
            wait(e, q_x9(2, ss))
            if ss >= 1:
                wait(e, T9mm(2, 4 * (ss - 1), 1))
            for kc in range(8):
                if ss == 0:
                    wait(e, q_bd)
                    e.tensor_copy(out=scrh[:, 128 * kc:128 * kc + 1],
                                  in_=bndh[:, kc:kc + 1])
                    e.tensor_copy(out=scrl[:, 128 * kc:128 * kc + 1],
                                  in_=bndl[:, kc:kc + 1])
                else:
                    psl = (ss - 1) % 2
                    e.tensor_copy(out=scrh[:, 128 * kc:128 * kc + 1],
                                  in_=xslh[psl][:, 512 * kc + 511:512 * (kc + 1)])
                    e.tensor_copy(out=scrl[:, 128 * kc:128 * kc + 1],
                                  in_=xsll[psl][:, 512 * kc + 511:512 * (kc + 1)])
                e.tensor_copy(out=scrh[:, 128 * kc + 1:128 * (kc + 1)],
                              in_=xslh[sl][:, 512 * kc:512 * kc + 127])
                ins = e.tensor_copy(out=scrl[:, 128 * kc + 1:128 * (kc + 1)],
                                    in_=xsll[sl][:, 512 * kc:512 * kc + 127])
            vs.inc(ins)

    block.vector(p9_vec)
    for i in range(4):
        bankfree[i] = (asm, p9a + 6 * NT)
    for i in range(4, 8):
        bankfree[i] = V9msk(8)
    assert ts.n == p9t + 8 * NT + 8 + 2 * NT
    assert asm.n == p9a + 6 * NT
    assert vs.n == p9v + NT + 10 + NTG
    assert gs.n == p9g + 6
    AB16.pop()
    AF32.pop()


def _p11(block, X):
    g = X
    NT = g["NT"]; st = g["st"]; Q = g["Q"]
    ts = g["ts"]; vs = g["vs"]; asm = g["asm"]
    pbank = g["pbank"]; bankfree = g["bankfree"]
    AF32 = g["AF32"]; AB16 = g["AB16"]
    ident_s = g["ident_s"]; ones1f_s = g["ones1f_s"]; epscol = g["epscol"]
    pxn1 = g["pxn1"]; py1f = g["py1f"]; rhT = g["rhT"]; prs1 = g["prs1"]

    AF32.mark()
    rht = [AF32.alloc(f"r11t{i}", [128, 1024]) for i in range(2)]
    xnt = [AF32.alloc(f"x11t{i}", [128, 1024]) for i in range(2)]
    qv = AF32.alloc("q11v", [128, 1024])
    qa = AF32.alloc("q11a", [128, 1024])
    qb = AF32.alloc("q11b", [128, 1024])
    qm = AF32.alloc("q11m", [128, 1024])
    ydq = [AF32.alloc(f"y11d{i}", [128, 1024]) for i in range(2)]
    rsb = AF32.alloc("r11sb", [128, 1024])
    rsrow = AF32.alloc("r11sr", [128, 1024])
    cq = AF32.alloc("c11q", [128, 16])
    fil = AF32.alloc("f11", [128, 16])
    AB16.mark()
    qr9 = AB16.alloc("q11r", [128, 8192])
    qlT = [AB16.alloc(f"q11l{i}", [128, 1024]) for i in range(2)]

    p11t = ts.n
    p11a = asm.n
    p11v = vs.n
    B11 = {n: Q(n).n for n in ("rh110", "rh111", "xl110", "xl111",
                               "yp110", "yp111", "rsc11", "rsl11")}


    def T11rt(i, ic):
        return (ts, p11t + 8 * i + ic + 1)
    T11rsb = (ts, p11t + 64 + 1)

    def T11lt(t, dc):
        return (ts, p11t + 65 + 10 * t + dc + 1)

    def T11mm(t, ob):
        return (ts, p11t + 65 + 10 * t + 8 + ob + 1)

    def A11rq(i):
        return (asm, p11a + 9 * i + 1)

    def A11re(i, ic):
        return (asm, p11a + 9 * i + 1 + ic + 1)

    def A11lq(t):
        return (asm, p11a + 72 + 11 * t + 1)

    def A11le(t, dc):
        return (asm, p11a + 72 + 11 * t + 1 + dc + 1)

    def A11dq(t, ob):
        return (asm, p11a + 72 + 11 * t + 9 + ob + 1)

    def V11rs(i):
        return (vs, p11v + 2 * i + 1)

    def V11rq(i):
        return (vs, p11v + 2 * i + 2)
    V11rsb = (vs, p11v + 16 + 1)
    V11rsr = (vs, p11v + 16 + 2)

    def V11ls(t):
        return (vs, p11v + 18 + 3 * t + 1)

    def V11lq(t):
        return (vs, p11v + 18 + 3 * t + 2)

    def V11yf(t):
        return (vs, p11v + 18 + 3 * t + 3)

    def q_rh(i):
        return (Q(f"rh11{i % 2}"), B11[f"rh11{i % 2}"] + 16 * (i // 2 + 1))

    def q_xl(t):
        return (Q(f"xl11{t % 2}"), B11[f"xl11{t % 2}"] + 16 * (t // 2 + 1))

    def q_yp(t):
        return (Q(f"yp11{t % 2}"), B11[f"yp11{t % 2}"] + 16 * (t // 2 + 1))
    q_rsl = (Q("rsl11"), B11["rsl11"] + 16)

    def p11_sync(e):
        wait(e, *st["p9_cpark"])
        for i in range(8):
            if i >= 2:
                wait(e, A11rq(i - 2), (Q("rsc11"), B11["rsc11"] + 16 * (i - 1)))
            Q(f"rh11{i % 2}").dma(e.dma_start(
                out=rht[i % 2][:, :], in_=rhT[128 * i:128 * (i + 1), :]))
            wait(e, V11rs(i))
            Q("rsc11").dma(e.dma_start(
                out=prs1[0:1, 128 * i:128 * (i + 1)].rearrange("r k -> k r"),
                in_=cq[:, 8 + (i % 2):8 + (i % 2) + 1]))
        wait(e, (Q("rsc11"), B11["rsc11"] + 16 * 8))
        Q("rsl11").dma(e.dma_start(out=rsrow[0:1, :], in_=prs1[0:1, :]))
        for t in range(NT):
            if t >= 2:
                wait(e, A11lq(t - 2))
            Q(f"xl11{t % 2}").dma(e.dma_start(
                out=xnt[t % 2][:, :],
                in_=pxn1[1 + 128 * t:1 + 128 * (t + 1), :]))
            if t >= 1:
                wait(e, V11yf(t - 1))
                Q(f"yp11{(t - 1) % 2}").dma(e.dma_start(
                    out=py1f[128 * (t - 1):128 * t, :],
                    in_=ydq[(t - 1) % 2][:, :]))
        wait(e, V11yf(NT - 1))
        Q(f"yp11{(NT - 1) % 2}").dma(e.dma_start(
            out=py1f[128 * (NT - 1):128 * NT, :], in_=ydq[(NT - 1) % 2][:, :]))
        st["p11_park"] = [Q("yp110").tok(), Q("yp111").tok()]

    def p11_ladder(e):
        e.tensor_scalar(out=qb[:, :], in0=qv[:, :], scalar1=127.5,
                        scalar2=None, op0=ALU.add)
        e.tensor_copy(out=qm[:, :], in_=qb[:, :])
        for k in range(7, -1, -1):
            e.tensor_scalar(out=qa[:, :], in0=qm[:, :],
                            scalar1=float(2 ** k), scalar2=float(2 ** k),
                            op0=ALU.is_ge, op1=ALU.mult)
            e.tensor_tensor(out=qm[:, :], in0=qm[:, :], in1=qa[:, :],
                            op=ALU.subtract)
        e.tensor_tensor(out=qa[:, :], in0=qb[:, :], in1=qm[:, :],
                        op=ALU.subtract)
        e.tensor_scalar(out=qa[:, :], in0=qa[:, :], scalar1=-127.0,
                        scalar2=None, op0=ALU.add)
        e.tensor_scalar(out=qa[:, :], in0=qa[:, :], scalar1=-127.0,
                        scalar2=None, op0=ALU.max)
        return vs.inc(e.tensor_scalar(out=qa[:, :], in0=qa[:, :],
                                      scalar1=127.0, scalar2=None,
                                      op0=ALU.min))

    def p11_vec(e):
        for i in range(8):
            wait(e, q_rh(i))
            if i >= 2:
                wait(e, (Q("rsc11"), B11["rsc11"] + 16 * (i - 1)))
            sc = 8 + (i % 2)
            e.tensor_reduce(out=cq[:, 0:1], in_=rht[i % 2][:, :], axis=AX,
                            op=ALU.max, apply_absolute_value=True)
            for _ in range(4):
                e.tensor_copy(out=fil[:, 0:1], in_=epscol[:, 0:1])
            e.tensor_scalar(out=cq[:, 1:2], in0=cq[:, 0:1], scalar1=1e-12,
                            scalar2=None, op0=ALU.max)
            for _ in range(4):
                e.tensor_copy(out=fil[:, 0:1], in_=epscol[:, 0:1])
            e.reciprocal(out=cq[:, 2:3], in_=cq[:, 1:2])
            e.tensor_scalar(out=cq[:, sc:sc + 1], in0=cq[:, 1:2],
                            scalar1=1.0 / 127.0, scalar2=None, op0=ALU.mult)
            for _ in range(3):
                e.tensor_copy(out=fil[:, 0:1], in_=epscol[:, 0:1])
            vs.inc(e.tensor_scalar(out=cq[:, 3:4], in0=cq[:, 2:3],
                                   scalar1=127.0, scalar2=None, op0=ALU.mult))
            wait(e, A11rq(i))
            if i >= 1:
                wait(e, T11rt(i - 1, 7))
            p11_ladder(e)
        wait(e, T11rsb)
        e.tensor_copy(out=rsb[:, 0:512], in_=pbank[2][:, 0:512])
        vs.inc(e.tensor_copy(out=rsb[:, 512:1024], in_=pbank[3][:, 0:512]))
        wait(e, q_rsl)
        vs.inc(e.tensor_copy(out=fil[:, 1:2], in_=epscol[:, 0:1]))
        for t in range(NT):
            wait(e, q_xl(t))
            if t < 2:
                wait(e, (Q("rsc11"), B11["rsc11"] + 128))
            else:
                wait(e, A11dq(t - 2, 1))
            sc = 8 + (t % 2)
            e.tensor_reduce(out=cq[:, 0:1], in_=xnt[t % 2][:, :], axis=AX,
                            op=ALU.max, apply_absolute_value=True)
            for _ in range(4):
                e.tensor_copy(out=fil[:, 0:1], in_=epscol[:, 0:1])
            e.tensor_scalar(out=cq[:, 1:2], in0=cq[:, 0:1], scalar1=1e-12,
                            scalar2=None, op0=ALU.max)
            for _ in range(4):
                e.tensor_copy(out=fil[:, 0:1], in_=epscol[:, 0:1])
            e.reciprocal(out=cq[:, 2:3], in_=cq[:, 1:2])
            e.tensor_scalar(out=cq[:, sc:sc + 1], in0=cq[:, 1:2],
                            scalar1=1.0 / 127.0, scalar2=None, op0=ALU.mult)
            for _ in range(3):
                e.tensor_copy(out=fil[:, 0:1], in_=epscol[:, 0:1])
            vs.inc(e.tensor_scalar(out=cq[:, 3:4], in0=cq[:, 2:3],
                                   scalar1=127.0, scalar2=None, op0=ALU.mult))
            wait(e, A11lq(t))
            wait(e, T11rt(7, 7) if t == 0 else T11lt(t - 1, 7))
            p11_ladder(e)
            wait(e, A11dq(t, 1))
            if t >= 2:
                wait(e, q_yp(t - 2))
            vs.inc(e.tensor_tensor(out=ydq[t % 2][:, :], in0=ydq[t % 2][:, :],
                                   in1=rsb[:, :], op=ALU.mult))

    def p11_act(e):
        for i in range(8):
            wait(e, V11rs(i))
            if i >= 1:
                wait(e, V11rq(i - 1))
            asm.inc(e.activation(out=qv[:, :], in_=rht[i % 2][:, :],
                                 func=AF.Identity, scale=cq[0:128, 3:4]))
            for ic in range(8):
                wait(e, T11rt(i, ic))
                asm.inc(e.activation(
                    out=qr9[:, 1024 * ic + 128 * i:1024 * ic + 128 * (i + 1)],
                    in_=pbank[4 + (8 * i + ic) % 4][:, 0:128],
                    func=AF.Identity))
        for t in range(NT):
            wait(e, V11ls(t))
            wait(e, V11rq(7) if t == 0 else V11lq(t - 1))
            asm.inc(e.activation(out=qv[:, :], in_=xnt[t % 2][:, :],
                                 func=AF.Identity, scale=cq[0:128, 3:4]))
            for dc in range(8):
                wait(e, T11lt(t, dc))
                asm.inc(e.activation(
                    out=qlT[t % 2][:, 128 * dc:128 * (dc + 1)],
                    in_=pbank[4 + (8 * t + dc) % 4][:, 0:128],
                    func=AF.Identity))
            for ob in range(2):
                wait(e, T11mm(t, ob))
                if t >= 2:
                    wait(e, q_yp(t - 2))
                sc = 8 + (t % 2)
                asm.inc(e.activation(
                    out=ydq[t % 2][:, 512 * ob:512 * (ob + 1)],
                    in_=pbank[ob][:, :], func=AF.Identity,
                    scale=cq[0:128, sc:sc + 1]))

    def p11_tsec(teng):
        for i in range(8):
            wait(teng, bankfree[i])
            bankfree[i] = None
        for i in range(8):
            for ic in range(8):
                gdx = 8 * i + ic
                wait(teng, V11rq(i))
                if gdx >= 4:
                    pi, pic = divmod(gdx - 4, 8)
                    wait(teng, A11re(pi, pic))
                ts.inc(teng.transpose(
                    pbank[4 + gdx % 4][:, 0:128],
                    qa[:, 128 * ic:128 * (ic + 1)], ident_s[:, :]))
        wait(teng, q_rsl)
        teng.matmul(pbank[2][:, 0:512], ones1f_s[0:1, :], rsrow[0:1, 0:512],
                    start=True, stop=True)
        ts.inc(teng.matmul(pbank[3][:, 0:512], ones1f_s[0:1, :],
                           rsrow[0:1, 512:1024], start=True, stop=True))
        for t in range(NT):
            for dc in range(8):
                gdx = 8 * t + dc
                wait(teng, V11lq(t))
                if gdx >= 4:
                    pt, pdc = divmod(gdx - 4, 8)
                    wait(teng, A11le(pt, pdc))
                else:
                    wait(teng, A11re(7, 4 + dc))
                ts.inc(teng.transpose(
                    pbank[4 + gdx % 4][:, 0:128],
                    qa[:, 128 * dc:128 * (dc + 1)], ident_s[:, :]))
            for ob in range(2):
                wait(teng, A11le(t, 7))
                if t >= 2:
                    wait(teng, A11dq(t - 2, ob))
                if t == 0 and ob == 0:
                    wait(teng, V11rsb)
                last = None
                for ic in range(8):
                    last = teng.matmul(
                        pbank[ob][:, :],
                        qlT[t % 2][:, 128 * ic:128 * (ic + 1)],
                        qr9[:, 1024 * ic + 512 * ob:1024 * ic + 512 * (ob + 1)],
                        start=(ic == 0), stop=(ic == 7))
                ts.inc(last)

    block.sync(p11_sync)
    block.vector(p11_vec)
    block.scalar(p11_act)
    block.tensor(p11_tsec)
    for i in range(8):
        bankfree[i] = (asm, p11a + 72 + 11 * NT)
    assert ts.n == p11t + 65 + 10 * NT
    assert asm.n == p11a + 72 + 11 * NT
    assert vs.n == p11v + 18 + 3 * NT
    AB16.pop()
    AF32.pop()


def _p10(block, X):
    g = X
    NT = g["NT"]; st = g["st"]; Q = g["Q"]
    ts = g["ts"]; vs = g["vs"]; asm = g["asm"]; gs = g["gs"]; cc = g["cc"]
    pbank = g["pbank"]; bankfree = g["bankfree"]
    AF32 = g["AF32"]; AB16 = g["AB16"]
    ident_s = g["ident_s"]; ones1f_s = g["ones1f_s"]; onescol_s = g["onescol_s"]
    triu_s = g["triu_s"]; bc127_s = g["bc127_s"]
    prevm_s = g["prevm_s"]; e0m_s = g["e0m_s"]
    pk1 = g["pk1"]; pkv1 = g["pkv1"]; pq1 = g["pq1"]; poar = g["poar"]
    pxn1 = g["pxn1"]; pk21 = g["pk21"]; pk2e = g["pk2e"]; py1f = g["py1f"]
    px1 = g["px1"]; y = g["y"]; pprefs = g["pprefs"]
    pag2in = g["pag2in"]; pag2out = g["pag2out"]
    wpjT = g["wpjT"]; bpjr = g["bpjr"]
    C0232 = float(np.float32(0.02) / 32.0)

    p10t = ts.n
    p10a_ = asm.n
    p10v = vs.n
    p10g = gs.n
    B10 = {n: Q(n).n for n in (
        "k100", "k101", "v100", "v101", "q100", "q101", "oa100", "oa101",
        "os100", "os101", "vx100", "vx101", "kk100", "kk101", "ke100",
        "ke101", "kel100", "kel101", "q2100", "q2101", "oal100", "oal101",
        "x100", "x101", "yfl100", "yfl101", "yy100", "yy101", "pf10",
        "g210", "a2p", "wl10", "b10")}

    def BQ(pre, t):
        nm = f"{pre}{t % 2}"
        return (Q(nm), B10[nm] + 16 * (t // 2 + 1))

    AF32.mark()
    pf2r = AF32.alloc("pf2r", [128, 1024])

    def T10ch(t, c):
        return (ts, p10t + 4 * t + c + 1)

    def T10tt(t, b):
        return (ts, p10t + 4 * NT + 2 * t + b + 1)

    def T10mk(b):
        return (ts, p10t + 6 * NT + b + 1)

    def T10s2(t, b):
        return (ts, p10t + 6 * NT + 2 + 12 * t + b + 1)

    def T10tr(t, dc):
        return (ts, p10t + 6 * NT + 2 + 12 * t + 2 + dc + 1)

    def T10pm(t, ob):
        return (ts, p10t + 6 * NT + 2 + 12 * t + 10 + ob + 1)

    def A10h(t, dc):
        return (asm, p10a_ + 8 * t + dc + 1)

    def V10ev(t, c):
        return (vs, p10v + 5 * t + c + 1)

    def V10oa(t):
        return (vs, p10v + 5 * t + 5)

    def V10ke(t):
        return (vs, p10v + 5 * NT + t + 1)
    V10t2 = (vs, p10v + 6 * NT + 1)

    def V10pf(b):
        return (vs, p10v + 6 * NT + 1 + b + 1)

    def V10s2(t, b):
        return (vs, p10v + 6 * NT + 3 + 12 * t + b + 1)

    def V10os(t):
        return (vs, p10v + 6 * NT + 3 + 12 * t + 3)

    def V10lo(t, dc):
        return (vs, p10v + 6 * NT + 3 + 12 * t + 3 + dc + 1)

    def V10y(t):
        return (vs, p10v + 6 * NT + 3 + 12 * t + 12)

    def q_k10(t):
        return BQ("k10", t)

    def q_v10(t):
        return BQ("v10", t)

    def q_q10(t):
        return BQ("q10", t)

    def q_oa(t):
        return BQ("oa10", t)

    def q_os(t):
        return BQ("os10", t)

    def q_vx(t):
        return BQ("vx10", t)

    def q_kk(t):
        return BQ("kk10", t)

    def q_ke(t):
        return BQ("ke10", t)

    def q_kel(t):
        return BQ("kel10", t)

    def q_q2(t):
        return BQ("q210", t)

    def q_oal(t):
        return BQ("oal10", t)

    def q_x1(t):
        return BQ("x10", t)

    def q_yfl(t):
        return BQ("yfl10", t)

    def q_yy(t):
        return BQ("yy10", t)
    q_pf10 = (Q("pf10"), B10["pf10"] + 32)
    q_g2 = (Q("g210"), B10["g210"] + 16)
    q_a2p = (Q("a2p"), B10["a2p"] + 16)
    q_bp = (Q("b10"), B10["b10"] + 16)

    AF32.mark()
    kcl = [AF32.alloc(f"kcl{i}", [128, 1024]) for i in range(2)]
    kvl = [AF32.alloc(f"kvl{i}", [128, 1024]) for i in range(2)]
    ql10 = [AF32.alloc(f"ql10{i}", [128, 1024]) for i in range(2)]
    kce = [AF32.alloc(f"kce{i}", [128, 1024]) for i in range(2)]
    kvce = [AF32.alloc(f"kvce{i}", [128, 1024]) for i in range(2)]
    rkb = AF32.alloc("rkb", [128, 1024])
    Sb = AF32.alloc("Sb", [128, 1024])
    oarb = [AF32.alloc(f"oarb{i}", [128, 1024]) for i in range(2)]
    pf10 = AF32.alloc("pf10b", [128, 2048])

    def p10a_sync(e):
        wait(e, st["p9_prefs"], *st["p11_park"])
        Q("pf10").dma(e.dma_start(out=pf10[0:1, 0:1024], in_=pprefs[0:1, :]))
        Q("pf10").dma(e.dma_start(out=pf10[0:1, 1024:2048], in_=pprefs[1:2, :]))
        for t in range(NT):
            if t >= 2:
                wait(e, T10ch(t - 2, 1))
            wait(e, *st["p9_kpark"]) if t == 0 else None
            Q(f"k10{t % 2}").dma(e.dma_start(
                out=kcl[t % 2][:, :], in_=pk1[128 * t:128 * (t + 1), :]))
            if t == 0:
                wait(e, *st["p9_vpark"])
            if t >= 2:
                wait(e, T10ch(t - 2, 3))
            Q(f"v10{t % 2}").dma(e.dma_start(
                out=kvl[t % 2][:, :], in_=pkv1[128 * t:128 * (t + 1), :]))
            if t == 0:
                wait(e, *st["p9_qpark"], st["p9_bnd"])
            if t >= 2:
                wait(e, V10oa(t - 2))
            Q(f"q10{t % 2}").dma(e.dma_start(
                out=ql10[t % 2][:, :], in_=pq1[1 + 128 * t:1 + 128 * (t + 1), :]))
            if t >= 1:
                wait(e, V10oa(t - 1))
                Q(f"oa10{(t - 1) % 2}").dma(e.dma_start(
                    out=poar[1 + 128 * (t - 1):1 + 128 * t, :],
                    in_=oarb[(t - 1) % 2][:, :]))
        wait(e, V10oa(NT - 1))
        Q(f"oa10{(NT - 1) % 2}").dma(e.dma_start(
            out=poar[1 + 128 * (NT - 1):1 + 128 * NT, :],
            in_=oarb[(NT - 1) % 2][:, :]))

    def p10a_tsec(teng):
        for i in range(8):
            wait(teng, bankfree[i])
            bankfree[i] = None
        for t in range(NT):
            for c in range(4):
                qq, b = divmod(c, 2)
                src = kcl if qq == 0 else kvl
                wait(teng, q_k10(t) if qq == 0 else q_v10(t))
                if t == 0:
                    wait(teng, q_pf10)
                else:
                    wait(teng, V10ev(t - 1, c))
                teng.matmul(pbank[c][:, :], triu_s[:, :],
                            src[t % 2][:, 512 * b:512 * (b + 1)],
                            start=True, stop=False)
                if t == 0:
                    ts.inc(teng.matmul(
                        pbank[c][:, :], ones1f_s[0:1, :],
                        pf10[0:1, 1024 * qq + 512 * b:1024 * qq + 512 * (b + 1)],
                        start=False, stop=True))
                else:
                    ts.inc(teng.matmul(
                        pbank[c][:, :], bc127_s[:, :],
                        (kce if qq == 0 else kvce)[(t - 1) % 2][
                            :, 512 * b:512 * (b + 1)],
                        start=False, stop=True))

    def p10a_vec(e):
        for t in range(NT):
            for c in range(4):
                qq, b = divmod(c, 2)
                dst = (kce if qq == 0 else kvce)[t % 2]
                wait(e, T10ch(t, c))
                vs.inc(e.tensor_copy(out=dst[:, 512 * b:512 * (b + 1)],
                                     in_=pbank[c][:, :]))
            e.reciprocal(out=rkb[:, :], in_=kce[t % 2][:, :])
            e.tensor_tensor(out=Sb[:, :], in0=kvce[t % 2][:, :],
                            in1=rkb[:, :], op=ALU.mult)
            wait(e, q_q10(t))
            if t >= 2:
                wait(e, q_oa(t - 2))
            vs.inc(e.tensor_tensor(out=oarb[t % 2][:, :], in0=Sb[:, :],
                                   in1=ql10[t % 2][:, :], op=ALU.mult))

    block.sync(p10a_sync)
    block.tensor(p10a_tsec)
    block.vector(p10a_vec)
    AF32.pop()

    AF32.mark()
    oas = [AF32.alloc(f"oas{i}", [128, 1024]) for i in range(2)]
    vtl = [AF32.alloc(f"vtl{i}", [128, 1024]) for i in range(2)]
    k2l = [AF32.alloc(f"k2l{i}", [128, 1024]) for i in range(2)]
    k2eb = [AF32.alloc(f"k2eb{i}", [128, 1024]) for i in range(2)]
    ag2r = AF32.alloc("ag2r", [128, 1024])
    g2t = AF32.alloc("g2t", [128, 1024])

    def p10b_sync(e):
        wait(e, T10ch(NT - 1, 3), V10oa(NT - 1), q_oa(NT - 1),
             q_oa(NT - 2) if NT >= 2 else None)
        for t in range(NT):
            if t >= 2:
                wait(e, V10ke(t - 2))
            wait(e, q_oa(t))
            if t >= 1:
                wait(e, q_oa(t - 1))
            Q(f"os10{t % 2}").dma(e.dma_start(
                out=oas[t % 2][:, :], in_=poar[128 * t:128 * (t + 1), :]))
            Q(f"vx10{t % 2}").dma(e.dma_start(
                out=vtl[t % 2][:, :], in_=pxn1[1 + 128 * t:1 + 128 * (t + 1), :]))
            if t == 0:
                wait(e, *st["p9_cpark"])
            Q(f"kk10{t % 2}").dma(e.dma_start(
                out=k2l[t % 2][:, :], in_=pk21[128 * t:128 * (t + 1), :]))
            if t >= 1:
                wait(e, V10ke(t - 1), T10tt(t - 1, 1))
                Q(f"ke10{(t - 1) % 2}").dma(e.dma_start(
                    out=pk2e[128 * (t - 1):128 * t, :],
                    in_=k2eb[(t - 1) % 2][:, :]))
        wait(e, V10ke(NT - 1), T10tt(NT - 1, 1))
        Q(f"ke10{(NT - 1) % 2}").dma(e.dma_start(
            out=pk2e[128 * (NT - 1):128 * NT, :],
            in_=k2eb[(NT - 1) % 2][:, :]))
        wait(e, V10t2)
        Q("a2p").dma(e.dma_start(out=pag2in[0:1, :], in_=ag2r[0:1, :]))
        wait(e, cc.at(3))
        Q("g210").dma(e.dma_start(out=g2t[0:8, :], in_=pag2out[0:8, :]))

    def p10b_tsec(teng):
        for t in range(NT):
            for b in range(2):
                wait(teng, V10ke(t))
                ts.inc(teng.matmul(
                    pbank[6 + b][0:1, :], onescol_s[:, 0:1],
                    k2eb[t % 2][:, 512 * b:512 * (b + 1)],
                    start=(t == 0), stop=(t == NT - 1)))
        for b in range(2):
            wait(teng, q_g2, V10t2)
            if b >= 1:
                wait(teng, V10pf(b - 1))
            ts.inc(teng.matmul(
                pbank[6][0:1, 0:512], prevm_s[0:8, 0:1],
                g2t[0:8, 512 * b:512 * (b + 1)], start=True, stop=True))

    def p10b_vec(e):
        for t in range(NT):
            wait(e, q_os(t), q_vx(t), q_kk(t))
            e.tensor_tensor(out=k2eb[t % 2][:, :], in0=vtl[t % 2][:, :],
                            in1=oas[t % 2][:, :], op=ALU.subtract)
            if t == 0:
                e.tensor_scalar(out=k2eb[0][:, :], in0=k2eb[0][:, :],
                                scalar1=e0m_s[:, 0:1], scalar2=None,
                                op0=ALU.mult)
            if t >= 2:
                wait(e, q_ke(t - 2))
            vs.inc(e.tensor_tensor(out=k2eb[t % 2][:, :], in0=k2eb[t % 2][:, :],
                                   in1=k2l[t % 2][:, :], op=ALU.mult))
        wait(e, T10tt(NT - 1, 1))
        e.tensor_copy(out=ag2r[0:1, 0:512], in_=pbank[6][0:1, 0:512])
        vs.inc(e.tensor_copy(out=ag2r[0:1, 512:1024], in_=pbank[7][0:1, 0:512]))
        for b in range(2):
            wait(e, T10mk(b))
            vs.inc(e.tensor_copy(out=pf2r[0:1, 512 * b:512 * (b + 1)],
                                 in_=pbank[6][0:1, 0:512]))

    def p10b_gps(e):
        wait(e, q_a2p)
        cc.inc(e.collective_compute(
            "AllGather", ALU.bypass,
            replica_groups=[list(range(W))],
            ins=[pag2in[:, :]], outs=[pag2out[:, :]]))

    block.sync(p10b_sync)
    block.tensor(p10b_tsec)
    block.vector(p10b_vec)
    block.gpsimd(p10b_gps)
    AF32.pop()

    AF32.mark()
    wtmp10 = AF32.alloc("wtmp10", [128, 4096])
    bprow = AF32.alloc("bprow", [128, 1024])
    x1l = [AF32.alloc(f"x1l{i}", [128, 1024]) for i in range(2)]
    yfl = [AF32.alloc(f"yfl{i}", [128, 1024]) for i in range(2)]
    k2el = [AF32.alloc(f"k2el{i}", [128, 1024]) for i in range(2)]
    q2l = [AF32.alloc(f"q2l{i}", [128, 1024]) for i in range(2)]
    oarl = [AF32.alloc(f"oarl{i}", [128, 1024]) for i in range(2)]
    s2ce = [AF32.alloc(f"s2ce{i}", [128, 1024]) for i in range(2)]
    omb = AF32.alloc("omb", [128, 1024])
    osb = AF32.alloc("osb", [128, 1024])
    ybuf = [AF32.alloc(f"ybuf{i}", [128, 1024]) for i in range(2)]
    AB16.mark()
    wph = AB16.alloc("wph", [128, 8192])
    wpl = AB16.alloc("wpl", [128, 8192])
    osTh = [AB16.alloc(f"osTh{i}", [128, 1024]) for i in range(2)]
    osTl = [AB16.alloc(f"osTl{i}", [128, 1024]) for i in range(2)]

    def G10sp(ob):
        return (gs, p10g + ob + 1)

    def p10c_sync(e):
        wait(e, V10pf(1), q_ke(NT - 1), q_ke(NT - 2) if NT >= 2 else None)
        for ob in range(2):
            if ob == 1:
                wait(e, G10sp(0))
            Q("wl10").dma(e.dma_start(
                out=wtmp10[:, :],
                in_=wpjT[:, 512 * ob:512 * (ob + 1)].rearrange(
                    "(kc p) n -> p kc n", p=128)))
        Q("b10").dma(e.dma_start(out=bprow[0:1, :], in_=bpjr[0:1, :]))
        for t in range(NT):
            if t >= 2:
                wait(e, T10s2(t - 2, 1))
            Q(f"kel10{t % 2}").dma(e.dma_start(
                out=k2el[t % 2][:, :], in_=pk2e[128 * t:128 * (t + 1), :]))
            if t >= 2:
                wait(e, V10os(t - 2))
            Q(f"q210{t % 2}").dma(e.dma_start(
                out=q2l[t % 2][:, :], in_=pq1[128 * t:128 * (t + 1), :]))
            Q(f"oal10{t % 2}").dma(e.dma_start(
                out=oarl[t % 2][:, :], in_=poar[1 + 128 * t:1 + 128 * (t + 1), :]))
            if t >= 2:
                wait(e, V10y(t - 2))
            Q(f"x10{t % 2}").dma(e.dma_start(
                out=x1l[t % 2][:, :], in_=px1[128 * t:128 * (t + 1), :]))
            Q(f"yfl10{t % 2}").dma(e.dma_start(
                out=yfl[t % 2][:, :], in_=py1f[128 * t:128 * (t + 1), :]))
            if t >= 1:
                wait(e, V10y(t - 1))
                Q(f"yy10{(t - 1) % 2}").dma(e.dma_start(
                    out=y[128 * (t - 1):128 * t, :],
                    in_=ybuf[(t - 1) % 2][:, :]))
        wait(e, V10y(NT - 1))
        Q(f"yy10{(NT - 1) % 2}").dma(e.dma_start(
            out=y[128 * (NT - 1):128 * NT, :], in_=ybuf[(NT - 1) % 2][:, :]))
        st["p10_y"] = [Q("yy100").tok(), Q("yy101").tok()]

    def p10c_gps(e):
        for ob in range(2):
            wait(e, (Q("wl10"), B10["wl10"] + 16 * (ob + 1)))
            if ob == 0:
                wait(e, V10pf(1))
            e.tensor_copy(out=wph[:, 4096 * ob:4096 * (ob + 1)],
                          in_=wtmp10[:, :])
            e.tensor_tensor(out=wtmp10[:, :], in0=wtmp10[:, :],
                            in1=wph[:, 4096 * ob:4096 * (ob + 1)],
                            op=ALU.subtract)
            gs.inc(e.tensor_copy(out=wpl[:, 4096 * ob:4096 * (ob + 1)],
                                 in_=wtmp10[:, :]))

    def p10c_tsec(teng):
        for t in range(NT):
            for b in range(2):
                wait(teng, q_kel(t))
                if t == 0:
                    wait(teng, V10pf(1))
                else:
                    wait(teng, V10s2(t - 1, b))
                teng.matmul(pbank[b][:, :], triu_s[:, :],
                            k2el[t % 2][:, 512 * b:512 * (b + 1)],
                            start=True, stop=False)
                if t == 0:
                    ts.inc(teng.matmul(
                        pbank[b][:, :], ones1f_s[0:1, :],
                        pf2r[0:1, 512 * b:512 * (b + 1)],
                        start=False, stop=True))
                else:
                    ts.inc(teng.matmul(
                        pbank[b][:, :], bc127_s[:, :],
                        s2ce[(t - 1) % 2][:, 512 * b:512 * (b + 1)],
                        start=False, stop=True))
            for dc in range(8):
                gdx = 8 * t + dc
                wait(teng, V10os(t))
                if gdx >= 4:
                    pt, pdc = divmod(gdx - 4, 8)
                    wait(teng, V10lo(pt, pdc))
                ts.inc(teng.transpose(
                    pbank[4 + gdx % 4][:, 0:128],
                    osb[:, 128 * dc:128 * (dc + 1)], ident_s[:, :]))
            for ob in range(2):
                wait(teng, A10h(t, 7), V10lo(t, 7), G10sp(ob), q_bp)
                if t >= 2:
                    wait(teng, V10y(t - 2))
                bk = pbank[2 + ob]
                for kc in range(8):
                    lh = osTh[t % 2][:, 128 * kc:128 * (kc + 1)]
                    ll = osTl[t % 2][:, 128 * kc:128 * (kc + 1)]
                    rh = wph[:, 4096 * ob + 512 * kc:4096 * ob + 512 * (kc + 1)]
                    rl = wpl[:, 4096 * ob + 512 * kc:4096 * ob + 512 * (kc + 1)]
                    teng.matmul(bk[:, :], lh, rh, start=(kc == 0), stop=False)
                    teng.matmul(bk[:, :], lh, rl, start=False, stop=False)
                    teng.matmul(bk[:, :], ll, rh, start=False, stop=False)
                ts.inc(teng.matmul(bk[:, :], ones1f_s[0:1, :],
                                   bprow[0:1, 512 * ob:512 * (ob + 1)],
                                   start=False, stop=True))

    def p10c_act(e):
        for t in range(NT):
            for dc in range(8):
                wait(e, (ts, T10tr(t, dc)[1]))
                if t >= 2:
                    wait(e, T10pm(t - 2, 1))
                asm.inc(e.activation(
                    out=osTh[t % 2][:, 128 * dc:128 * (dc + 1)],
                    in_=pbank[4 + (8 * t + dc) % 4][:, 0:128],
                    func=AF.Identity))

    def p10c_vec(e):
        for t in range(NT):
            for b in range(2):
                wait(e, T10s2(t, b))
                vs.inc(e.tensor_copy(out=s2ce[t % 2][:, 512 * b:512 * (b + 1)],
                                     in_=pbank[b][:, :]))
            wait(e, q_q2(t), q_oal(t))
            e.scalar_tensor_tensor(out=omb[:, :], in0=s2ce[t % 2][:, :],
                                   scalar=C0232, in1=q2l[t % 2][:, :],
                                   op0=ALU.mult, op1=ALU.mult)
            if t >= 1:
                wait(e, (ts, T10tr(t - 1, 7)[1]))
            vs.inc(e.tensor_tensor(out=osb[:, :], in0=omb[:, :],
                                   in1=oarl[t % 2][:, :], op=ALU.add))
            for dc in range(8):
                wait(e, A10h(t, dc))
                vs.inc(e.tensor_tensor(
                    out=osTl[t % 2][:, 128 * dc:128 * (dc + 1)],
                    in0=pbank[4 + (8 * t + dc) % 4][:, 0:128],
                    in1=osTh[t % 2][:, 128 * dc:128 * (dc + 1)],
                    op=ALU.subtract))
            wait(e, T10pm(t, 0), q_x1(t), q_yfl(t))
            if t >= 2:
                wait(e, q_yy(t - 2))
            yb = ybuf[t % 2]
            e.tensor_tensor(out=yb[:, 0:512], in0=pbank[2][:, :],
                            in1=x1l[t % 2][:, 0:512], op=ALU.add)
            wait(e, T10pm(t, 1))
            e.tensor_tensor(out=yb[:, 512:1024], in0=pbank[3][:, :],
                            in1=x1l[t % 2][:, 512:1024], op=ALU.add)
            vs.inc(e.tensor_tensor(out=yb[:, :], in0=yb[:, :],
                                   in1=yfl[t % 2][:, :], op=ALU.add))

    block.sync(p10c_sync)
    block.gpsimd(p10c_gps)
    block.tensor(p10c_tsec)
    block.scalar(p10c_act)
    block.vector(p10c_vec)
    for i in range(8):
        bankfree[i] = (vs, vs.n)
    assert ts.n == p10t + 18 * NT + 2
    assert asm.n == p10a_ + 8 * NT
    assert vs.n == p10v + 18 * NT + 3
    assert gs.n == p10g + 2
    AB16.pop()
    AF32.pop()
    AF32.pop()

    def p10_fin(e):
        wait(e, *st["p10_y"])

    block.sync(p10_fin)


def kernel(**inputs):
    full = {k: np.asarray(v) for k, v in inputs.items()}
    n_tok = full["L0_attn_x"].shape[1]
    out = run(n_tok, full)
    return out.astype(np.float32)
